# version 05
# baseline (speedup 1.0000x reference)
import numpy as np
import ml_dtypes
import concourse.bass as bass
import concourse.mybir as mybir
from concourse.bass_utils import run_bass_kernel_spmd
from contextlib import ExitStack

F32 = mybir.dt.float32
BF16 = mybir.dt.bfloat16
AF = mybir.ActivationFunctionType
ALU = mybir.AluOpType
AX = mybir.AxisListType

NT = 4096
E = 1024
H = 16
HID = 4096
NQC = 8
TFF = 1024

ENGINES = ("sync", "scalar", "vector", "tensor", "gpsimd")


class Plan:

    def __init__(self, nc, stack):
        self.nc = nc
        self.stack = stack
        self.thunks = {e: [] for e in ENGINES}
        self.sems = {}
        self.counts = {}

    def sem(self, name):
        if name not in self.sems:
            self.sems[name] = self.stack.enter_context(self.nc.semaphore(name))
            self.counts[name] = 0
        return self.sems[name]

    def op(self, eng, fn, inc=None, inc_n=1):
        if inc is not None:
            self.sem(inc)
            self.counts[inc] += inc_n
            target = self.counts[inc]
            sem = self.sems[inc]

            def thunk(e, fn=fn, sem=sem, inc_n=inc_n):
                fn(e).then_inc(sem, inc_n)
        else:
            target = None

            def thunk(e, fn=fn):
                fn(e)
        self.thunks[eng].append(thunk)
        return target

    def wait(self, eng, name, val):
        if val is None or val <= 0:
            return
        sem = self.sem(name)
        self.thunks[eng].append(lambda e, s=sem, v=val: e.wait_ge(s, v))

    def val(self, name):
        self.sem(name)
        return self.counts[name]

    def phase_barrier(self, exclude=()):
        snap = [(n, v) for n, v in self.counts.items()
                if v > 0 and n not in exclude]
        for eng in ENGINES:
            for n, v in snap:
                self.wait(eng, n, v)

    def flush(self, block):
        for eng in ENGINES:
            if not self.thunks[eng]:
                continue
            fns = self.thunks[eng]
            self.thunks[eng] = []
            getattr(block, eng)(lambda e, fns=fns: [f(e) for f in fns])


def plan_ln(P, x_ap, out_ap, n, eps, sl, ready_val=0):
    P.wait("vector", "ve", ready_val)
    c1 = P.op("vector", lambda e: e.reduce_sum(sl["rs"], x_ap, axis=AX.X), inc="ve")
    P.wait("vector", "ve", c1)
    c2 = P.op("vector", lambda e: e.tensor_scalar_mul(sl["nmu"], sl["rs"], -1.0 / n),
              inc="ve")
    P.wait("scalar", "ve", c2)
    c3 = P.op("scalar", lambda e: e.activation(sl["sq_scr"], x_ap, AF.Square,
                                               bias=sl["nmu"], scale=1.0,
                                               accum_out=sl["var"]), inc="se")
    P.wait("vector", "se", c3)
    c4 = P.op("vector", lambda e: e.tensor_scalar(sl["var"], sl["var"], 1.0 / n, eps,
                                                  ALU.mult, ALU.add), inc="ve")
    P.wait("scalar", "ve", c4)
    c5 = P.op("scalar", lambda e: e.activation(sl["sd"], sl["var"], AF.Sqrt), inc="se")
    P.wait("vector", "se", c5)
    c6 = P.op("vector", lambda e: e.reciprocal(sl["rstd"], sl["sd"]), inc="ve")
    P.wait("vector", "ve", c6)
    c7 = P.op("vector", lambda e: e.tensor_tensor(sl["nmur"], sl["nmu"], sl["rstd"],
                                                  ALU.mult), inc="ve")
    P.wait("vector", "ve", c7)
    c8 = P.op("vector", lambda e: e.tensor_scalar(out_ap, x_ap, sl["rstd"], sl["nmur"],
                                                  ALU.mult, ALU.add), inc="ve")
    return c8


def build_program(debug=False, phases=3, nqc=NQC, sub="all"):
    nc = bass.Bass("TRN2", target_bir_lowering=False)

    d_x = nc.dram_tensor("x", [NT, E], F32, kind="ExternalInput").ap()
    d_wq = nc.dram_tensor("wq", [E, E], F32, kind="ExternalInput").ap()
    d_wk = nc.dram_tensor("wk", [E, E], F32, kind="ExternalInput").ap()
    d_wv = nc.dram_tensor("wv", [E, E], F32, kind="ExternalInput").ap()
    d_fvt = nc.dram_tensor("fvt", [NT, 2048], F32, kind="ExternalInput").ap()
    d_cmat = nc.dram_tensor("cmat", [128, 32], F32, kind="ExternalInput").ap()
    d_gb = nc.dram_tensor("gamma_b", [128, E], F32, kind="ExternalInput").ap()
    d_bb = nc.dram_tensor("beta_b", [128, E], F32, kind="ExternalInput").ap()
    d_w1t = nc.dram_tensor("w1t", [E, HID], F32, kind="ExternalInput").ap()
    d_w2t = nc.dram_tensor("w2t", [E, HID], F32, kind="ExternalInput").ap()
    d_h1t = nc.dram_tensor("h1t", [HID, HID], F32, kind="ExternalInput").ap()
    d_h2t = nc.dram_tensor("h2t", [HID, HID], F32, kind="ExternalInput").ap()
    d_w3t = nc.dram_tensor("w3t", [HID, E], F32, kind="ExternalInput").ap()
    d_s1 = nc.dram_tensor("s1m", [128, 32], F32, kind="ExternalInput").ap()
    d_s2 = nc.dram_tensor("s2m", [128, 32], F32, kind="ExternalInput").ap()
    d_sh1 = nc.dram_tensor("sh1m", [128, 32], F32, kind="ExternalInput").ap()
    d_sh2 = nc.dram_tensor("sh2m", [128, 32], F32, kind="ExternalInput").ap()
    d_s3 = nc.dram_tensor("s3m", [128, 8], F32, kind="ExternalInput").ap()
    d_idb = nc.dram_tensor("id_bf", [128, 128], BF16, kind="ExternalInput").ap()
    d_idf = nc.dram_tensor("id_f32", [128, 128], F32, kind="ExternalInput").ap()
    d_out = nc.dram_tensor("out", [NT, E], F32, kind="ExternalOutput").ap()

    sk = "ExternalOutput" if debug else "Internal"
    d_qs = nc.dram_tensor("qs", [NQC, E, 512], BF16, kind=sk).ap()
    d_ks = nc.dram_tensor("ks", [NQC, E, 512], BF16, kind=sk).ap()
    d_vs = nc.dram_tensor("vs", [NT, H * 65], BF16, kind=sk).ap()
    d_x2 = nc.dram_tensor("x2s", [NT, E], F32, kind=sk).ap()
    d_fvtb = nc.dram_tensor("fvtb", [NT, 2048], BF16, kind=sk).ap()
    d_w1b = nc.dram_tensor("w1b", [E, HID], BF16, kind=sk).ap()
    d_w2b = nc.dram_tensor("w2b", [E, HID], BF16, kind="Internal").ap()
    d_h1b = nc.dram_tensor("h1b", [HID, HID], BF16, kind="Internal").ap()
    d_h2b = nc.dram_tensor("h2b", [HID, HID], BF16, kind="Internal").ap()
    d_w3b = nc.dram_tensor("w3b", [HID, E], BF16, kind="Internal").ap()

    stack = ExitStack()
    with stack:
        P = Plan(nc, stack)
        sb = lambda name, shape, dt: stack.enter_context(nc.sbuf_tensor(name, shape, dt))

        gb_sb = sb("gb", [128, E], F32)
        bb_sb = sb("bb", [128, E], F32)
        c_f32 = sb("c_f32", [128, 32], F32)
        c_bf = sb("c_bf", [128, 32], BF16)
        s1_sb = sb("s1", [128, 32], F32)
        s2_sb = sb("s2", [128, 32], F32)
        sh1_sb = sb("sh1", [128, 32], F32)
        sh2_sb = sb("sh2", [128, 32], F32)
        s3_sb = sb("s3", [128, 8], F32)
        idb_sb = sb("idb", [128, 128], BF16)
        idf_sb = sb("idf", [128, 128], F32)
        ones_f = sb("ones_f", [1, 128], F32)
        w_bc = sb("w_bc", [128, H * 65], F32)
        sq_scr = sb("sq_scr", [128, E], F32)
        t1_sb = sb("t1", [128, E], F32)
        st_t = sb("stats", [128, 32], F32)
        stats = []
        for s in range(4):
            stats.append({"rs": st_t[:, 8 * s:8 * s + 1],
                          "nmu": st_t[:, 8 * s + 1:8 * s + 2],
                          "var": st_t[:, 8 * s + 2:8 * s + 3],
                          "sd": st_t[:, 8 * s + 3:8 * s + 4],
                          "rstd": st_t[:, 8 * s + 4:8 * s + 5],
                          "nmur": st_t[:, 8 * s + 5:8 * s + 6],
                          "sq_scr": sq_scr[:]})

        sAB = ExitStack()
        pcst = [sAB.enter_context(nc.sbuf_tensor(f"pcst{i}", [128, 4096], BF16))
                for i in range(2)]

        block = stack.enter_context(nc.Block())

        with ExitStack() as sA:
            sbA = lambda n_, s_, d_: sA.enter_context(nc.sbuf_tensor(n_, s_, d_))
            wq_sb = sbA("wq_sb", [128, 8 * E], BF16)
            wk_sb = sbA("wk_sb", [128, 8 * E], BF16)
            wv_sb = sbA("wv_sb", [128, 8 * E], BF16)
            xn1_sb = [sbA(f"xn1_{i}", [128, E], BF16) for i in range(2)]
            xnT_sb = sbA("xnT", [128, 8 * 512], BF16)
            qe_sb = [sbA(f"qe{i}", [128, 512], BF16) for i in range(2)]
            ke_sb = [sbA(f"ke{i}", [128, 512], BF16) for i in range(2)]
            va_sb = [sbA(f"va{i}", [128, H * 65], BF16) for i in range(2)]
            w_row = sbA("w_row", [1, H * 65], F32)
            w_hi = sbA("w_hi", [1, H * 65], BF16)
            w_rem = sbA("w_rem", [1, H * 65], F32)
            w_lo = sbA("w_lo", [1, H * 65], BF16)
            ones_b = sbA("ones_b", [1, 128], BF16)
            x_sb = [sbA(f"xA{i}", [128, E], F32) for i in range(2)]
            ps_mm = sA.enter_context(nc.psum_tensor("ps_mm", [128, 1024], F32))
            ps_tr = sA.enter_context(nc.psum_tensor("ps_tr", [128, 2048], BF16))
            ps_w = sA.enter_context(nc.psum_tensor("ps_w", [1, 1536], F32))

            for dd, ss in [(d_gb, gb_sb), (d_bb, bb_sb), (d_cmat, c_f32),
                           (d_s1, s1_sb), (d_s2, s2_sb), (d_sh1, sh1_sb),
                           (d_sh2, sh2_sb), (d_s3, s3_sb), (d_idb, idb_sb),
                           (d_idf, idf_sb)]:
                l_set = P.op("sync", lambda e, dd=dd, ss=ss:
                             e.dma_start(out=ss[:], in_=dd[:]), inc="l_set", inc_n=16)
            for dd, ss in [(d_wq, wq_sb), (d_wk, wk_sb), (d_wv, wv_sb)]:
                src = dd.rearrange("(ki p) o -> p ki o", p=128)
                dst = ss[:].rearrange("p (ki o) -> p ki o", ki=8)
                l_wqkv = P.op("gpsimd", lambda e, s1=src, d1=dst:
                              e.dma_start(out=d1, in_=s1), inc="l_wqkv", inc_n=16)
            pc_ld = [0, 0]
            pc_st = [0, 0]
            pc_idx = 0
            fvt_pc_done = None
            for dsrc, ddst, nch, wid in [(d_fvt, d_fvtb, 32, 2048),
                                         (d_w1t, d_w1b, 8, HID),
                                         (d_w2t, d_w2b, 8, HID),
                                         (d_h1t, d_h1b, 32, HID),
                                         (d_h2t, d_h2b, 32, HID),
                                         (d_w3t, d_w3b, 32, E)]:
                for c in range(nch):
                    sl = pc_idx % 2
                    pc_idx += 1
                    P.wait("gpsimd", f"pcs{sl}", pc_st[sl])
                    pc_ld[sl] = P.op("gpsimd", lambda e, dsrc=dsrc, c=c, sl=sl,
                                     wid=wid: e.dma_start(
                                         out=pcst[sl][:, 0:wid],
                                         in_=dsrc[128 * c:128 * c + 128, :]),
                                     inc=f"pcl{sl}", inc_n=16)
                    P.wait("gpsimd", f"pcl{sl}", pc_ld[sl])
                    pc_st[sl] = P.op("gpsimd", lambda e, ddst=ddst, c=c, sl=sl,
                                     wid=wid: e.dma_start(
                                         out=ddst[128 * c:128 * c + 128, :],
                                         in_=pcst[sl][:, 0:wid]),
                                     inc=f"pcs{sl}", inc_n=16)
                if fvt_pc_done is None:
                    fvt_pc_done = (pc_st[0], pc_st[1])
            P.wait("vector", "l_set", l_set)
            P.op("vector", lambda e: e.tensor_copy(c_bf[:], c_f32[:]), inc="ve")
            P.op("vector", lambda e: e.memset(ones_f[:], 1.0), inc="ve")
            setup_ve = P.op("vector", lambda e: e.memset(ones_b[:], 1.0), inc="ve")

            lx_cnt = [0, 0]
            xn_free_ve = [0, 0]
            x_freeA = [0, 0]
            stq_free = [0, 0]
            stk_free = [0, 0]
            stv_free = [0, 0]
            trp = 0
            tr_evac_ve = [0, 0]
            mm_free_ve = [0, 0]
            te_w = 0
            last_g = 4 * nqc - 1
            for qc in range(nqc):
                for t in range(4):
                    g_tt = 4 * qc + t
                    s = g_tt % 2
                    P.wait("sync", "ve", x_freeA[s])
                    lx_cnt[s] = P.op("sync", lambda e, g=g_tt, s=s: e.dma_start(
                        out=x_sb[s][:], in_=d_x[128 * g:128 * g + 128, :]),
                        inc=f"lx{s}", inc_n=16)
                    P.wait("vector", f"lx{s}", lx_cnt[s])
                    ln_ve = plan_ln(P, x_sb[s][:], xn1_sb[s][:], E, 1e-5, stats[s],
                                    ready_val=xn_free_ve[s])
                    x_freeA[s] = ln_ve
                    P.wait("tensor", "ve", ln_ve)
                    P.wait("tensor", "ve", tr_evac_ve[trp])
                    for f in range(8):
                        te_tr = P.op("tensor", lambda e, s=s, f=f, trp=trp: e.transpose(
                            ps_tr[:, 1024 * trp + 128 * f:1024 * trp + 128 * f + 128],
                            xn1_sb[s][:, 128 * f:128 * f + 128], idb_sb[:]),
                            inc="te" if f == 7 else None)
                    P.wait("vector", "te", te_tr)
                    for f in range(8):
                        ev = P.op("vector", lambda e, f=f, t=t, trp=trp: e.tensor_copy(
                            xnT_sb[:, 512 * f + 128 * t:512 * f + 128 * t + 128],
                            ps_tr[:, 1024 * trp + 128 * f:1024 * trp + 128 * f + 128]),
                            inc="ve" if f == 7 else None)
                    tr_evac_ve[trp] = ev
                    xn_free_ve[s] = ev
                    trp ^= 1
                if sub not in ("qk", "all"):
                    continue
                P.wait("tensor", "l_wqkv", l_wqkv)
                P.wait("tensor", "ve", setup_ve)
                xnT_ready = P.val("ve")
                P.wait("tensor", "ve", xnT_ready)
                for w_sb, ev_sb, ev_free, d_dst, sem_st in [
                        (wq_sb, qe_sb, stq_free, d_qs, "stq"),
                        (wk_sb, ke_sb, stk_free, d_ks, "stk")]:
                    for f in range(8):
                        hp = f % 2
                        P.wait("tensor", "ve", mm_free_ve[hp])
                        for ki in range(8):
                            te_mm = P.op(
                                "tensor", lambda e, w_sb=w_sb, f=f, ki=ki, hp=hp:
                                e.matmul(ps_mm[:, 512 * hp:512 * hp + 512],
                                         w_sb[:, E * ki + 128 * f:E * ki + 128 * f + 128],
                                         xnT_sb[:, 512 * ki:512 * ki + 512],
                                         start=(ki == 0), stop=(ki == 7)),
                                inc="te" if ki == 7 else None)
                        P.wait("vector", "te", te_mm)
                        P.wait("vector", f"{sem_st}{hp}", ev_free[hp])
                        cv = P.op("vector", lambda e, ev_sb=ev_sb, hp=hp: e.tensor_copy(
                            ev_sb[hp][:], ps_mm[:, 512 * hp:512 * hp + 512]), inc="ve")
                        mm_free_ve[hp] = cv
                        P.wait("sync", "ve", cv)
                        ev_free[hp] = P.op(
                            "sync", lambda e, ev_sb=ev_sb, hp=hp, qc=qc, f=f,
                            d_dst=d_dst: e.dma_start(
                                out=d_dst[qc, 128 * f:128 * f + 128, :],
                                in_=ev_sb[hp][:]),
                            inc=f"{sem_st}{hp}", inc_n=16)
                if sub not in ("v", "all", "vmm", "vst"):
                    continue
                for t in range(4):
                    g_tt = 4 * qc + t
                    hp = t % 2
                    P.wait("vector", f"stv{hp}", stv_free[hp])
                    mv = P.op("vector", lambda e, hp=hp: e.memset(va_sb[hp][:], 1.0),
                              inc="ve")
                    for nf in range(2):
                        P.wait("tensor", "ve", mm_free_ve[nf])
                        for ki in range(8):
                            te_mm = P.op("tensor", lambda e, t=t, ki=ki, nf=nf:
                                         e.matmul(
                                             ps_mm[:, 512 * nf:512 * nf + 512],
                                             xnT_sb[:, 512 * ki + 128 * t:
                                                    512 * ki + 128 * t + 128],
                                             wv_sb[:, E * ki + 512 * nf:
                                                   E * ki + 512 * nf + 512],
                                             start=(ki == 0), stop=(ki == 7)),
                                         inc="te" if ki == 7 else None)
                        P.wait("vector", "te", te_mm)
                        P.wait("vector", "ve", mv)
                        dst = va_sb[hp][:].rearrange("p (h c) -> p h c", c=65)[
                            :, 8 * nf:8 * nf + 8, 0:64]
                        src = ps_mm[:, 512 * nf:512 * nf + 512].rearrange(
                            "p (h c) -> p h c", c=64)
                        cv = P.op("vector", lambda e, dst=dst, src=src:
                                  e.tensor_copy(dst, src), inc="ve")
                        mm_free_ve[nf] = cv
                    va_ve = P.val("ve")
                    if sub != "vmm":
                        P.wait("sync", "ve", va_ve)
                        stv_free[hp] = P.op("sync", lambda e, hp=hp, g=g_tt: e.dma_start(
                            out=d_vs[128 * g:128 * g + 128, :], in_=va_sb[hp][:]),
                            inc=f"stv{hp}", inc_n=16)
                    if sub in ("vmm", "vst"):
                        continue
                    P.wait("tensor", "ve", va_ve)
                    for ch, (c0, c1) in enumerate([(0, 512), (512, 1024),
                                                   (1024, 1040)]):
                        te_w = P.op("tensor", lambda e, g=g_tt, hp=hp, c0=c0, c1=c1:
                                    e.matmul(ps_w[0:1, c0:c1], c_bf[:, g:g + 1],
                                             va_sb[hp][:, c0:c1],
                                             start=(g == 0), stop=(g == last_g),
                                             skip_group_check=True),
                                    inc="te" if (ch == 2 and g_tt == last_g) else None)
            w_ready_ve = 0
            if sub in ("v", "all"):
                P.wait("vector", "te", te_w)
                cw = P.op("vector", lambda e: e.tensor_copy(
                    w_row[:], ps_w[0:1, 0:H * 65]), inc="ve")
                P.wait("vector", "ve", cw)
                c_hi = P.op("vector", lambda e: e.tensor_copy(w_hi[:], w_row[:]),
                            inc="ve")
                P.wait("vector", "ve", c_hi)
                c_rm = P.op("vector", lambda e: e.tensor_tensor(
                    w_rem[:], w_row[:], w_hi[:], ALU.subtract), inc="ve")
                P.wait("vector", "ve", c_rm)
                c_lo = P.op("vector", lambda e: e.tensor_copy(w_lo[:], w_rem[:]),
                            inc="ve")
                P.wait("tensor", "ve", c_lo)
                P.wait("tensor", "ve", mm_free_ve[0])
                P.wait("tensor", "ve", mm_free_ve[1])
                for c0, c1 in [(0, 512), (512, 1024), (1024, 1040)]:
                    P.op("tensor", lambda e, c0=c0, c1=c1: e.matmul(
                        ps_mm[:, 0:c1 - c0], ones_b[:], w_hi[0:1, c0:c1],
                        start=True, stop=False))
                    tw = P.op("tensor", lambda e, c0=c0, c1=c1: e.matmul(
                        ps_mm[:, 0:c1 - c0], ones_b[:], w_lo[0:1, c0:c1],
                        start=False, stop=True), inc="te")
                    P.wait("vector", "te", tw)
                    cbc = P.op("vector", lambda e, c0=c0, c1=c1: e.tensor_copy(
                        w_bc[:, c0:c1], ps_mm[:, 0:c1 - c0]), inc="ve")
                    P.wait("tensor", "ve", cbc)
                    w_ready_ve = cbc
            a_store_totals = {"stq0": stq_free[0], "stq1": stq_free[1],
                              "stk0": stk_free[0], "stk1": stk_free[1],
                              "stv0": stv_free[0], "stv1": stv_free[1]}
            if phases <= 1:
                for s_name, tot in a_store_totals.items():
                    P.wait("sync", s_name, tot)
            P.flush(block)
        if phases <= 1:
            sAB.close()
            return nc

        with ExitStack() as sB:
            sbB = lambda n_, s_, d_: sB.enter_context(nc.sbuf_tensor(n_, s_, d_))
            kp_sb = [sbB(f"kp{i}", [128, NT], BF16) for i in range(2)]
            vp_sb = [sbB(f"vp{i}", [128, 32 * 130], BF16) for i in range(2)]
            qp_sb = [sbB(f"qp{i}", [128, 512], BF16) for i in range(2)]
            fvt_sb = [sbB(f"fvt{i}", [128, 32 * 256], BF16) for i in range(4)]
            p_sb = [sbB(f"p{i}", [128, 1024], BF16) for i in range(2)]
            of_sb = sbB("of", [128, 1024], F32)
            bvf_sb = sbB("bvf", [128, 1024], F32)
            bvt_sb = sbB("bvt", [128, 1024], F32)
            asm_sb = [sbB(f"asm{i}", [128, E], F32) for i in range(4)]
            x2_sb = [sbB(f"x2_{i}", [128, E], F32) for i in range(2)]
            t2_sb = sbB("t2", [128, E], F32)
            x_sb = [sbB(f"xB{i}", [128, E], F32) for i in range(2)]
            rcp_sb = sbB("rcp", [128, 8], F32)
            ps_S = [sB.enter_context(nc.psum_tensor(f"psS{i}", [128, 1024], F32))
                    for i in range(2)]
            ps_O = sB.enter_context(nc.psum_tensor("psO", [128, 1024], F32))
            ps_B = sB.enter_context(nc.psum_tensor("psB", [128, 1024], F32))

            P.phase_barrier(exclude=("pcl0", "pcl1", "pcs0", "pcs1"))
            P.wait("sync", "pcs0", fvt_pc_done[0])
            P.wait("sync", "pcs1", fvt_pc_done[1])

            kv_cnt = [0, 0]
            fvt_cnt = [0, 0, 0, 0]
            kv_free_te = [0, 0]
            fvt_free_te = [0, 0, 0, 0]
            psS_free_se = [0, 0]
            psO_free_ve = 0
            psB_free_ve = 0
            asm_free_ve = [0, 0, 0, 0]
            x2_free = [0, 0]
            x_freeB = [0, 0]
            lxb_cnt = [0, 0]

            for qc in range(NQC):
                for pr in range(8):
                    s = pr % 2
                    combos = []
                    for h01 in range(2):
                        for vh in range(2):
                            a = 2 * pr + h01
                            v = 2 * qc + vh
                            if a - v >= 8:
                                combos.append((h01, vh, 256 * (a - v) - 2048))
                    P.wait("sync", "te", kv_free_te[s])
                    src = d_ks[:, 128 * pr:128 * pr + 128, :].rearrange(
                        "b p c -> p b c")
                    dst = kp_sb[s][:].rearrange("p (b c) -> p b c", b=8)
                    P.op("sync", lambda e, s1=src, d1=dst: e.dma_start(out=d1, in_=s1),
                         inc=f"lkv{s}", inc_n=16)
                    src = d_vs[:, 130 * pr:130 * pr + 130].rearrange(
                        "(kc p) c -> p kc c", p=128)
                    dst = vp_sb[s][:].rearrange("p (kc c) -> p kc c", kc=32)
                    P.op("sync", lambda e, s1=src, d1=dst: e.dma_start(out=d1, in_=s1),
                         inc=f"lkv{s}", inc_n=16)
                    kv_cnt[s] = P.op("sync", lambda e, qc=qc, pr=pr, s=s: e.dma_start(
                        out=qp_sb[s][:], in_=d_qs[qc, 128 * pr:128 * pr + 128, :]),
                        inc=f"lkv{s}", inc_n=16)
                    for (h01, vh, gr0) in combos:
                        slot = 2 * h01 + vh
                        P.wait("sync", "te", fvt_free_te[slot])
                        src = d_fvtb[:, gr0:gr0 + 256].rearrange(
                            "(kc p) c -> p kc c", p=128)
                        dst = fvt_sb[slot][:].rearrange("p (kc c) -> p kc c", kc=32)
                        fvt_cnt[slot] = P.op("sync", lambda e, s1=src, d1=dst:
                                             e.dma_start(out=d1, in_=s1),
                                             inc=f"lfv{slot}", inc_n=16)

                    P.wait("tensor", f"lkv{s}", kv_cnt[s])
                    s_te = {}
                    exp_se = {}
                    pv_te = {}

                    def emit_S(kc, s=s):
                        b = kc % 2
                        P.wait("tensor", "se", psS_free_se[b])
                        P.op("tensor", lambda e, kc=kc, b=b, s=s: e.matmul(
                            ps_S[b][:, 0:512],
                            kp_sb[s][0:64, 128 * kc:128 * kc + 128],
                            qp_sb[s][0:64, :], start=True, stop=True))
                        s_te[kc] = P.op("tensor", lambda e, kc=kc, b=b, s=s: e.matmul(
                            ps_S[b][:, 512:1024],
                            kp_sb[s][64:128, 128 * kc:128 * kc + 128],
                            qp_sb[s][64:128, :], start=True, stop=True), inc="te")

                    def emit_exp(kc):
                        b = kc % 2
                        gate = pv_te[kc - 2] if kc >= 2 else s_te[kc]
                        P.wait("scalar", "te", gate)
                        c = P.op("scalar", lambda e, b=b: e.activation(
                            p_sb[b][:], ps_S[b][:], AF.Exp, scale=1.0 / 32.0),
                            inc="se")
                        exp_se[kc] = c
                        psS_free_se[b] = c

                    def emit_PV(kc, s=s, combos=combos):
                        b = kc % 2
                        P.wait("tensor", "se", exp_se[kc])
                        if kc == 0:
                            P.wait("tensor", "ve", psO_free_ve)
                            if combos:
                                P.wait("tensor", "ve", psB_free_ve)
                                for (h01, vh, gr0) in combos:
                                    slot = 2 * h01 + vh
                                    P.wait("tensor", f"lfv{slot}", fvt_cnt[slot])
                        n_ops = 2 + len(combos)
                        i = 0
                        t = None
                        for h01 in range(2):
                            i += 1
                            t = P.op("tensor", lambda e, kc=kc, b=b, s=s, h01=h01:
                                     e.matmul(
                                         ps_O[0:65, 512 * h01:512 * h01 + 512],
                                         vp_sb[s][:, 130 * kc + 65 * h01:
                                                   130 * kc + 65 * h01 + 65],
                                         p_sb[b][:, 512 * h01:512 * h01 + 512],
                                         start=(kc == 0), stop=(kc == 31),
                                         skip_group_check=True),
                                     inc="te" if i == n_ops else None)
                        seen_banks = set()
                        for (h01, vh, gr0) in combos:
                            i += 1
                            slot = 2 * h01 + vh
                            bank = slot // 2
                            st_flag = (kc == 0) and bank not in seen_banks
                            if kc == 0:
                                seen_banks.add(bank)
                            t = P.op("tensor", lambda e, kc=kc, b=b, s=s, h01=h01,
                                     slot=slot, st_flag=st_flag: e.matmul(
                                         ps_B[0:64, 256 * slot:256 * slot + 256],
                                         vp_sb[s][:, 130 * kc + 65 * h01:
                                                   130 * kc + 65 * h01 + 64],
                                         fvt_sb[slot][:, 256 * kc:256 * kc + 256],
                                         start=st_flag, stop=(kc == 31),
                                         skip_group_check=True),
                                     inc="te" if i == n_ops else None)
                        pv_te[kc] = t

                    emit_S(0)
                    emit_exp(0)
                    emit_S(1)
                    emit_exp(1)
                    for kc in range(32):
                        if kc + 2 < 32:
                            emit_S(kc + 2)
                        emit_PV(kc)
                        if kc + 2 < 32:
                            emit_exp(kc + 2)
                    last_te = pv_te[31]
                    kv_free_te[s] = last_te
                    for (h01, vh, gr0) in combos:
                        fvt_free_te[2 * h01 + vh] = last_te

                    P.wait("vector", "te", last_te)
                    evac_ve = P.op("vector", lambda e: e.tensor_copy(
                        of_sb[0:65, :], ps_O[0:65, :]),
                        inc="ve" if not combos else None)
                    for ci, (h01, vh, gr0) in enumerate(combos):
                        slot = 2 * h01 + vh
                        evac_ve = P.op("vector", lambda e, slot=slot: e.tensor_copy(
                            bvf_sb[0:64, 256 * slot:256 * slot + 256],
                            ps_B[0:64, 256 * slot:256 * slot + 256]),
                            inc="ve" if ci == len(combos) - 1 else None)
                    P.wait("tensor", "ve", evac_ve)
                    n_tr = 8 + 2 * len(combos)
                    i = 0
                    for h01 in range(2):
                        for u in range(4):
                            i += 1
                            tr_te = P.op("tensor", lambda e, h01=h01, u=u: e.transpose(
                                ps_O[:, 512 * h01 + 65 * u:512 * h01 + 65 * u + 65],
                                of_sb[0:65, 512 * h01 + 128 * u:
                                      512 * h01 + 128 * u + 128],
                                idf_sb[0:65, 0:65]),
                                inc="te" if i == n_tr else None)
                    for (h01, vh, gr0) in combos:
                        slot = 2 * h01 + vh
                        for uu in range(2):
                            i += 1
                            tr_te = P.op("tensor", lambda e, slot=slot, uu=uu:
                                         e.transpose(
                                             ps_B[:, 256 * slot + 64 * uu:
                                                  256 * slot + 64 * uu + 64],
                                             bvf_sb[0:64, 256 * slot + 128 * uu:
                                                    256 * slot + 128 * uu + 128],
                                             idf_sb[0:64, 0:64]),
                                         inc="te" if i == n_tr else None)
                    P.wait("vector", "te", tr_te)
                    if combos:
                        cb = None
                        for (h01, vh, gr0) in combos:
                            slot = 2 * h01 + vh
                            cb = P.op("vector", lambda e, slot=slot: e.tensor_copy(
                                bvt_sb[:, 256 * slot:256 * slot + 256],
                                ps_B[:, 256 * slot:256 * slot + 256]), inc="ve")
                        P.wait("vector", "ve", cb)
                    for h01 in range(2):
                        a = 2 * pr + h01
                        for u in range(4):
                            vh = u // 2
                            cr = P.op("vector", lambda e, h01=h01, u=u: e.reciprocal(
                                rcp_sb[:, 4 * h01 + u:4 * h01 + u + 1],
                                ps_O[:, 512 * h01 + 65 * u + 64:
                                     512 * h01 + 65 * u + 65]), inc="ve")
                            P.wait("vector", "ve", cr)
                            aff = any(c[0] == h01 and c[1] == vh for c in combos)
                            if aff:
                                slot = 2 * h01 + vh
                                in1 = bvt_sb[:, 256 * slot + 64 * (u % 2):
                                             256 * slot + 64 * (u % 2) + 64]
                            else:
                                in1 = w_bc[:, 65 * a:65 * a + 64]
                            P.wait("vector", "ve", asm_free_ve[u])
                            last = h01 == 1 and u == 3
                            stt_ve = P.op(
                                "vector", lambda e, h01=h01, u=u, a=a, in1=in1:
                                e.scalar_tensor_tensor(
                                    asm_sb[u][:, 64 * a:64 * a + 64],
                                    ps_O[:, 512 * h01 + 65 * u:
                                         512 * h01 + 65 * u + 64],
                                    rcp_sb[:, 4 * h01 + u:4 * h01 + u + 1],
                                    in1, ALU.mult, ALU.add),
                                inc="ve" if last else None)
                    psO_free_ve = stt_ve
                    psB_free_ve = stt_ve

                asm_ready = P.val("ve")
                for u in range(4):
                    sl = stats[u % 2]
                    g_tt = 4 * qc + u
                    hp = u % 2
                    P.wait("sync", "ve", x_freeB[hp])
                    lxb_cnt[hp] = P.op("sync", lambda e, g=g_tt, hp=hp: e.dma_start(
                        out=x_sb[hp][:], in_=d_x[128 * g:128 * g + 128, :]),
                        inc=f"lxb{hp}", inc_n=16)
                    c_ln1 = plan_ln(P, asm_sb[u][:], t1_sb[:], E, 1e-3, sl,
                                    ready_val=asm_ready)
                    asm_free_ve[u] = c_ln1
                    P.wait("vector", "ve", c_ln1)
                    c_g = P.op("vector", lambda e: e.scalar_tensor_tensor(
                        t2_sb[:], t1_sb[:], 0.0, gb_sb[:], ALU.bypass, ALU.mult),
                        inc="ve")
                    P.wait("vector", "ve", c_g)
                    c_b = P.op("vector", lambda e: e.scalar_tensor_tensor(
                        t2_sb[:], t2_sb[:], 0.0, bb_sb[:], ALU.bypass, ALU.add),
                        inc="ve")
                    c_ln2 = plan_ln(P, t2_sb[:], t1_sb[:], E, 1e-5, sl,
                                    ready_val=c_b)
                    P.wait("vector", f"lxb{hp}", lxb_cnt[hp])
                    P.wait("vector", "ve", c_ln2)
                    P.wait("vector", f"sx2{hp}", x2_free[hp])
                    c_x2 = P.op("vector", lambda e, hp=hp: e.tensor_tensor(
                        x2_sb[hp][:], x_sb[hp][:], t1_sb[:], ALU.add), inc="ve")
                    x_freeB[hp] = c_x2
                    P.wait("sync", "ve", c_x2)
                    x2_free[hp] = P.op("sync", lambda e, g=g_tt, hp=hp: e.dma_start(
                        out=d_x2[128 * g:128 * g + 128, :], in_=x2_sb[hp][:]),
                        inc=f"sx2{hp}", inc_n=16)
            b_x2_totals = {"sx20": x2_free[0], "sx21": x2_free[1]}
            if phases <= 2:
                for s_name, tot in b_x2_totals.items():
                    P.wait("sync", s_name, tot)
            P.flush(block)
        if phases <= 2:
            sAB.close()
            return nc

        sAB.close()
        with ExitStack() as sC:
            sbC = lambda n_, s_, d_: sC.enter_context(nc.sbuf_tensor(n_, s_, d_))
            xn2T = sbC("xn2T", [128, 8 * TFF], BF16)
            a_sb = sbC("a_sb", [128, 32 * TFF], BF16)
            h_sb = sbC("h_sb", [128, 32 * TFF], BF16)
            fT_sb = h_sb
            w12_sb = [sbC(f"w12_{i}", [128, 8 * 128], BF16) for i in range(4)]
            hsl_sb = [sbC(f"hsl{i}", [128, 32 * 128], BF16) for i in range(2)]
            x2t_sb = [sbC(f"x2t{i}", [128, E], F32) for i in range(2)]
            xn2_sb = [sbC(f"xn2_{i}", [128, E], BF16) for i in range(2)]
            silu_sb = [sbC(f"silu{i}", [128, 512], F32) for i in range(2)]
            ftm_sb = h_sb[:, 8 * TFF:8 * TFF + 2048].bitcast(F32)
            ps_M = [sC.enter_context(nc.psum_tensor(f"psM{i}", [128, 512], F32))
                    for i in range(4)]
            ps_tr2 = sC.enter_context(nc.psum_tensor("ps_tr2", [128, 2048], BF16))

            P.phase_barrier()

            lx2_cnt = [0, 0]
            x2t_free = [0, 0]
            w12_cnt = [0, 0, 0, 0]
            w12_free_te = [0, 0, 0, 0]
            hsl_cnt = [0, 0]
            hsl_free_te = [0, 0]
            silu_free_ve = [0, 0]
            psM_free = [0, 0, 0, 0]
            psM_free_eng = ["ve", "ve", "ve", "ve"]
            trp2 = [0]
            tr2_evac_ve = [0, 0]
            xn2_free_ve = [0, 0]
            out_free = [0, 0]

            def emit_xn2_unit(ft, u8):
                g_tt = 8 * ft + u8
                hp = u8 % 2
                tp2 = trp2[0]
                P.wait("sync", "ve", x2t_free[hp])
                lx2_cnt[hp] = P.op("sync", lambda e, g=g_tt, hp=hp: e.dma_start(
                    out=x2t_sb[hp][:], in_=d_x2[128 * g:128 * g + 128, :]),
                    inc=f"lx2{hp}", inc_n=16)
                P.wait("vector", f"lx2{hp}", lx2_cnt[hp])
                c_ln = plan_ln(P, x2t_sb[hp][:], xn2_sb[hp][:], E, 1e-5,
                               stats[hp], ready_val=xn2_free_ve[hp])
                x2t_free[hp] = c_ln
                P.wait("tensor", "ve", c_ln)
                P.wait("tensor", "ve", tr2_evac_ve[tp2])
                for f in range(8):
                    te_tr = P.op("tensor", lambda e, hp=hp, f=f, tp2=tp2:
                                 e.transpose(
                                     ps_tr2[:, 1024 * tp2 + 128 * f:
                                            1024 * tp2 + 128 * f + 128],
                                     xn2_sb[hp][:, 128 * f:128 * f + 128],
                                     idb_sb[:]), inc="te" if f == 7 else None)
                P.wait("vector", "te", te_tr)
                for f in range(8):
                    ev = P.op("vector", lambda e, f=f, u8=u8, tp2=tp2:
                              e.tensor_copy(
                                  xn2T[:, TFF * f + 128 * u8:
                                       TFF * f + 128 * u8 + 128],
                                  ps_tr2[:, 1024 * tp2 + 128 * f:
                                         1024 * tp2 + 128 * f + 128]),
                              inc="ve" if f == 7 else None)
                tr2_evac_ve[tp2] = ev
                xn2_free_ve[hp] = ev
                trp2[0] ^= 1

            for u8 in range(8):
                emit_xn2_unit(0, u8)

            for ft in range(4):
                xn2T_ve = P.val("ve")
                P.wait("tensor", "ve", xn2T_ve)

                for mc in range(32):
                    for wi, d_w in enumerate([d_w1b, d_w2b]):
                        slot = 2 * (mc % 2) + wi
                        P.wait("scalar", "te", w12_free_te[slot])
                        src = d_w[:, 128 * mc:128 * mc + 128].rearrange(
                            "(ki p) o -> p ki o", p=128)
                        dst = w12_sb[slot][:].rearrange("p (ki o) -> p ki o", ki=8)
                        w12_cnt[slot] = P.op("scalar", lambda e, s1=src, d1=dst:
                                             e.dma_start(out=d1, in_=s1),
                                             inc=f"lw12_{slot}", inc_n=16)
                    for wi in range(2):
                        P.wait("tensor", f"lw12_{2 * (mc % 2) + wi}",
                               w12_cnt[2 * (mc % 2) + wi])
                    for nt in range(2):
                        for wi in range(2):
                            b = 2 * (mc % 2) + wi
                            slot = 2 * (mc % 2) + wi
                            P.wait("tensor", psM_free_eng[b], psM_free[b])
                            for ki in range(8):
                                te_w12 = P.op(
                                    "tensor", lambda e, slot=slot, ki=ki, nt=nt, b=b:
                                    e.matmul(
                                        ps_M[b][:],
                                        w12_sb[slot][:, 128 * ki:128 * ki + 128],
                                        xn2T[:, TFF * ki + 512 * nt:
                                             TFF * ki + 512 * nt + 512],
                                        start=(ki == 0), stop=(ki == 7)),
                                    inc="te" if ki == 7 else None)
                        b1, b2 = 2 * (mc % 2), 2 * (mc % 2) + 1
                        hp2 = nt
                        P.wait("scalar", "te", te_w12)
                        P.wait("scalar", "ve", silu_free_ve[hp2])
                        c_si = P.op("scalar", lambda e, b1=b1, mc=mc, hp2=hp2:
                                    e.activation(silu_sb[hp2][:], ps_M[b1][:],
                                                 AF.Silu,
                                                 scale=s1_sb[:, mc:mc + 1]),
                                    inc="se")
                        psM_free[b1] = c_si
                        psM_free_eng[b1] = "se"
                        P.wait("vector", "se", c_si)
                        c_a = P.op("vector", lambda e, b2=b2, mc=mc, nt=nt, hp2=hp2:
                                   e.scalar_tensor_tensor(
                                       a_sb[:, TFF * mc + 512 * nt:
                                            TFF * mc + 512 * nt + 512],
                                       ps_M[b2][:], s2_sb[:, mc:mc + 1],
                                       silu_sb[hp2][:], ALU.mult, ALU.mult),
                                   inc="ve")
                        silu_free_ve[hp2] = c_a
                        psM_free[b2] = c_a
                        psM_free_eng[b2] = "ve"
                        w12_free_te[b1] = te_w12
                        w12_free_te[b2] = te_w12

                def h_stage(d_ht, s_ap, src_sb, dst_sb, sname, interleave_xn2):
                    te_h = 0
                    for mc in range(32):
                        if interleave_xn2 and mc < 8:
                            emit_xn2_unit(ft + 1, mc)
                        slot = mc % 2
                        P.wait("scalar", "te", hsl_free_te[slot])
                        src = d_ht[:, 128 * mc:128 * mc + 128].rearrange(
                            "(ki p) o -> p ki o", p=128)
                        dst = hsl_sb[slot][:].rearrange("p (ki o) -> p ki o", ki=32)
                        hsl_cnt[slot] = P.op("scalar", lambda e, s1=src, d1=dst:
                                             e.dma_start(out=d1, in_=s1),
                                             inc=f"{sname}{slot}", inc_n=16)
                        P.wait("tensor", f"{sname}{slot}", hsl_cnt[slot])
                        for nt in range(2):
                            b = 2 * slot + nt
                            P.wait("tensor", psM_free_eng[b], psM_free[b])
                            for ki in range(32):
                                te_h = P.op(
                                    "tensor", lambda e, slot=slot, ki=ki, nt=nt, b=b,
                                    src_sb=src_sb: e.matmul(
                                        ps_M[b][:],
                                        hsl_sb[slot][:, 128 * ki:128 * ki + 128],
                                        src_sb[:, TFF * ki + 512 * nt:
                                               TFF * ki + 512 * nt + 512],
                                        start=(ki == 0), stop=(ki == 31)),
                                    inc="te" if ki == 31 else None)
                            P.wait("scalar", "te", te_h)
                            c_g2 = P.op("scalar", lambda e, b=b, mc=mc, nt=nt,
                                        dst_sb=dst_sb, s_ap=s_ap: e.activation(
                                            dst_sb[:, TFF * mc + 512 * nt:
                                                   TFF * mc + 512 * nt + 512],
                                            ps_M[b][:], AF.Gelu,
                                            scale=s_ap[:, mc:mc + 1]), inc="se")
                            psM_free[b] = c_g2
                            psM_free_eng[b] = "se"
                        hsl_free_te[slot] = te_h
                    return P.val("se")

                a_ve = P.val("ve")
                P.wait("tensor", "ve", a_ve)
                se_h1 = h_stage(d_h1b, sh1_sb, a_sb, h_sb, "lh1_", ft < 3)
                P.wait("tensor", "se", se_h1)
                se_h2 = h_stage(d_h2b, sh2_sb, h_sb, a_sb, "lh2_", False)
                P.wait("tensor", "se", se_h2)

                for oc in range(8):
                    slot = oc % 2
                    P.wait("scalar", "te", hsl_free_te[slot])
                    src = d_w3b[:, 128 * oc:128 * oc + 128].rearrange(
                        "(ki p) o -> p ki o", p=128)
                    dst = hsl_sb[slot][:].rearrange("p (ki o) -> p ki o", ki=32)
                    hsl_cnt[slot] = P.op("scalar", lambda e, s1=src, d1=dst:
                                         e.dma_start(out=d1, in_=s1),
                                         inc=f"lw3_{slot}", inc_n=16)
                    P.wait("tensor", f"lw3_{slot}", hsl_cnt[slot])
                    for nt in range(2):
                        b = 2 * slot + nt
                        P.wait("tensor", psM_free_eng[b], psM_free[b])
                        for ki in range(32):
                            te_w3 = P.op(
                                "tensor", lambda e, slot=slot, ki=ki, nt=nt, b=b:
                                e.matmul(ps_M[b][:],
                                         hsl_sb[slot][:, 128 * ki:128 * ki + 128],
                                         a_sb[:, TFF * ki + 512 * nt:
                                              TFF * ki + 512 * nt + 512],
                                         start=(ki == 0), stop=(ki == 31)),
                                inc="te" if ki == 31 else None)
                        P.wait("vector", "te", te_w3)
                        c_f = P.op("vector", lambda e, b=b, oc=oc, nt=nt:
                                   e.tensor_scalar(
                                       fT_sb[:, TFF * oc + 512 * nt:
                                             TFF * oc + 512 * nt + 512],
                                       ps_M[b][:], s3_sb[:, oc:oc + 1], None,
                                       ALU.mult), inc="ve")
                        psM_free[b] = c_f
                        psM_free_eng[b] = "ve"
                    hsl_free_te[slot] = te_w3
                fT_ve = P.val("ve")
                P.wait("tensor", "ve", fT_ve)

                for u8 in range(8):
                    g_tt = 8 * ft + u8
                    hp = u8 % 2
                    tp2 = trp2[0]
                    P.wait("tensor", "ve", tr2_evac_ve[tp2])
                    for f in range(8):
                        te_tr = P.op("tensor", lambda e, f=f, u8=u8, tp2=tp2:
                                     e.transpose(
                                         ps_tr2[:, 1024 * tp2 + 128 * f:
                                                1024 * tp2 + 128 * f + 128],
                                         fT_sb[:, TFF * f + 128 * u8:
                                               TFF * f + 128 * u8 + 128],
                                         idb_sb[:]), inc="te" if f == 7 else None)
                    P.wait("vector", "te", te_tr)
                    c_ftm = P.op("vector", lambda e, tp2=tp2: e.tensor_copy(
                        ftm_sb, ps_tr2[:, 1024 * tp2:1024 * tp2 + 1024]),
                        inc="ve")
                    tr2_evac_ve[tp2] = c_ftm
                    trp2[0] ^= 1
                    P.wait("sync", "ve", x2t_free[hp])
                    lx2_cnt[hp] = P.op("sync", lambda e, g=g_tt, hp=hp: e.dma_start(
                        out=x2t_sb[hp][:], in_=d_x2[128 * g:128 * g + 128, :]),
                        inc=f"lx2{hp}", inc_n=16)
                    c_ln = plan_ln(P, ftm_sb, t1_sb[:], E, 1e-5, stats[2 + hp],
                                   ready_val=c_ftm)
                    P.wait("vector", f"lx2{hp}", lx2_cnt[hp])
                    P.wait("vector", "ve", c_ln)
                    c_o = P.op("vector", lambda e, hp=hp: e.tensor_tensor(
                        x2t_sb[hp][:], x2t_sb[hp][:], t1_sb[:], ALU.add), inc="ve")
                    x2t_free[hp] = c_o
                    P.wait("sync", "ve", c_o)
                    out_free[hp] = P.op("sync", lambda e, g=g_tt, hp=hp: e.dma_start(
                        out=d_out[128 * g:128 * g + 128, :], in_=x2t_sb[hp][:]),
                        inc=f"sout{hp}", inc_n=16)
                    P.wait("sync", f"sout{hp}", out_free[hp])

            P.wait("sync", "sout0", out_free[0])
            P.wait("sync", "sout1", out_free[1])
            P.flush(block)

    return nc


_PERM = None


def _perm():
    global _PERM
    if _PERM is None:
        p = np.arange(NT)
        _PERM = 16 * (p % 256) + p // 256
    return _PERM


def _prep_core_inputs(xb, d):
    perm = _perm()
    rel = d["L0_attn_rel_table"]
    jj = perm // 16
    cc = perm % 16
    gr = np.arange(2048)
    idx = np.minimum(2047 - gr[None, :] + jj[:, None], 2046)
    fvt = np.ascontiguousarray(rel[idx, cc[:, None]])
    cmat = np.ascontiguousarray(
        np.broadcast_to(np.repeat(rel[2046], 2)[None, :], (128, 32)))
    return {
        "x": np.ascontiguousarray(xb[perm]),
        "wq": d["L0_attn_Wq"], "wk": d["L0_attn_Wk"], "wv": d["L0_attn_Wv"],
        "fvt": fvt, "cmat": cmat,
        "gamma_b": np.ascontiguousarray(
            np.broadcast_to(d["L0_attn_gamma"][None, :], (128, E))),
        "beta_b": np.ascontiguousarray(
            np.broadcast_to(d["L0_attn_beta"][None, :], (128, E))),
        "w1t": np.ascontiguousarray(d["L0_ffn_w1_w"].T),
        "w2t": np.ascontiguousarray(d["L0_ffn_w2_w"].T),
        "h1t": np.ascontiguousarray(d["L0_ffn_h1_w"].T),
        "h2t": np.ascontiguousarray(d["L0_ffn_h2_w"].T),
        "w3t": np.ascontiguousarray(d["L0_ffn_w3_w"].T),
        "s1m": np.ascontiguousarray(d["L0_ffn_w1_s"].reshape(32, 128).T),
        "s2m": np.ascontiguousarray(d["L0_ffn_w2_s"].reshape(32, 128).T),
        "sh1m": np.ascontiguousarray(d["L0_ffn_h1_s"].reshape(32, 128).T),
        "sh2m": np.ascontiguousarray(d["L0_ffn_h2_s"].reshape(32, 128).T),
        "s3m": np.ascontiguousarray(d["L0_ffn_w3_s"].reshape(8, 128).T),
        "id_bf": np.eye(128, dtype=np.float32).astype(ml_dtypes.bfloat16),
        "id_f32": np.eye(128, dtype=np.float32),
    }


def kernel(**inputs):
    x = np.asarray(inputs["L0_attn_x"], dtype=np.float32)
    d = {k: np.asarray(v, dtype=np.float32) for k, v in inputs.items()}
    nc = build_program()
    in_maps = [_prep_core_inputs(x[b], d) for b in range(8)]
    res = run_bass_kernel_spmd(nc, in_maps, core_ids=list(range(8)))
    perm = _perm()
    out = np.empty((8, NT, E), dtype=np.float32)
    for b in range(8):
        out[b][perm] = res.results[b]["out"]
    return out


# version 06
# speedup vs baseline: 1.0716x; 1.0716x over previous
import numpy as np
import ml_dtypes
import concourse.bass as bass
import concourse.mybir as mybir
from concourse.bass_utils import run_bass_kernel_spmd
from contextlib import ExitStack

F32 = mybir.dt.float32
BF16 = mybir.dt.bfloat16
AF = mybir.ActivationFunctionType
ALU = mybir.AluOpType
AX = mybir.AxisListType

NT = 4096
E = 1024
H = 16
HID = 4096
NQC = 8
TFF = 1024

ENGINES = ("sync", "scalar", "vector", "tensor", "gpsimd")


class Plan:

    def __init__(self, nc, stack):
        self.nc = nc
        self.stack = stack
        self.thunks = {e: [] for e in ENGINES}
        self.sems = {}
        self.counts = {}

    def sem(self, name):
        if name not in self.sems:
            self.sems[name] = self.stack.enter_context(self.nc.semaphore(name))
            self.counts[name] = 0
        return self.sems[name]

    def op(self, eng, fn, inc=None, inc_n=1):
        if inc is not None:
            self.sem(inc)
            self.counts[inc] += inc_n
            target = self.counts[inc]
            sem = self.sems[inc]

            def thunk(e, fn=fn, sem=sem, inc_n=inc_n):
                fn(e).then_inc(sem, inc_n)
        else:
            target = None

            def thunk(e, fn=fn):
                fn(e)
        self.thunks[eng].append(thunk)
        return target

    def wait(self, eng, name, val):
        if val is None or val <= 0:
            return
        sem = self.sem(name)
        self.thunks[eng].append(lambda e, s=sem, v=val: e.wait_ge(s, v))

    def val(self, name):
        self.sem(name)
        return self.counts[name]

    def phase_barrier(self, exclude=()):
        snap = [(n, v) for n, v in self.counts.items()
                if v > 0 and n not in exclude]
        for eng in ENGINES:
            for n, v in snap:
                self.wait(eng, n, v)

    def flush(self, block):
        for eng in ENGINES:
            if not self.thunks[eng]:
                continue
            fns = self.thunks[eng]
            self.thunks[eng] = []
            getattr(block, eng)(lambda e, fns=fns: [f(e) for f in fns])


def plan_ln(P, x_ap, out_ap, n, eps, sl, ready_val=0):
    P.wait("vector", "ve", ready_val)
    c1 = P.op("vector", lambda e: e.reduce_sum(sl["rs"], x_ap, axis=AX.X), inc="ve")
    P.wait("vector", "ve", c1)
    c2 = P.op("vector", lambda e: e.tensor_scalar_mul(sl["nmu"], sl["rs"], -1.0 / n),
              inc="ve")
    P.wait("scalar", "ve", c2)
    c3 = P.op("scalar", lambda e: e.activation(sl["sq_scr"], x_ap, AF.Square,
                                               bias=sl["nmu"], scale=1.0,
                                               accum_out=sl["var"]), inc="se")
    P.wait("vector", "se", c3)
    c4 = P.op("vector", lambda e: e.tensor_scalar(sl["var"], sl["var"], 1.0 / n, eps,
                                                  ALU.mult, ALU.add), inc="ve")
    P.wait("scalar", "ve", c4)
    c5 = P.op("scalar", lambda e: e.activation(sl["sd"], sl["var"], AF.Sqrt), inc="se")
    P.wait("vector", "se", c5)
    c6 = P.op("vector", lambda e: e.reciprocal(sl["rstd"], sl["sd"]), inc="ve")
    P.wait("vector", "ve", c6)
    c7 = P.op("vector", lambda e: e.tensor_tensor(sl["nmur"], sl["nmu"], sl["rstd"],
                                                  ALU.mult), inc="ve")
    P.wait("vector", "ve", c7)
    c8 = P.op("vector", lambda e: e.tensor_scalar(out_ap, x_ap, sl["rstd"], sl["nmur"],
                                                  ALU.mult, ALU.add), inc="ve")
    return c8


def build_program(debug=False, phases=3, nqc=NQC, sub="all"):
    nc = bass.Bass("TRN2", target_bir_lowering=False)

    d_x = nc.dram_tensor("x", [NT, E], F32, kind="ExternalInput").ap()
    d_wq = nc.dram_tensor("wq", [E, E], F32, kind="ExternalInput").ap()
    d_wk = nc.dram_tensor("wk", [E, E], F32, kind="ExternalInput").ap()
    d_wv = nc.dram_tensor("wv", [E, E], F32, kind="ExternalInput").ap()
    d_fvt = nc.dram_tensor("fvt", [NT, 2048], F32, kind="ExternalInput").ap()
    d_cmat = nc.dram_tensor("cmat", [128, 32], F32, kind="ExternalInput").ap()
    d_gb = nc.dram_tensor("gamma_b", [128, E], F32, kind="ExternalInput").ap()
    d_bb = nc.dram_tensor("beta_b", [128, E], F32, kind="ExternalInput").ap()
    d_w1t = nc.dram_tensor("w1t", [E, HID], F32, kind="ExternalInput").ap()
    d_w2t = nc.dram_tensor("w2t", [E, HID], F32, kind="ExternalInput").ap()
    d_h1t = nc.dram_tensor("h1t", [HID, HID], F32, kind="ExternalInput").ap()
    d_h2t = nc.dram_tensor("h2t", [HID, HID], F32, kind="ExternalInput").ap()
    d_w3t = nc.dram_tensor("w3t", [HID, E], F32, kind="ExternalInput").ap()
    d_s1 = nc.dram_tensor("s1m", [128, 32], F32, kind="ExternalInput").ap()
    d_s2 = nc.dram_tensor("s2m", [128, 32], F32, kind="ExternalInput").ap()
    d_sh1 = nc.dram_tensor("sh1m", [128, 32], F32, kind="ExternalInput").ap()
    d_sh2 = nc.dram_tensor("sh2m", [128, 32], F32, kind="ExternalInput").ap()
    d_s3 = nc.dram_tensor("s3m", [128, 8], F32, kind="ExternalInput").ap()
    d_idb = nc.dram_tensor("id_bf", [128, 128], BF16, kind="ExternalInput").ap()
    d_idf = nc.dram_tensor("id_f32", [128, 128], F32, kind="ExternalInput").ap()
    d_out = nc.dram_tensor("out", [NT, E], F32, kind="ExternalOutput").ap()

    sk = "ExternalOutput" if debug else "Internal"
    d_qs = nc.dram_tensor("qs", [NQC, E, 512], BF16, kind=sk).ap()
    d_ks = nc.dram_tensor("ks", [NQC, E, 512], BF16, kind=sk).ap()
    d_vs = nc.dram_tensor("vs", [NT, H * 65], BF16, kind=sk).ap()
    d_x2 = nc.dram_tensor("x2s", [NT, E], F32, kind=sk).ap()
    d_fvtb = nc.dram_tensor("fvtb", [NT, 2048], BF16, kind=sk).ap()
    d_w1b = nc.dram_tensor("w1b", [32, 128, 1024], BF16, kind=sk).ap()
    d_w2b = nc.dram_tensor("w2b", [32, 128, 1024], BF16, kind="Internal").ap()
    d_h1b = nc.dram_tensor("h1b", [32, 128, 4096], BF16, kind="Internal").ap()
    d_h2b = nc.dram_tensor("h2b", [32, 128, 4096], BF16, kind="Internal").ap()
    d_w3b = nc.dram_tensor("w3b", [8, 128, 4096], BF16, kind="Internal").ap()

    stack = ExitStack()
    with stack:
        P = Plan(nc, stack)
        sb = lambda name, shape, dt: stack.enter_context(nc.sbuf_tensor(name, shape, dt))

        gb_sb = sb("gb", [128, E], F32)
        bb_sb = sb("bb", [128, E], F32)
        c_f32 = sb("c_f32", [128, 32], F32)
        c_bf = sb("c_bf", [128, 32], BF16)
        s1_sb = sb("s1", [128, 32], F32)
        s2_sb = sb("s2", [128, 32], F32)
        sh1_sb = sb("sh1", [128, 32], F32)
        sh2_sb = sb("sh2", [128, 32], F32)
        s3_sb = sb("s3", [128, 8], F32)
        idb_sb = sb("idb", [128, 128], BF16)
        idf_sb = sb("idf", [128, 128], F32)
        ones_f = sb("ones_f", [1, 128], F32)
        w_bc = sb("w_bc", [128, H * 65], F32)
        sq_scr = sb("sq_scr", [128, E], F32)
        t1_sb = sb("t1", [128, E], F32)
        st_t = sb("stats", [128, 32], F32)
        stats = []
        for s in range(4):
            stats.append({"rs": st_t[:, 8 * s:8 * s + 1],
                          "nmu": st_t[:, 8 * s + 1:8 * s + 2],
                          "var": st_t[:, 8 * s + 2:8 * s + 3],
                          "sd": st_t[:, 8 * s + 3:8 * s + 4],
                          "rstd": st_t[:, 8 * s + 4:8 * s + 5],
                          "nmur": st_t[:, 8 * s + 5:8 * s + 6],
                          "sq_scr": sq_scr[:]})

        sAB = ExitStack()
        pcst = [sAB.enter_context(nc.sbuf_tensor(f"pcst{i}", [128, 4096], BF16))
                for i in range(2)]

        block = stack.enter_context(nc.Block())

        with ExitStack() as sA:
            sbA = lambda n_, s_, d_: sA.enter_context(nc.sbuf_tensor(n_, s_, d_))
            wq_sb = sbA("wq_sb", [128, 8 * E], BF16)
            wk_sb = sbA("wk_sb", [128, 8 * E], BF16)
            wv_sb = sbA("wv_sb", [128, 8 * E], BF16)
            xn1_sb = [sbA(f"xn1_{i}", [128, E], BF16) for i in range(2)]
            xnT_sb = sbA("xnT", [128, 8 * 512], BF16)
            qe_sb = [sbA(f"qe{i}", [128, 512], BF16) for i in range(2)]
            ke_sb = [sbA(f"ke{i}", [128, 512], BF16) for i in range(2)]
            va_sb = [sbA(f"va{i}", [128, H * 65], BF16) for i in range(2)]
            w_row = sbA("w_row", [1, H * 65], F32)
            w_hi = sbA("w_hi", [1, H * 65], BF16)
            w_rem = sbA("w_rem", [1, H * 65], F32)
            w_lo = sbA("w_lo", [1, H * 65], BF16)
            ones_b = sbA("ones_b", [1, 128], BF16)
            x_sb = [sbA(f"xA{i}", [128, E], F32) for i in range(2)]
            ps_mm = sA.enter_context(nc.psum_tensor("ps_mm", [128, 1024], F32))
            ps_tr = sA.enter_context(nc.psum_tensor("ps_tr", [128, 2048], BF16))
            ps_w = sA.enter_context(nc.psum_tensor("ps_w", [1, 1536], F32))

            for dd, ss in [(d_gb, gb_sb), (d_bb, bb_sb), (d_cmat, c_f32),
                           (d_s1, s1_sb), (d_s2, s2_sb), (d_sh1, sh1_sb),
                           (d_sh2, sh2_sb), (d_s3, s3_sb), (d_idb, idb_sb),
                           (d_idf, idf_sb)]:
                l_set = P.op("sync", lambda e, dd=dd, ss=ss:
                             e.dma_start(out=ss[:], in_=dd[:]), inc="l_set", inc_n=16)
            for dd, ss in [(d_wq, wq_sb), (d_wk, wk_sb), (d_wv, wv_sb)]:
                src = dd.rearrange("(ki p) o -> p ki o", p=128)
                dst = ss[:].rearrange("p (ki o) -> p ki o", ki=8)
                l_wqkv = P.op("gpsimd", lambda e, s1=src, d1=dst:
                              e.dma_start(out=d1, in_=s1), inc="l_wqkv", inc_n=16)
            pc_ld = [0, 0]
            pc_st = [0, 0]
            pc_idx = 0
            for c in range(32):
                sl = pc_idx % 2
                pc_idx += 1
                P.wait("gpsimd", f"pcs{sl}", pc_st[sl])
                pc_ld[sl] = P.op("gpsimd", lambda e, c=c, sl=sl: e.dma_start(
                    out=pcst[sl][:, 0:2048],
                    in_=d_fvt[128 * c:128 * c + 128, :]), inc=f"pcl{sl}", inc_n=16)
                P.wait("gpsimd", f"pcl{sl}", pc_ld[sl])
                pc_st[sl] = P.op("gpsimd", lambda e, c=c, sl=sl: e.dma_start(
                    out=d_fvtb[128 * c:128 * c + 128, :],
                    in_=pcst[sl][:, 0:2048]), inc=f"pcs{sl}", inc_n=16)
            fvt_pc_done = (pc_st[0], pc_st[1])
            for dsrc, ddst, nsl, nki in [(d_w1t, d_w1b, 32, 8),
                                         (d_w2t, d_w2b, 32, 8),
                                         (d_h1t, d_h1b, 32, 32),
                                         (d_h2t, d_h2b, 32, 32),
                                         (d_w3t, d_w3b, 8, 32)]:
                wid = nki * 128
                for c in range(nsl):
                    sl = pc_idx % 2
                    pc_idx += 1
                    srcap = dsrc[:, 128 * c:128 * c + 128].rearrange(
                        "(ki p) o -> p ki o", p=128)
                    dstap = pcst[sl][:, 0:wid].rearrange(
                        "p (ki o) -> p ki o", ki=nki)
                    P.wait("gpsimd", f"pcs{sl}", pc_st[sl])
                    pc_ld[sl] = P.op("gpsimd", lambda e, srcap=srcap, dstap=dstap,
                                     sl=sl: e.dma_start(out=dstap, in_=srcap),
                                     inc=f"pcl{sl}", inc_n=16)
                    P.wait("gpsimd", f"pcl{sl}", pc_ld[sl])
                    pc_st[sl] = P.op("gpsimd", lambda e, ddst=ddst, c=c, sl=sl,
                                     wid=wid: e.dma_start(
                                         out=ddst[c, :, :],
                                         in_=pcst[sl][:, 0:wid]),
                                     inc=f"pcs{sl}", inc_n=16)
            P.wait("vector", "l_set", l_set)
            P.op("vector", lambda e: e.tensor_copy(c_bf[:], c_f32[:]), inc="ve")
            P.op("vector", lambda e: e.memset(ones_f[:], 1.0), inc="ve")
            setup_ve = P.op("vector", lambda e: e.memset(ones_b[:], 1.0), inc="ve")

            lx_cnt = [0, 0]
            xn_free_ve = [0, 0]
            x_freeA = [0, 0]
            stq_free = [0, 0]
            stk_free = [0, 0]
            stv_free = [0, 0]
            trp = 0
            tr_evac_ve = [0, 0]
            mm_free_ve = [0, 0]
            te_w = 0
            last_g = 4 * nqc - 1
            for qc in range(nqc):
                for t in range(4):
                    g_tt = 4 * qc + t
                    s = g_tt % 2
                    P.wait("sync", "ve", x_freeA[s])
                    lx_cnt[s] = P.op("sync", lambda e, g=g_tt, s=s: e.dma_start(
                        out=x_sb[s][:], in_=d_x[128 * g:128 * g + 128, :]),
                        inc=f"lx{s}", inc_n=16)
                    P.wait("vector", f"lx{s}", lx_cnt[s])
                    ln_ve = plan_ln(P, x_sb[s][:], xn1_sb[s][:], E, 1e-5, stats[s],
                                    ready_val=xn_free_ve[s])
                    x_freeA[s] = ln_ve
                    P.wait("tensor", "ve", ln_ve)
                    P.wait("tensor", "ve", tr_evac_ve[trp])
                    for f in range(8):
                        te_tr = P.op("tensor", lambda e, s=s, f=f, trp=trp: e.transpose(
                            ps_tr[:, 1024 * trp + 128 * f:1024 * trp + 128 * f + 128],
                            xn1_sb[s][:, 128 * f:128 * f + 128], idb_sb[:]),
                            inc="te" if f == 7 else None)
                    P.wait("vector", "te", te_tr)
                    for f in range(8):
                        ev = P.op("vector", lambda e, f=f, t=t, trp=trp: e.tensor_copy(
                            xnT_sb[:, 512 * f + 128 * t:512 * f + 128 * t + 128],
                            ps_tr[:, 1024 * trp + 128 * f:1024 * trp + 128 * f + 128]),
                            inc="ve" if f == 7 else None)
                    tr_evac_ve[trp] = ev
                    xn_free_ve[s] = ev
                    trp ^= 1
                if sub not in ("qk", "all"):
                    continue
                P.wait("tensor", "l_wqkv", l_wqkv)
                P.wait("tensor", "ve", setup_ve)
                xnT_ready = P.val("ve")
                P.wait("tensor", "ve", xnT_ready)
                for w_sb, ev_sb, ev_free, d_dst, sem_st in [
                        (wq_sb, qe_sb, stq_free, d_qs, "stq"),
                        (wk_sb, ke_sb, stk_free, d_ks, "stk")]:
                    for f in range(8):
                        hp = f % 2
                        P.wait("tensor", "ve", mm_free_ve[hp])
                        for ki in range(8):
                            te_mm = P.op(
                                "tensor", lambda e, w_sb=w_sb, f=f, ki=ki, hp=hp:
                                e.matmul(ps_mm[:, 512 * hp:512 * hp + 512],
                                         w_sb[:, E * ki + 128 * f:E * ki + 128 * f + 128],
                                         xnT_sb[:, 512 * ki:512 * ki + 512],
                                         start=(ki == 0), stop=(ki == 7)),
                                inc="te" if ki == 7 else None)
                        P.wait("vector", "te", te_mm)
                        P.wait("vector", f"{sem_st}{hp}", ev_free[hp])
                        cv = P.op("vector", lambda e, ev_sb=ev_sb, hp=hp: e.tensor_copy(
                            ev_sb[hp][:], ps_mm[:, 512 * hp:512 * hp + 512]), inc="ve")
                        mm_free_ve[hp] = cv
                        P.wait("sync", "ve", cv)
                        ev_free[hp] = P.op(
                            "sync", lambda e, ev_sb=ev_sb, hp=hp, qc=qc, f=f,
                            d_dst=d_dst: e.dma_start(
                                out=d_dst[qc, 128 * f:128 * f + 128, :],
                                in_=ev_sb[hp][:]),
                            inc=f"{sem_st}{hp}", inc_n=16)
                if sub not in ("v", "all", "vmm", "vst"):
                    continue
                for t in range(4):
                    g_tt = 4 * qc + t
                    hp = t % 2
                    P.wait("vector", f"stv{hp}", stv_free[hp])
                    mv = P.op("vector", lambda e, hp=hp: e.memset(va_sb[hp][:], 1.0),
                              inc="ve")
                    for nf in range(2):
                        P.wait("tensor", "ve", mm_free_ve[nf])
                        for ki in range(8):
                            te_mm = P.op("tensor", lambda e, t=t, ki=ki, nf=nf:
                                         e.matmul(
                                             ps_mm[:, 512 * nf:512 * nf + 512],
                                             xnT_sb[:, 512 * ki + 128 * t:
                                                    512 * ki + 128 * t + 128],
                                             wv_sb[:, E * ki + 512 * nf:
                                                   E * ki + 512 * nf + 512],
                                             start=(ki == 0), stop=(ki == 7)),
                                         inc="te" if ki == 7 else None)
                        P.wait("vector", "te", te_mm)
                        P.wait("vector", "ve", mv)
                        dst = va_sb[hp][:].rearrange("p (h c) -> p h c", c=65)[
                            :, 8 * nf:8 * nf + 8, 0:64]
                        src = ps_mm[:, 512 * nf:512 * nf + 512].rearrange(
                            "p (h c) -> p h c", c=64)
                        cv = P.op("vector", lambda e, dst=dst, src=src:
                                  e.tensor_copy(dst, src), inc="ve")
                        mm_free_ve[nf] = cv
                    va_ve = P.val("ve")
                    if sub != "vmm":
                        P.wait("sync", "ve", va_ve)
                        stv_free[hp] = P.op("sync", lambda e, hp=hp, g=g_tt: e.dma_start(
                            out=d_vs[128 * g:128 * g + 128, :], in_=va_sb[hp][:]),
                            inc=f"stv{hp}", inc_n=16)
                    if sub in ("vmm", "vst"):
                        continue
                    P.wait("tensor", "ve", va_ve)
                    for ch, (c0, c1) in enumerate([(0, 512), (512, 1024),
                                                   (1024, 1040)]):
                        te_w = P.op("tensor", lambda e, g=g_tt, hp=hp, c0=c0, c1=c1:
                                    e.matmul(ps_w[0:1, c0:c1], c_bf[:, g:g + 1],
                                             va_sb[hp][:, c0:c1],
                                             start=(g == 0), stop=(g == last_g),
                                             skip_group_check=True),
                                    inc="te" if (ch == 2 and g_tt == last_g) else None)
            w_ready_ve = 0
            if sub in ("v", "all"):
                P.wait("vector", "te", te_w)
                cw = P.op("vector", lambda e: e.tensor_copy(
                    w_row[:], ps_w[0:1, 0:H * 65]), inc="ve")
                P.wait("vector", "ve", cw)
                c_hi = P.op("vector", lambda e: e.tensor_copy(w_hi[:], w_row[:]),
                            inc="ve")
                P.wait("vector", "ve", c_hi)
                c_rm = P.op("vector", lambda e: e.tensor_tensor(
                    w_rem[:], w_row[:], w_hi[:], ALU.subtract), inc="ve")
                P.wait("vector", "ve", c_rm)
                c_lo = P.op("vector", lambda e: e.tensor_copy(w_lo[:], w_rem[:]),
                            inc="ve")
                P.wait("tensor", "ve", c_lo)
                P.wait("tensor", "ve", mm_free_ve[0])
                P.wait("tensor", "ve", mm_free_ve[1])
                for c0, c1 in [(0, 512), (512, 1024), (1024, 1040)]:
                    P.op("tensor", lambda e, c0=c0, c1=c1: e.matmul(
                        ps_mm[:, 0:c1 - c0], ones_b[:], w_hi[0:1, c0:c1],
                        start=True, stop=False))
                    tw = P.op("tensor", lambda e, c0=c0, c1=c1: e.matmul(
                        ps_mm[:, 0:c1 - c0], ones_b[:], w_lo[0:1, c0:c1],
                        start=False, stop=True), inc="te")
                    P.wait("vector", "te", tw)
                    cbc = P.op("vector", lambda e, c0=c0, c1=c1: e.tensor_copy(
                        w_bc[:, c0:c1], ps_mm[:, 0:c1 - c0]), inc="ve")
                    P.wait("tensor", "ve", cbc)
                    w_ready_ve = cbc
            a_store_totals = {"stq0": stq_free[0], "stq1": stq_free[1],
                              "stk0": stk_free[0], "stk1": stk_free[1],
                              "stv0": stv_free[0], "stv1": stv_free[1]}
            if phases <= 1:
                for s_name, tot in a_store_totals.items():
                    P.wait("sync", s_name, tot)
            P.flush(block)
        if phases <= 1:
            sAB.close()
            return nc

        with ExitStack() as sB:
            sbB = lambda n_, s_, d_: sB.enter_context(nc.sbuf_tensor(n_, s_, d_))
            kp_sb = [sbB(f"kp{i}", [128, NT], BF16) for i in range(2)]
            vp_sb = [sbB(f"vp{i}", [128, 32 * 130], BF16) for i in range(2)]
            qp_sb = [sbB(f"qp{i}", [128, 512], BF16) for i in range(2)]
            fvt_sb = [sbB(f"fvt{i}", [128, 32 * 256], BF16) for i in range(4)]
            p_sb = [sbB(f"p{i}", [128, 1024], BF16) for i in range(2)]
            of_sb = sbB("of", [128, 1024], F32)
            bvf_sb = sbB("bvf", [128, 1024], F32)
            bvt_sb = sbB("bvt", [128, 1024], F32)
            asm_sb = [sbB(f"asm{i}", [128, E], F32) for i in range(4)]
            x2_sb = [sbB(f"x2_{i}", [128, E], F32) for i in range(2)]
            t2_sb = sbB("t2", [128, E], F32)
            x_sb = [sbB(f"xB{i}", [128, E], F32) for i in range(2)]
            rcp_sb = sbB("rcp", [128, 8], F32)
            ps_S = [sB.enter_context(nc.psum_tensor(f"psS{i}", [128, 1024], F32))
                    for i in range(2)]
            ps_O = sB.enter_context(nc.psum_tensor("psO", [128, 1024], F32))
            ps_B = sB.enter_context(nc.psum_tensor("psB", [128, 1024], F32))

            P.phase_barrier(exclude=("pcl0", "pcl1", "pcs0", "pcs1"))
            P.wait("sync", "pcs0", fvt_pc_done[0])
            P.wait("sync", "pcs1", fvt_pc_done[1])

            kv_cnt = [0, 0]
            fvt_cnt = [0, 0, 0, 0]
            kv_free_te = [0, 0]
            fvt_free_te = [0, 0, 0, 0]
            psS_free_se = [0, 0]
            psO_free_ve = 0
            psB_free_ve = 0
            asm_free_ve = [0, 0, 0, 0]
            x2_free = [0, 0]
            x_freeB = [0, 0]
            lxb_cnt = [0, 0]

            for qc in range(NQC):
                for pr in range(8):
                    s = pr % 2
                    combos = []
                    for h01 in range(2):
                        for vh in range(2):
                            a = 2 * pr + h01
                            v = 2 * qc + vh
                            if a - v >= 8:
                                combos.append((h01, vh, 256 * (a - v) - 2048))
                    P.wait("sync", "te", kv_free_te[s])
                    src = d_ks[:, 128 * pr:128 * pr + 128, :].rearrange(
                        "b p c -> p b c")
                    dst = kp_sb[s][:].rearrange("p (b c) -> p b c", b=8)
                    P.op("sync", lambda e, s1=src, d1=dst: e.dma_start(out=d1, in_=s1),
                         inc=f"lkv{s}", inc_n=16)
                    src = d_vs[:, 130 * pr:130 * pr + 130].rearrange(
                        "(kc p) c -> p kc c", p=128)
                    dst = vp_sb[s][:].rearrange("p (kc c) -> p kc c", kc=32)
                    P.op("sync", lambda e, s1=src, d1=dst: e.dma_start(out=d1, in_=s1),
                         inc=f"lkv{s}", inc_n=16)
                    kv_cnt[s] = P.op("sync", lambda e, qc=qc, pr=pr, s=s: e.dma_start(
                        out=qp_sb[s][:], in_=d_qs[qc, 128 * pr:128 * pr + 128, :]),
                        inc=f"lkv{s}", inc_n=16)
                    for (h01, vh, gr0) in combos:
                        slot = 2 * h01 + vh
                        P.wait("sync", "te", fvt_free_te[slot])
                        src = d_fvtb[:, gr0:gr0 + 256].rearrange(
                            "(kc p) c -> p kc c", p=128)
                        dst = fvt_sb[slot][:].rearrange("p (kc c) -> p kc c", kc=32)
                        fvt_cnt[slot] = P.op("sync", lambda e, s1=src, d1=dst:
                                             e.dma_start(out=d1, in_=s1),
                                             inc=f"lfv{slot}", inc_n=16)

                    P.wait("tensor", f"lkv{s}", kv_cnt[s])
                    s_te = {}
                    exp_se = {}
                    pv_te = {}

                    def emit_S(kc, s=s):
                        b = kc % 2
                        P.wait("tensor", "se", psS_free_se[b])
                        P.op("tensor", lambda e, kc=kc, b=b, s=s: e.matmul(
                            ps_S[b][:, 0:512],
                            kp_sb[s][0:64, 128 * kc:128 * kc + 128],
                            qp_sb[s][0:64, :], start=True, stop=True))
                        s_te[kc] = P.op("tensor", lambda e, kc=kc, b=b, s=s: e.matmul(
                            ps_S[b][:, 512:1024],
                            kp_sb[s][64:128, 128 * kc:128 * kc + 128],
                            qp_sb[s][64:128, :], start=True, stop=True), inc="te")

                    def emit_exp(kc):
                        b = kc % 2
                        gate = pv_te[kc - 2] if kc >= 2 else s_te[kc]
                        P.wait("scalar", "te", gate)
                        c = P.op("scalar", lambda e, b=b: e.activation(
                            p_sb[b][:], ps_S[b][:], AF.Exp, scale=1.0 / 32.0),
                            inc="se")
                        exp_se[kc] = c
                        psS_free_se[b] = c

                    def emit_PV(kc, s=s, combos=combos):
                        b = kc % 2
                        P.wait("tensor", "se", exp_se[kc])
                        if kc == 0:
                            P.wait("tensor", "ve", psO_free_ve)
                            if combos:
                                P.wait("tensor", "ve", psB_free_ve)
                                for (h01, vh, gr0) in combos:
                                    slot = 2 * h01 + vh
                                    P.wait("tensor", f"lfv{slot}", fvt_cnt[slot])
                        n_ops = 2 + len(combos)
                        i = 0
                        t = None
                        for h01 in range(2):
                            i += 1
                            t = P.op("tensor", lambda e, kc=kc, b=b, s=s, h01=h01:
                                     e.matmul(
                                         ps_O[0:65, 512 * h01:512 * h01 + 512],
                                         vp_sb[s][:, 130 * kc + 65 * h01:
                                                   130 * kc + 65 * h01 + 65],
                                         p_sb[b][:, 512 * h01:512 * h01 + 512],
                                         start=(kc == 0), stop=(kc == 31),
                                         skip_group_check=True),
                                     inc="te" if i == n_ops else None)
                        seen_banks = set()
                        for (h01, vh, gr0) in combos:
                            i += 1
                            slot = 2 * h01 + vh
                            bank = slot // 2
                            st_flag = (kc == 0) and bank not in seen_banks
                            if kc == 0:
                                seen_banks.add(bank)
                            t = P.op("tensor", lambda e, kc=kc, b=b, s=s, h01=h01,
                                     slot=slot, st_flag=st_flag: e.matmul(
                                         ps_B[0:64, 256 * slot:256 * slot + 256],
                                         vp_sb[s][:, 130 * kc + 65 * h01:
                                                   130 * kc + 65 * h01 + 64],
                                         fvt_sb[slot][:, 256 * kc:256 * kc + 256],
                                         start=st_flag, stop=(kc == 31),
                                         skip_group_check=True),
                                     inc="te" if i == n_ops else None)
                        pv_te[kc] = t

                    emit_S(0)
                    emit_exp(0)
                    emit_S(1)
                    emit_exp(1)
                    for kc in range(32):
                        if kc + 2 < 32:
                            emit_S(kc + 2)
                        emit_PV(kc)
                        if kc + 2 < 32:
                            emit_exp(kc + 2)
                    last_te = pv_te[31]
                    kv_free_te[s] = last_te
                    for (h01, vh, gr0) in combos:
                        fvt_free_te[2 * h01 + vh] = last_te

                    P.wait("vector", "te", last_te)
                    evac_ve = P.op("vector", lambda e: e.tensor_copy(
                        of_sb[0:65, :], ps_O[0:65, :]),
                        inc="ve" if not combos else None)
                    for ci, (h01, vh, gr0) in enumerate(combos):
                        slot = 2 * h01 + vh
                        evac_ve = P.op("vector", lambda e, slot=slot: e.tensor_copy(
                            bvf_sb[0:64, 256 * slot:256 * slot + 256],
                            ps_B[0:64, 256 * slot:256 * slot + 256]),
                            inc="ve" if ci == len(combos) - 1 else None)
                    P.wait("tensor", "ve", evac_ve)
                    n_tr = 8 + 2 * len(combos)
                    i = 0
                    for h01 in range(2):
                        for u in range(4):
                            i += 1
                            tr_te = P.op("tensor", lambda e, h01=h01, u=u: e.transpose(
                                ps_O[:, 512 * h01 + 65 * u:512 * h01 + 65 * u + 65],
                                of_sb[0:65, 512 * h01 + 128 * u:
                                      512 * h01 + 128 * u + 128],
                                idf_sb[0:65, 0:65]),
                                inc="te" if i == n_tr else None)
                    for (h01, vh, gr0) in combos:
                        slot = 2 * h01 + vh
                        for uu in range(2):
                            i += 1
                            tr_te = P.op("tensor", lambda e, slot=slot, uu=uu:
                                         e.transpose(
                                             ps_B[:, 256 * slot + 64 * uu:
                                                  256 * slot + 64 * uu + 64],
                                             bvf_sb[0:64, 256 * slot + 128 * uu:
                                                    256 * slot + 128 * uu + 128],
                                             idf_sb[0:64, 0:64]),
                                         inc="te" if i == n_tr else None)
                    P.wait("vector", "te", tr_te)
                    if combos:
                        cb = None
                        for (h01, vh, gr0) in combos:
                            slot = 2 * h01 + vh
                            cb = P.op("vector", lambda e, slot=slot: e.tensor_copy(
                                bvt_sb[:, 256 * slot:256 * slot + 256],
                                ps_B[:, 256 * slot:256 * slot + 256]), inc="ve")
                        P.wait("vector", "ve", cb)
                    for h01 in range(2):
                        a = 2 * pr + h01
                        for u in range(4):
                            vh = u // 2
                            cr = P.op("vector", lambda e, h01=h01, u=u: e.reciprocal(
                                rcp_sb[:, 4 * h01 + u:4 * h01 + u + 1],
                                ps_O[:, 512 * h01 + 65 * u + 64:
                                     512 * h01 + 65 * u + 65]), inc="ve")
                            P.wait("vector", "ve", cr)
                            aff = any(c[0] == h01 and c[1] == vh for c in combos)
                            if aff:
                                slot = 2 * h01 + vh
                                in1 = bvt_sb[:, 256 * slot + 64 * (u % 2):
                                             256 * slot + 64 * (u % 2) + 64]
                            else:
                                in1 = w_bc[:, 65 * a:65 * a + 64]
                            P.wait("vector", "ve", asm_free_ve[u])
                            last = h01 == 1 and u == 3
                            stt_ve = P.op(
                                "vector", lambda e, h01=h01, u=u, a=a, in1=in1:
                                e.scalar_tensor_tensor(
                                    asm_sb[u][:, 64 * a:64 * a + 64],
                                    ps_O[:, 512 * h01 + 65 * u:
                                         512 * h01 + 65 * u + 64],
                                    rcp_sb[:, 4 * h01 + u:4 * h01 + u + 1],
                                    in1, ALU.mult, ALU.add),
                                inc="ve" if last else None)
                    psO_free_ve = stt_ve
                    psB_free_ve = stt_ve

                asm_ready = P.val("ve")
                for u in range(4):
                    sl = stats[u % 2]
                    g_tt = 4 * qc + u
                    hp = u % 2
                    P.wait("sync", "ve", x_freeB[hp])
                    lxb_cnt[hp] = P.op("sync", lambda e, g=g_tt, hp=hp: e.dma_start(
                        out=x_sb[hp][:], in_=d_x[128 * g:128 * g + 128, :]),
                        inc=f"lxb{hp}", inc_n=16)
                    c_ln1 = plan_ln(P, asm_sb[u][:], t1_sb[:], E, 1e-3, sl,
                                    ready_val=asm_ready)
                    asm_free_ve[u] = c_ln1
                    P.wait("vector", "ve", c_ln1)
                    c_g = P.op("vector", lambda e: e.scalar_tensor_tensor(
                        t2_sb[:], t1_sb[:], 0.0, gb_sb[:], ALU.bypass, ALU.mult),
                        inc="ve")
                    P.wait("vector", "ve", c_g)
                    c_b = P.op("vector", lambda e: e.scalar_tensor_tensor(
                        t2_sb[:], t2_sb[:], 0.0, bb_sb[:], ALU.bypass, ALU.add),
                        inc="ve")
                    c_ln2 = plan_ln(P, t2_sb[:], t1_sb[:], E, 1e-5, sl,
                                    ready_val=c_b)
                    P.wait("vector", f"lxb{hp}", lxb_cnt[hp])
                    P.wait("vector", "ve", c_ln2)
                    P.wait("vector", f"sx2{hp}", x2_free[hp])
                    c_x2 = P.op("vector", lambda e, hp=hp: e.tensor_tensor(
                        x2_sb[hp][:], x_sb[hp][:], t1_sb[:], ALU.add), inc="ve")
                    x_freeB[hp] = c_x2
                    P.wait("sync", "ve", c_x2)
                    x2_free[hp] = P.op("sync", lambda e, g=g_tt, hp=hp: e.dma_start(
                        out=d_x2[128 * g:128 * g + 128, :], in_=x2_sb[hp][:]),
                        inc=f"sx2{hp}", inc_n=16)
            b_x2_totals = {"sx20": x2_free[0], "sx21": x2_free[1]}
            if phases <= 2:
                for s_name, tot in b_x2_totals.items():
                    P.wait("sync", s_name, tot)
            P.flush(block)
        if phases <= 2:
            sAB.close()
            return nc

        sAB.close()
        with ExitStack() as sC:
            sbC = lambda n_, s_, d_: sC.enter_context(nc.sbuf_tensor(n_, s_, d_))
            xn2T = sbC("xn2T", [128, 8 * TFF], BF16)
            a_sb = sbC("a_sb", [128, 32 * TFF], BF16)
            h_sb = sbC("h_sb", [128, 32 * TFF], BF16)
            fT_sb = h_sb
            w12_sb = [sbC(f"w12_{i}", [128, 8 * 128], BF16) for i in range(4)]
            hsl_sb = [sbC(f"hsl{i}", [128, 32 * 128], BF16) for i in range(2)]
            x2t_sb = [sbC(f"x2t{i}", [128, E], F32) for i in range(2)]
            xn2_sb = [sbC(f"xn2_{i}", [128, E], BF16) for i in range(2)]
            silu_sb = [sbC(f"silu{i}", [128, 512], F32) for i in range(2)]
            ftm_sb = h_sb[:, 8 * TFF:8 * TFF + 2048].bitcast(F32)
            ps_M = [sC.enter_context(nc.psum_tensor(f"psM{i}", [128, 512], F32))
                    for i in range(4)]
            ps_tr2 = sC.enter_context(nc.psum_tensor("ps_tr2", [128, 2048], BF16))

            P.phase_barrier()

            lx2_cnt = [0, 0]
            x2t_free = [0, 0]
            w12_cnt = [0, 0, 0, 0]
            w12_free_te = [0, 0, 0, 0]
            hsl_cnt = [0, 0]
            hsl_free_te = [0, 0]
            silu_free_ve = [0, 0]
            psM_free = [0, 0, 0, 0]
            psM_free_eng = ["ve", "ve", "ve", "ve"]
            trp2 = [0]
            tr2_evac_ve = [0, 0]
            xn2_free_ve = [0, 0]
            out_free = [0, 0]

            def emit_xn2_unit(ft, u8):
                g_tt = 8 * ft + u8
                hp = u8 % 2
                tp2 = trp2[0]
                P.wait("sync", "ve", x2t_free[hp])
                lx2_cnt[hp] = P.op("sync", lambda e, g=g_tt, hp=hp: e.dma_start(
                    out=x2t_sb[hp][:], in_=d_x2[128 * g:128 * g + 128, :]),
                    inc=f"lx2{hp}", inc_n=16)
                P.wait("vector", f"lx2{hp}", lx2_cnt[hp])
                c_ln = plan_ln(P, x2t_sb[hp][:], xn2_sb[hp][:], E, 1e-5,
                               stats[hp], ready_val=xn2_free_ve[hp])
                x2t_free[hp] = c_ln
                P.wait("tensor", "ve", c_ln)
                P.wait("tensor", "ve", tr2_evac_ve[tp2])
                for f in range(8):
                    te_tr = P.op("tensor", lambda e, hp=hp, f=f, tp2=tp2:
                                 e.transpose(
                                     ps_tr2[:, 1024 * tp2 + 128 * f:
                                            1024 * tp2 + 128 * f + 128],
                                     xn2_sb[hp][:, 128 * f:128 * f + 128],
                                     idb_sb[:]), inc="te" if f == 7 else None)
                P.wait("vector", "te", te_tr)
                for f in range(8):
                    ev = P.op("vector", lambda e, f=f, u8=u8, tp2=tp2:
                              e.tensor_copy(
                                  xn2T[:, TFF * f + 128 * u8:
                                       TFF * f + 128 * u8 + 128],
                                  ps_tr2[:, 1024 * tp2 + 128 * f:
                                         1024 * tp2 + 128 * f + 128]),
                              inc="ve" if f == 7 else None)
                tr2_evac_ve[tp2] = ev
                xn2_free_ve[hp] = ev
                trp2[0] ^= 1

            for u8 in range(8):
                emit_xn2_unit(0, u8)

            for ft in range(4):
                xn2T_ve = P.val("ve")
                P.wait("tensor", "ve", xn2T_ve)

                for mc in range(32):
                    for wi, d_w in enumerate([d_w1b, d_w2b]):
                        slot = 2 * (mc % 2) + wi
                        P.wait("scalar", "te", w12_free_te[slot])
                        w12_cnt[slot] = P.op("scalar", lambda e, d_w=d_w, mc=mc,
                                             slot=slot: e.dma_start(
                                                 out=w12_sb[slot][:],
                                                 in_=d_w[mc, :, :]),
                                             inc=f"lw12_{slot}", inc_n=16)
                    for wi in range(2):
                        P.wait("tensor", f"lw12_{2 * (mc % 2) + wi}",
                               w12_cnt[2 * (mc % 2) + wi])
                    for nt in range(2):
                        for wi in range(2):
                            b = 2 * (mc % 2) + wi
                            slot = 2 * (mc % 2) + wi
                            P.wait("tensor", psM_free_eng[b], psM_free[b])
                            for ki in range(8):
                                te_w12 = P.op(
                                    "tensor", lambda e, slot=slot, ki=ki, nt=nt, b=b:
                                    e.matmul(
                                        ps_M[b][:],
                                        w12_sb[slot][:, 128 * ki:128 * ki + 128],
                                        xn2T[:, TFF * ki + 512 * nt:
                                             TFF * ki + 512 * nt + 512],
                                        start=(ki == 0), stop=(ki == 7)),
                                    inc="te" if ki == 7 else None)
                        b1, b2 = 2 * (mc % 2), 2 * (mc % 2) + 1
                        hp2 = nt
                        P.wait("scalar", "te", te_w12)
                        P.wait("scalar", "ve", silu_free_ve[hp2])
                        c_si = P.op("scalar", lambda e, b1=b1, mc=mc, hp2=hp2:
                                    e.activation(silu_sb[hp2][:], ps_M[b1][:],
                                                 AF.Silu,
                                                 scale=s1_sb[:, mc:mc + 1]),
                                    inc="se")
                        psM_free[b1] = c_si
                        psM_free_eng[b1] = "se"
                        P.wait("vector", "se", c_si)
                        c_a = P.op("vector", lambda e, b2=b2, mc=mc, nt=nt, hp2=hp2:
                                   e.scalar_tensor_tensor(
                                       a_sb[:, TFF * mc + 512 * nt:
                                            TFF * mc + 512 * nt + 512],
                                       ps_M[b2][:], s2_sb[:, mc:mc + 1],
                                       silu_sb[hp2][:], ALU.mult, ALU.mult),
                                   inc="ve")
                        silu_free_ve[hp2] = c_a
                        psM_free[b2] = c_a
                        psM_free_eng[b2] = "ve"
                        w12_free_te[b1] = te_w12
                        w12_free_te[b2] = te_w12

                def h_stage(d_ht, s_ap, src_sb, dst_sb, sname, interleave_xn2):
                    te_h = 0
                    for mc in range(32):
                        if interleave_xn2 and mc < 8:
                            emit_xn2_unit(ft + 1, mc)
                        slot = mc % 2
                        P.wait("scalar", "te", hsl_free_te[slot])
                        hsl_cnt[slot] = P.op("scalar", lambda e, d_ht=d_ht, mc=mc,
                                             slot=slot: e.dma_start(
                                                 out=hsl_sb[slot][:],
                                                 in_=d_ht[mc, :, :]),
                                             inc=f"{sname}{slot}", inc_n=16)
                        P.wait("tensor", f"{sname}{slot}", hsl_cnt[slot])
                        for nt in range(2):
                            b = 2 * slot + nt
                            P.wait("tensor", psM_free_eng[b], psM_free[b])
                            for ki in range(32):
                                te_h = P.op(
                                    "tensor", lambda e, slot=slot, ki=ki, nt=nt, b=b,
                                    src_sb=src_sb: e.matmul(
                                        ps_M[b][:],
                                        hsl_sb[slot][:, 128 * ki:128 * ki + 128],
                                        src_sb[:, TFF * ki + 512 * nt:
                                               TFF * ki + 512 * nt + 512],
                                        start=(ki == 0), stop=(ki == 31)),
                                    inc="te" if ki == 31 else None)
                            P.wait("scalar", "te", te_h)
                            c_g2 = P.op("scalar", lambda e, b=b, mc=mc, nt=nt,
                                        dst_sb=dst_sb, s_ap=s_ap: e.activation(
                                            dst_sb[:, TFF * mc + 512 * nt:
                                                   TFF * mc + 512 * nt + 512],
                                            ps_M[b][:], AF.Gelu,
                                            scale=s_ap[:, mc:mc + 1]), inc="se")
                            psM_free[b] = c_g2
                            psM_free_eng[b] = "se"
                        hsl_free_te[slot] = te_h
                    return P.val("se")

                a_ve = P.val("ve")
                P.wait("tensor", "ve", a_ve)
                se_h1 = h_stage(d_h1b, sh1_sb, a_sb, h_sb, "lh1_", ft < 3)
                P.wait("tensor", "se", se_h1)
                se_h2 = h_stage(d_h2b, sh2_sb, h_sb, a_sb, "lh2_", False)
                P.wait("tensor", "se", se_h2)

                for oc in range(8):
                    slot = oc % 2
                    P.wait("scalar", "te", hsl_free_te[slot])
                    hsl_cnt[slot] = P.op("scalar", lambda e, oc=oc, slot=slot:
                                         e.dma_start(out=hsl_sb[slot][:],
                                                     in_=d_w3b[oc, :, :]),
                                         inc=f"lw3_{slot}", inc_n=16)
                    P.wait("tensor", f"lw3_{slot}", hsl_cnt[slot])
                    for nt in range(2):
                        b = 2 * slot + nt
                        P.wait("tensor", psM_free_eng[b], psM_free[b])
                        for ki in range(32):
                            te_w3 = P.op(
                                "tensor", lambda e, slot=slot, ki=ki, nt=nt, b=b:
                                e.matmul(ps_M[b][:],
                                         hsl_sb[slot][:, 128 * ki:128 * ki + 128],
                                         a_sb[:, TFF * ki + 512 * nt:
                                              TFF * ki + 512 * nt + 512],
                                         start=(ki == 0), stop=(ki == 31)),
                                inc="te" if ki == 31 else None)
                        P.wait("vector", "te", te_w3)
                        c_f = P.op("vector", lambda e, b=b, oc=oc, nt=nt:
                                   e.tensor_scalar(
                                       fT_sb[:, TFF * oc + 512 * nt:
                                             TFF * oc + 512 * nt + 512],
                                       ps_M[b][:], s3_sb[:, oc:oc + 1], None,
                                       ALU.mult), inc="ve")
                        psM_free[b] = c_f
                        psM_free_eng[b] = "ve"
                    hsl_free_te[slot] = te_w3
                fT_ve = P.val("ve")
                P.wait("tensor", "ve", fT_ve)

                for u8 in range(8):
                    g_tt = 8 * ft + u8
                    hp = u8 % 2
                    tp2 = trp2[0]
                    P.wait("tensor", "ve", tr2_evac_ve[tp2])
                    for f in range(8):
                        te_tr = P.op("tensor", lambda e, f=f, u8=u8, tp2=tp2:
                                     e.transpose(
                                         ps_tr2[:, 1024 * tp2 + 128 * f:
                                                1024 * tp2 + 128 * f + 128],
                                         fT_sb[:, TFF * f + 128 * u8:
                                               TFF * f + 128 * u8 + 128],
                                         idb_sb[:]), inc="te" if f == 7 else None)
                    P.wait("vector", "te", te_tr)
                    c_ftm = P.op("vector", lambda e, tp2=tp2: e.tensor_copy(
                        ftm_sb, ps_tr2[:, 1024 * tp2:1024 * tp2 + 1024]),
                        inc="ve")
                    tr2_evac_ve[tp2] = c_ftm
                    trp2[0] ^= 1
                    P.wait("sync", "ve", x2t_free[hp])
                    lx2_cnt[hp] = P.op("sync", lambda e, g=g_tt, hp=hp: e.dma_start(
                        out=x2t_sb[hp][:], in_=d_x2[128 * g:128 * g + 128, :]),
                        inc=f"lx2{hp}", inc_n=16)
                    c_ln = plan_ln(P, ftm_sb, t1_sb[:], E, 1e-5, stats[2 + hp],
                                   ready_val=c_ftm)
                    P.wait("vector", f"lx2{hp}", lx2_cnt[hp])
                    P.wait("vector", "ve", c_ln)
                    c_o = P.op("vector", lambda e, hp=hp: e.tensor_tensor(
                        x2t_sb[hp][:], x2t_sb[hp][:], t1_sb[:], ALU.add), inc="ve")
                    x2t_free[hp] = c_o
                    P.wait("sync", "ve", c_o)
                    out_free[hp] = P.op("sync", lambda e, g=g_tt, hp=hp: e.dma_start(
                        out=d_out[128 * g:128 * g + 128, :], in_=x2t_sb[hp][:]),
                        inc=f"sout{hp}", inc_n=16)
                    P.wait("sync", f"sout{hp}", out_free[hp])

            P.wait("sync", "sout0", out_free[0])
            P.wait("sync", "sout1", out_free[1])
            P.flush(block)

    return nc


_PERM = None


def _perm():
    global _PERM
    if _PERM is None:
        p = np.arange(NT)
        _PERM = 16 * (p % 256) + p // 256
    return _PERM


def _prep_core_inputs(xb, d):
    perm = _perm()
    rel = d["L0_attn_rel_table"]
    jj = perm // 16
    cc = perm % 16
    gr = np.arange(2048)
    idx = np.minimum(2047 - gr[None, :] + jj[:, None], 2046)
    fvt = np.ascontiguousarray(rel[idx, cc[:, None]])
    cmat = np.ascontiguousarray(
        np.broadcast_to(np.repeat(rel[2046], 2)[None, :], (128, 32)))
    return {
        "x": np.ascontiguousarray(xb[perm]),
        "wq": d["L0_attn_Wq"], "wk": d["L0_attn_Wk"], "wv": d["L0_attn_Wv"],
        "fvt": fvt, "cmat": cmat,
        "gamma_b": np.ascontiguousarray(
            np.broadcast_to(d["L0_attn_gamma"][None, :], (128, E))),
        "beta_b": np.ascontiguousarray(
            np.broadcast_to(d["L0_attn_beta"][None, :], (128, E))),
        "w1t": np.ascontiguousarray(d["L0_ffn_w1_w"].T),
        "w2t": np.ascontiguousarray(d["L0_ffn_w2_w"].T),
        "h1t": np.ascontiguousarray(d["L0_ffn_h1_w"].T),
        "h2t": np.ascontiguousarray(d["L0_ffn_h2_w"].T),
        "w3t": np.ascontiguousarray(d["L0_ffn_w3_w"].T),
        "s1m": np.ascontiguousarray(d["L0_ffn_w1_s"].reshape(32, 128).T),
        "s2m": np.ascontiguousarray(d["L0_ffn_w2_s"].reshape(32, 128).T),
        "sh1m": np.ascontiguousarray(d["L0_ffn_h1_s"].reshape(32, 128).T),
        "sh2m": np.ascontiguousarray(d["L0_ffn_h2_s"].reshape(32, 128).T),
        "s3m": np.ascontiguousarray(d["L0_ffn_w3_s"].reshape(8, 128).T),
        "id_bf": np.eye(128, dtype=np.float32).astype(ml_dtypes.bfloat16),
        "id_f32": np.eye(128, dtype=np.float32),
    }


def kernel(**inputs):
    x = np.asarray(inputs["L0_attn_x"], dtype=np.float32)
    d = {k: np.asarray(v, dtype=np.float32) for k, v in inputs.items()}
    nc = build_program()
    in_maps = [_prep_core_inputs(x[b], d) for b in range(8)]
    res = run_bass_kernel_spmd(nc, in_maps, core_ids=list(range(8)))
    perm = _perm()
    out = np.empty((8, NT, E), dtype=np.float32)
    for b in range(8):
        out[b][perm] = res.results[b]["out"]
    return out


# version 07
# speedup vs baseline: 1.1160x; 1.0415x over previous
import numpy as np
import ml_dtypes
import concourse.bass as bass
import concourse.mybir as mybir
from concourse.bass_utils import run_bass_kernel_spmd
from contextlib import ExitStack

F32 = mybir.dt.float32
BF16 = mybir.dt.bfloat16
AF = mybir.ActivationFunctionType
ALU = mybir.AluOpType
AX = mybir.AxisListType

NT = 4096
E = 1024
H = 16
HID = 4096
NQC = 8
TFF = 1024

ENGINES = ("sync", "scalar", "vector", "tensor", "gpsimd")


class Plan:

    def __init__(self, nc, stack):
        self.nc = nc
        self.stack = stack
        self.thunks = {e: [] for e in ENGINES}
        self.sems = {}
        self.counts = {}

    def sem(self, name):
        if name not in self.sems:
            self.sems[name] = self.stack.enter_context(self.nc.semaphore(name))
            self.counts[name] = 0
        return self.sems[name]

    def op(self, eng, fn, inc=None, inc_n=1):
        if inc is not None:
            self.sem(inc)
            self.counts[inc] += inc_n
            target = self.counts[inc]
            sem = self.sems[inc]

            def thunk(e, fn=fn, sem=sem, inc_n=inc_n):
                fn(e).then_inc(sem, inc_n)
        else:
            target = None

            def thunk(e, fn=fn):
                fn(e)
        self.thunks[eng].append(thunk)
        return target

    def wait(self, eng, name, val):
        if val is None or val <= 0:
            return
        sem = self.sem(name)
        self.thunks[eng].append(lambda e, s=sem, v=val: e.wait_ge(s, v))

    def val(self, name):
        self.sem(name)
        return self.counts[name]

    def phase_barrier(self, exclude=()):
        snap = [(n, v) for n, v in self.counts.items()
                if v > 0 and n not in exclude]
        for eng in ENGINES:
            for n, v in snap:
                self.wait(eng, n, v)

    def flush(self, block):
        for eng in ENGINES:
            if not self.thunks[eng]:
                continue
            fns = self.thunks[eng]
            self.thunks[eng] = []
            getattr(block, eng)(lambda e, fns=fns: [f(e) for f in fns])


def plan_ln(P, x_ap, out_ap, n, eps, sl, ready_val=0):
    P.wait("vector", "ve", ready_val)
    c1 = P.op("vector", lambda e: e.reduce_sum(sl["rs"], x_ap, axis=AX.X), inc="ve")
    P.wait("vector", "ve", c1)
    c2 = P.op("vector", lambda e: e.tensor_scalar_mul(sl["nmu"], sl["rs"], -1.0 / n),
              inc="ve")
    P.wait("scalar", "ve", c2)
    c3 = P.op("scalar", lambda e: e.activation(sl["sq_scr"], x_ap, AF.Square,
                                               bias=sl["nmu"], scale=1.0,
                                               accum_out=sl["var"]), inc="se")
    P.wait("vector", "se", c3)
    c4 = P.op("vector", lambda e: e.tensor_scalar(sl["var"], sl["var"], 1.0 / n, eps,
                                                  ALU.mult, ALU.add), inc="ve")
    P.wait("scalar", "ve", c4)
    c5 = P.op("scalar", lambda e: e.activation(sl["sd"], sl["var"], AF.Sqrt), inc="se")
    P.wait("vector", "se", c5)
    c6 = P.op("vector", lambda e: e.reciprocal(sl["rstd"], sl["sd"]), inc="ve")
    P.wait("vector", "ve", c6)
    c7 = P.op("vector", lambda e: e.tensor_tensor(sl["nmur"], sl["nmu"], sl["rstd"],
                                                  ALU.mult), inc="ve")
    P.wait("vector", "ve", c7)
    c8 = P.op("vector", lambda e: e.tensor_scalar(out_ap, x_ap, sl["rstd"], sl["nmur"],
                                                  ALU.mult, ALU.add), inc="ve")
    return c8


def build_program(debug=False, phases=3, nqc=NQC, sub="all"):
    nc = bass.Bass("TRN2", target_bir_lowering=False)

    d_x = nc.dram_tensor("x", [NT, E], F32, kind="ExternalInput").ap()
    d_wq = nc.dram_tensor("wq", [E, E], F32, kind="ExternalInput").ap()
    d_wk = nc.dram_tensor("wk", [E, E], F32, kind="ExternalInput").ap()
    d_wv = nc.dram_tensor("wv", [E, E], F32, kind="ExternalInput").ap()
    d_fvt = nc.dram_tensor("fvt", [NT, 2048], F32, kind="ExternalInput").ap()
    d_cmat = nc.dram_tensor("cmat", [128, 32], F32, kind="ExternalInput").ap()
    d_gb = nc.dram_tensor("gamma_b", [128, E], F32, kind="ExternalInput").ap()
    d_bb = nc.dram_tensor("beta_b", [128, E], F32, kind="ExternalInput").ap()
    d_w1t = nc.dram_tensor("w1t", [E, HID], F32, kind="ExternalInput").ap()
    d_w2t = nc.dram_tensor("w2t", [E, HID], F32, kind="ExternalInput").ap()
    d_h1t = nc.dram_tensor("h1t", [HID, HID], F32, kind="ExternalInput").ap()
    d_h2t = nc.dram_tensor("h2t", [HID, HID], F32, kind="ExternalInput").ap()
    d_w3t = nc.dram_tensor("w3t", [HID, E], F32, kind="ExternalInput").ap()
    d_s1 = nc.dram_tensor("s1m", [128, 32], F32, kind="ExternalInput").ap()
    d_s2 = nc.dram_tensor("s2m", [128, 32], F32, kind="ExternalInput").ap()
    d_sh1 = nc.dram_tensor("sh1m", [128, 32], F32, kind="ExternalInput").ap()
    d_sh2 = nc.dram_tensor("sh2m", [128, 32], F32, kind="ExternalInput").ap()
    d_s3 = nc.dram_tensor("s3m", [128, 8], F32, kind="ExternalInput").ap()
    d_idb = nc.dram_tensor("id_bf", [128, 128], BF16, kind="ExternalInput").ap()
    d_idf = nc.dram_tensor("id_f32", [128, 128], F32, kind="ExternalInput").ap()
    d_out = nc.dram_tensor("out", [NT, E], F32, kind="ExternalOutput").ap()

    sk = "ExternalOutput" if debug else "Internal"
    d_qs = nc.dram_tensor("qs", [NQC, E, 512], BF16, kind=sk).ap()
    d_ks = nc.dram_tensor("ks", [NQC, E, 512], BF16, kind=sk).ap()
    d_vs = nc.dram_tensor("vs", [NT, H * 65], BF16, kind=sk).ap()
    d_x2 = nc.dram_tensor("x2s", [NT, E], F32, kind=sk).ap()
    d_fvtb = nc.dram_tensor("fvtb", [NT, 2048], BF16, kind=sk).ap()
    d_w1b = nc.dram_tensor("w1b", [32, 128, 1024], BF16, kind=sk).ap()
    d_w2b = nc.dram_tensor("w2b", [32, 128, 1024], BF16, kind="Internal").ap()
    d_h1b = nc.dram_tensor("h1b", [32, 128, 4096], BF16, kind="Internal").ap()
    d_h2b = nc.dram_tensor("h2b", [32, 128, 4096], BF16, kind="Internal").ap()
    d_w3b = nc.dram_tensor("w3b", [8, 128, 4096], BF16, kind="Internal").ap()

    stack = ExitStack()
    with stack:
        P = Plan(nc, stack)
        sb = lambda name, shape, dt: stack.enter_context(nc.sbuf_tensor(name, shape, dt))

        gb_sb = sb("gb", [128, E], F32)
        bb_sb = sb("bb", [128, E], F32)
        c_f32 = sb("c_f32", [128, 32], F32)
        c_bf = sb("c_bf", [128, 32], BF16)
        s1_sb = sb("s1", [128, 32], F32)
        s2_sb = sb("s2", [128, 32], F32)
        sh1_sb = sb("sh1", [128, 32], F32)
        sh2_sb = sb("sh2", [128, 32], F32)
        s3_sb = sb("s3", [128, 8], F32)
        idb_sb = sb("idb", [128, 128], BF16)
        idf_sb = sb("idf", [128, 128], F32)
        ones_f = sb("ones_f", [1, 128], F32)
        w_bc = sb("w_bc", [128, H * 65], F32)
        sq_scr = sb("sq_scr", [128, E], F32)
        t1_sb = sb("t1", [128, E], F32)
        st_t = sb("stats", [128, 32], F32)
        stats = []
        for s in range(4):
            stats.append({"rs": st_t[:, 8 * s:8 * s + 1],
                          "nmu": st_t[:, 8 * s + 1:8 * s + 2],
                          "var": st_t[:, 8 * s + 2:8 * s + 3],
                          "sd": st_t[:, 8 * s + 3:8 * s + 4],
                          "rstd": st_t[:, 8 * s + 4:8 * s + 5],
                          "nmur": st_t[:, 8 * s + 5:8 * s + 6],
                          "sq_scr": sq_scr[:]})

        sAB = ExitStack()
        pcst = [sAB.enter_context(nc.sbuf_tensor(f"pcst{i}", [128, 4096], BF16))
                for i in range(2)]

        block = stack.enter_context(nc.Block())

        with ExitStack() as sA:
            sbA = lambda n_, s_, d_: sA.enter_context(nc.sbuf_tensor(n_, s_, d_))
            wq_sb = sbA("wq_sb", [128, 8 * E], BF16)
            wk_sb = sbA("wk_sb", [128, 8 * E], BF16)
            wv_sb = sbA("wv_sb", [128, 8 * E], BF16)
            xn1_sb = [sbA(f"xn1_{i}", [128, E], BF16) for i in range(2)]
            xnT_sb = sbA("xnT", [128, 8 * 512], BF16)
            qe_sb = [sbA(f"qe{i}", [128, 512], BF16) for i in range(2)]
            ke_sb = [sbA(f"ke{i}", [128, 512], BF16) for i in range(2)]
            va_sb = [sbA(f"va{i}", [128, H * 65], BF16) for i in range(2)]
            w_row = sbA("w_row", [1, H * 65], F32)
            w_hi = sbA("w_hi", [1, H * 65], BF16)
            w_rem = sbA("w_rem", [1, H * 65], F32)
            w_lo = sbA("w_lo", [1, H * 65], BF16)
            ones_b = sbA("ones_b", [1, 128], BF16)
            x_sb = [sbA(f"xA{i}", [128, E], F32) for i in range(2)]
            ps_mm = sA.enter_context(nc.psum_tensor("ps_mm", [128, 1024], F32))
            ps_tr = sA.enter_context(nc.psum_tensor("ps_tr", [128, 2048], BF16))
            ps_w = sA.enter_context(nc.psum_tensor("ps_w", [1, 1536], F32))

            for dd, ss in [(d_gb, gb_sb), (d_bb, bb_sb), (d_cmat, c_f32),
                           (d_s1, s1_sb), (d_s2, s2_sb), (d_sh1, sh1_sb),
                           (d_sh2, sh2_sb), (d_s3, s3_sb), (d_idb, idb_sb),
                           (d_idf, idf_sb)]:
                l_set = P.op("sync", lambda e, dd=dd, ss=ss:
                             e.dma_start(out=ss[:], in_=dd[:]), inc="l_set", inc_n=16)
            for dd, ss in [(d_wq, wq_sb), (d_wk, wk_sb), (d_wv, wv_sb)]:
                src = dd.rearrange("(ki p) o -> p ki o", p=128)
                dst = ss[:].rearrange("p (ki o) -> p ki o", ki=8)
                l_wqkv = P.op("gpsimd", lambda e, s1=src, d1=dst:
                              e.dma_start(out=d1, in_=s1), inc="l_wqkv", inc_n=16)
            pc_ld = [0, 0]
            pc_st = [0, 0]
            pc_idx = 0
            for c in range(32):
                sl = pc_idx % 2
                pc_idx += 1
                P.wait("gpsimd", f"pcs{sl}", pc_st[sl])
                pc_ld[sl] = P.op("gpsimd", lambda e, c=c, sl=sl: e.dma_start(
                    out=pcst[sl][:, 0:2048],
                    in_=d_fvt[128 * c:128 * c + 128, :]), inc=f"pcl{sl}", inc_n=16)
                P.wait("gpsimd", f"pcl{sl}", pc_ld[sl])
                pc_st[sl] = P.op("gpsimd", lambda e, c=c, sl=sl: e.dma_start(
                    out=d_fvtb[128 * c:128 * c + 128, :],
                    in_=pcst[sl][:, 0:2048]), inc=f"pcs{sl}", inc_n=16)
            fvt_pc_done = (pc_st[0], pc_st[1])
            for dsrc, ddst, nsl, nki in [(d_w1t, d_w1b, 32, 8),
                                         (d_w2t, d_w2b, 32, 8),
                                         (d_h1t, d_h1b, 32, 32),
                                         (d_h2t, d_h2b, 32, 32),
                                         (d_w3t, d_w3b, 8, 32)]:
                wid = nki * 128
                for c in range(nsl):
                    sl = pc_idx % 2
                    pc_idx += 1
                    srcap = dsrc[:, 128 * c:128 * c + 128].rearrange(
                        "(ki p) o -> p ki o", p=128)
                    dstap = pcst[sl][:, 0:wid].rearrange(
                        "p (ki o) -> p ki o", ki=nki)
                    P.wait("gpsimd", f"pcs{sl}", pc_st[sl])
                    pc_ld[sl] = P.op("gpsimd", lambda e, srcap=srcap, dstap=dstap,
                                     sl=sl: e.dma_start(out=dstap, in_=srcap),
                                     inc=f"pcl{sl}", inc_n=16)
                    P.wait("gpsimd", f"pcl{sl}", pc_ld[sl])
                    pc_st[sl] = P.op("gpsimd", lambda e, ddst=ddst, c=c, sl=sl,
                                     wid=wid: e.dma_start(
                                         out=ddst[c, :, :],
                                         in_=pcst[sl][:, 0:wid]),
                                     inc=f"pcs{sl}", inc_n=16)
            P.wait("vector", "l_set", l_set)
            P.op("vector", lambda e: e.tensor_copy(c_bf[:], c_f32[:]), inc="ve")
            P.op("vector", lambda e: e.memset(ones_f[:], 1.0), inc="ve")
            setup_ve = P.op("vector", lambda e: e.memset(ones_b[:], 1.0), inc="ve")

            lx_cnt = [0, 0]
            xn_free_ve = [0, 0]
            x_freeA = [0, 0]
            stq_free = [0, 0]
            stk_free = [0, 0]
            stv_free = [0, 0]
            trp = 0
            tr_evac_ve = [0, 0]
            mm_free_ve = [0, 0]
            te_w = 0
            last_g = 4 * nqc - 1
            for qc in range(nqc):
                for t in range(4):
                    g_tt = 4 * qc + t
                    s = g_tt % 2
                    P.wait("sync", "ve", x_freeA[s])
                    lx_cnt[s] = P.op("sync", lambda e, g=g_tt, s=s: e.dma_start(
                        out=x_sb[s][:], in_=d_x[128 * g:128 * g + 128, :]),
                        inc=f"lx{s}", inc_n=16)
                    P.wait("vector", f"lx{s}", lx_cnt[s])
                    ln_ve = plan_ln(P, x_sb[s][:], xn1_sb[s][:], E, 1e-5, stats[s],
                                    ready_val=xn_free_ve[s])
                    x_freeA[s] = ln_ve
                    P.wait("tensor", "ve", ln_ve)
                    P.wait("tensor", "ve", tr_evac_ve[trp])
                    for f in range(8):
                        te_tr = P.op("tensor", lambda e, s=s, f=f, trp=trp: e.transpose(
                            ps_tr[:, 1024 * trp + 128 * f:1024 * trp + 128 * f + 128],
                            xn1_sb[s][:, 128 * f:128 * f + 128], idb_sb[:]),
                            inc="te" if f == 7 else None)
                    P.wait("vector", "te", te_tr)
                    for f in range(8):
                        ev = P.op("vector", lambda e, f=f, t=t, trp=trp: e.tensor_copy(
                            xnT_sb[:, 512 * f + 128 * t:512 * f + 128 * t + 128],
                            ps_tr[:, 1024 * trp + 128 * f:1024 * trp + 128 * f + 128]),
                            inc="ve" if f == 7 else None)
                    tr_evac_ve[trp] = ev
                    xn_free_ve[s] = ev
                    trp ^= 1
                if sub not in ("qk", "all"):
                    continue
                P.wait("tensor", "l_wqkv", l_wqkv)
                P.wait("tensor", "ve", setup_ve)
                xnT_ready = P.val("ve")
                P.wait("tensor", "ve", xnT_ready)
                for w_sb, ev_sb, ev_free, d_dst, sem_st in [
                        (wq_sb, qe_sb, stq_free, d_qs, "stq"),
                        (wk_sb, ke_sb, stk_free, d_ks, "stk")]:
                    for f in range(8):
                        hp = f % 2
                        P.wait("tensor", "ve", mm_free_ve[hp])
                        for ki in range(8):
                            te_mm = P.op(
                                "tensor", lambda e, w_sb=w_sb, f=f, ki=ki, hp=hp:
                                e.matmul(ps_mm[:, 512 * hp:512 * hp + 512],
                                         w_sb[:, E * ki + 128 * f:E * ki + 128 * f + 128],
                                         xnT_sb[:, 512 * ki:512 * ki + 512],
                                         start=(ki == 0), stop=(ki == 7)),
                                inc="te" if ki == 7 else None)
                        P.wait("vector", "te", te_mm)
                        P.wait("vector", f"{sem_st}{hp}", ev_free[hp])
                        cv = P.op("vector", lambda e, ev_sb=ev_sb, hp=hp: e.tensor_copy(
                            ev_sb[hp][:], ps_mm[:, 512 * hp:512 * hp + 512]), inc="ve")
                        mm_free_ve[hp] = cv
                        P.wait("sync", "ve", cv)
                        ev_free[hp] = P.op(
                            "sync", lambda e, ev_sb=ev_sb, hp=hp, qc=qc, f=f,
                            d_dst=d_dst: e.dma_start(
                                out=d_dst[qc, 128 * f:128 * f + 128, :],
                                in_=ev_sb[hp][:]),
                            inc=f"{sem_st}{hp}", inc_n=16)
                if sub not in ("v", "all", "vmm", "vst"):
                    continue
                for t in range(4):
                    g_tt = 4 * qc + t
                    hp = t % 2
                    P.wait("vector", f"stv{hp}", stv_free[hp])
                    mv = P.op("vector", lambda e, hp=hp: e.memset(va_sb[hp][:], 1.0),
                              inc="ve")
                    for nf in range(2):
                        P.wait("tensor", "ve", mm_free_ve[nf])
                        for ki in range(8):
                            te_mm = P.op("tensor", lambda e, t=t, ki=ki, nf=nf:
                                         e.matmul(
                                             ps_mm[:, 512 * nf:512 * nf + 512],
                                             xnT_sb[:, 512 * ki + 128 * t:
                                                    512 * ki + 128 * t + 128],
                                             wv_sb[:, E * ki + 512 * nf:
                                                   E * ki + 512 * nf + 512],
                                             start=(ki == 0), stop=(ki == 7)),
                                         inc="te" if ki == 7 else None)
                        P.wait("vector", "te", te_mm)
                        P.wait("vector", "ve", mv)
                        dst = va_sb[hp][:].rearrange("p (h c) -> p h c", c=65)[
                            :, 8 * nf:8 * nf + 8, 0:64]
                        src = ps_mm[:, 512 * nf:512 * nf + 512].rearrange(
                            "p (h c) -> p h c", c=64)
                        cv = P.op("vector", lambda e, dst=dst, src=src:
                                  e.tensor_copy(dst, src), inc="ve")
                        mm_free_ve[nf] = cv
                    va_ve = P.val("ve")
                    if sub != "vmm":
                        P.wait("sync", "ve", va_ve)
                        stv_free[hp] = P.op("sync", lambda e, hp=hp, g=g_tt: e.dma_start(
                            out=d_vs[128 * g:128 * g + 128, :], in_=va_sb[hp][:]),
                            inc=f"stv{hp}", inc_n=16)
                    if sub in ("vmm", "vst"):
                        continue
                    P.wait("tensor", "ve", va_ve)
                    for ch, (c0, c1) in enumerate([(0, 512), (512, 1024),
                                                   (1024, 1040)]):
                        te_w = P.op("tensor", lambda e, g=g_tt, hp=hp, c0=c0, c1=c1:
                                    e.matmul(ps_w[0:1, c0:c1], c_bf[:, g:g + 1],
                                             va_sb[hp][:, c0:c1],
                                             start=(g == 0), stop=(g == last_g),
                                             skip_group_check=True),
                                    inc="te" if (ch == 2 and g_tt == last_g) else None)
            w_ready_ve = 0
            if sub in ("v", "all"):
                P.wait("vector", "te", te_w)
                cw = P.op("vector", lambda e: e.tensor_copy(
                    w_row[:], ps_w[0:1, 0:H * 65]), inc="ve")
                P.wait("vector", "ve", cw)
                c_hi = P.op("vector", lambda e: e.tensor_copy(w_hi[:], w_row[:]),
                            inc="ve")
                P.wait("vector", "ve", c_hi)
                c_rm = P.op("vector", lambda e: e.tensor_tensor(
                    w_rem[:], w_row[:], w_hi[:], ALU.subtract), inc="ve")
                P.wait("vector", "ve", c_rm)
                c_lo = P.op("vector", lambda e: e.tensor_copy(w_lo[:], w_rem[:]),
                            inc="ve")
                P.wait("tensor", "ve", c_lo)
                P.wait("tensor", "ve", mm_free_ve[0])
                P.wait("tensor", "ve", mm_free_ve[1])
                for c0, c1 in [(0, 512), (512, 1024), (1024, 1040)]:
                    P.op("tensor", lambda e, c0=c0, c1=c1: e.matmul(
                        ps_mm[:, 0:c1 - c0], ones_b[:], w_hi[0:1, c0:c1],
                        start=True, stop=False))
                    tw = P.op("tensor", lambda e, c0=c0, c1=c1: e.matmul(
                        ps_mm[:, 0:c1 - c0], ones_b[:], w_lo[0:1, c0:c1],
                        start=False, stop=True), inc="te")
                    P.wait("vector", "te", tw)
                    cbc = P.op("vector", lambda e, c0=c0, c1=c1: e.tensor_copy(
                        w_bc[:, c0:c1], ps_mm[:, 0:c1 - c0]), inc="ve")
                    P.wait("tensor", "ve", cbc)
                    w_ready_ve = cbc
            a_store_totals = {"stq0": stq_free[0], "stq1": stq_free[1],
                              "stk0": stk_free[0], "stk1": stk_free[1],
                              "stv0": stv_free[0], "stv1": stv_free[1]}
            if phases <= 1:
                for s_name, tot in a_store_totals.items():
                    P.wait("sync", s_name, tot)
            P.flush(block)
        if phases <= 1:
            sAB.close()
            return nc

        with ExitStack() as sB:
            sbB = lambda n_, s_, d_: sB.enter_context(nc.sbuf_tensor(n_, s_, d_))
            kp_sb = [sbB(f"kp{i}", [128, NT], BF16) for i in range(2)]
            vp_sb = [sbB(f"vp{i}", [128, 32 * 130], BF16) for i in range(2)]
            qp_sb = [sbB(f"qp{i}", [128, 512], BF16) for i in range(2)]
            fvt_sb = [sbB(f"fvt{i}", [128, 32 * 256], BF16) for i in range(4)]
            p_sb = [sbB(f"p{i}", [128, 1024], BF16) for i in range(2)]
            of_sb = sbB("of", [128, 1024], F32)
            bvf_sb = sbB("bvf", [128, 1024], F32)
            bvt_sb = sbB("bvt", [128, 1024], F32)
            asm_sb = [sbB(f"asm{i}", [128, E], F32) for i in range(4)]
            x2_sb = [sbB(f"x2_{i}", [128, E], F32) for i in range(2)]
            t2_sb = sbB("t2", [128, E], F32)
            x_sb = [sbB(f"xB{i}", [128, E], F32) for i in range(2)]
            rcp_sb = sbB("rcp", [128, 8], F32)
            ps_S = [sB.enter_context(nc.psum_tensor(f"psS{i}", [128, 1024], F32))
                    for i in range(2)]
            ps_O = sB.enter_context(nc.psum_tensor("psO", [128, 1024], F32))
            ps_B = sB.enter_context(nc.psum_tensor("psB", [128, 1024], F32))

            P.phase_barrier(exclude=("pcl0", "pcl1", "pcs0", "pcs1"))
            P.wait("sync", "pcs0", fvt_pc_done[0])
            P.wait("sync", "pcs1", fvt_pc_done[1])

            kv_cnt = [0, 0]
            fvt_cnt = [0, 0, 0, 0]
            kv_free_te = [0, 0]
            fvt_free_te = [0, 0, 0, 0]
            psS_free_se = [0, 0]
            psO_free_ve = 0
            psB_free_ve = 0
            asm_free_ve = [0, 0, 0, 0]
            x2_free = [0, 0]
            x_freeB = [0, 0]
            lxb_cnt = [0, 0]

            for qc in range(NQC):
                for pr in range(8):
                    s = pr % 2
                    combos = []
                    for h01 in range(2):
                        for vh in range(2):
                            a = 2 * pr + h01
                            v = 2 * qc + vh
                            if a - v >= 8:
                                combos.append((h01, vh, 256 * (a - v) - 2048))
                    P.wait("sync", "te", kv_free_te[s])
                    src = d_ks[:, 128 * pr:128 * pr + 128, :].rearrange(
                        "b p c -> p b c")
                    dst = kp_sb[s][:].rearrange("p (b c) -> p b c", b=8)
                    P.op("sync", lambda e, s1=src, d1=dst: e.dma_start(out=d1, in_=s1),
                         inc=f"lkv{s}", inc_n=16)
                    src = d_vs[:, 130 * pr:130 * pr + 130].rearrange(
                        "(kc p) c -> p kc c", p=128)
                    dst = vp_sb[s][:].rearrange("p (kc c) -> p kc c", kc=32)
                    P.op("sync", lambda e, s1=src, d1=dst: e.dma_start(out=d1, in_=s1),
                         inc=f"lkv{s}", inc_n=16)
                    kv_cnt[s] = P.op("sync", lambda e, qc=qc, pr=pr, s=s: e.dma_start(
                        out=qp_sb[s][:], in_=d_qs[qc, 128 * pr:128 * pr + 128, :]),
                        inc=f"lkv{s}", inc_n=16)
                    for (h01, vh, gr0) in combos:
                        slot = 2 * h01 + vh
                        P.wait("sync", "te", fvt_free_te[slot])
                        src = d_fvtb[:, gr0:gr0 + 256].rearrange(
                            "(kc p) c -> p kc c", p=128)
                        dst = fvt_sb[slot][:].rearrange("p (kc c) -> p kc c", kc=32)
                        fvt_cnt[slot] = P.op("sync", lambda e, s1=src, d1=dst:
                                             e.dma_start(out=d1, in_=s1),
                                             inc=f"lfv{slot}", inc_n=16)

                    P.wait("tensor", f"lkv{s}", kv_cnt[s])
                    s_te = {}
                    exp_se = {}
                    pv_te = {}

                    def emit_S(kc, s=s):
                        b = kc % 2
                        P.wait("tensor", "se", psS_free_se[b])
                        P.op("tensor", lambda e, kc=kc, b=b, s=s: e.matmul(
                            ps_S[b][:, 0:512],
                            kp_sb[s][0:64, 128 * kc:128 * kc + 128],
                            qp_sb[s][0:64, :], start=True, stop=True))
                        s_te[kc] = P.op("tensor", lambda e, kc=kc, b=b, s=s: e.matmul(
                            ps_S[b][:, 512:1024],
                            kp_sb[s][64:128, 128 * kc:128 * kc + 128],
                            qp_sb[s][64:128, :], start=True, stop=True), inc="te")

                    def emit_exp(kc):
                        b = kc % 2
                        gate = pv_te[kc - 2] if kc >= 2 else s_te[kc]
                        P.wait("scalar", "te", gate)
                        c = P.op("scalar", lambda e, b=b: e.activation(
                            p_sb[b][:], ps_S[b][:], AF.Exp, scale=1.0 / 32.0),
                            inc="se")
                        exp_se[kc] = c
                        psS_free_se[b] = c

                    def emit_PV(kc, s=s, combos=combos):
                        b = kc % 2
                        P.wait("tensor", "se", exp_se[kc])
                        if kc == 0:
                            P.wait("tensor", "ve", psO_free_ve)
                            if combos:
                                P.wait("tensor", "ve", psB_free_ve)
                                for (h01, vh, gr0) in combos:
                                    slot = 2 * h01 + vh
                                    P.wait("tensor", f"lfv{slot}", fvt_cnt[slot])
                        n_ops = 2 + len(combos)
                        i = 0
                        t = None
                        for h01 in range(2):
                            i += 1
                            t = P.op("tensor", lambda e, kc=kc, b=b, s=s, h01=h01:
                                     e.matmul(
                                         ps_O[0:65, 512 * h01:512 * h01 + 512],
                                         vp_sb[s][:, 130 * kc + 65 * h01:
                                                   130 * kc + 65 * h01 + 65],
                                         p_sb[b][:, 512 * h01:512 * h01 + 512],
                                         start=(kc == 0), stop=(kc == 31),
                                         skip_group_check=True),
                                     inc="te" if i == n_ops else None)
                        seen_banks = set()
                        for (h01, vh, gr0) in combos:
                            i += 1
                            slot = 2 * h01 + vh
                            bank = slot // 2
                            st_flag = (kc == 0) and bank not in seen_banks
                            if kc == 0:
                                seen_banks.add(bank)
                            t = P.op("tensor", lambda e, kc=kc, b=b, s=s, h01=h01,
                                     slot=slot, st_flag=st_flag: e.matmul(
                                         ps_B[0:64, 256 * slot:256 * slot + 256],
                                         vp_sb[s][:, 130 * kc + 65 * h01:
                                                   130 * kc + 65 * h01 + 64],
                                         fvt_sb[slot][:, 256 * kc:256 * kc + 256],
                                         start=st_flag, stop=(kc == 31),
                                         skip_group_check=True),
                                     inc="te" if i == n_ops else None)
                        pv_te[kc] = t

                    emit_S(0)
                    emit_exp(0)
                    emit_S(1)
                    emit_exp(1)
                    for kc in range(32):
                        if kc + 2 < 32:
                            emit_S(kc + 2)
                        emit_PV(kc)
                        if kc + 2 < 32:
                            emit_exp(kc + 2)
                    last_te = pv_te[31]
                    kv_free_te[s] = last_te
                    for (h01, vh, gr0) in combos:
                        fvt_free_te[2 * h01 + vh] = last_te

                    P.wait("vector", "te", last_te)
                    evac_ve = P.op("vector", lambda e: e.tensor_copy(
                        of_sb[0:65, :], ps_O[0:65, :]),
                        inc="ve" if not combos else None)
                    for ci, (h01, vh, gr0) in enumerate(combos):
                        slot = 2 * h01 + vh
                        evac_ve = P.op("vector", lambda e, slot=slot: e.tensor_copy(
                            bvf_sb[0:64, 256 * slot:256 * slot + 256],
                            ps_B[0:64, 256 * slot:256 * slot + 256]),
                            inc="ve" if ci == len(combos) - 1 else None)
                    P.wait("tensor", "ve", evac_ve)
                    n_tr = 8 + 2 * len(combos)
                    i = 0
                    for h01 in range(2):
                        for u in range(4):
                            i += 1
                            tr_te = P.op("tensor", lambda e, h01=h01, u=u: e.transpose(
                                ps_O[:, 512 * h01 + 65 * u:512 * h01 + 65 * u + 65],
                                of_sb[0:65, 512 * h01 + 128 * u:
                                      512 * h01 + 128 * u + 128],
                                idf_sb[0:65, 0:65]),
                                inc="te" if i == n_tr else None)
                    for (h01, vh, gr0) in combos:
                        slot = 2 * h01 + vh
                        for uu in range(2):
                            i += 1
                            tr_te = P.op("tensor", lambda e, slot=slot, uu=uu:
                                         e.transpose(
                                             ps_B[:, 256 * slot + 64 * uu:
                                                  256 * slot + 64 * uu + 64],
                                             bvf_sb[0:64, 256 * slot + 128 * uu:
                                                    256 * slot + 128 * uu + 128],
                                             idf_sb[0:64, 0:64]),
                                         inc="te" if i == n_tr else None)
                    P.wait("vector", "te", tr_te)
                    if combos:
                        cb = None
                        for (h01, vh, gr0) in combos:
                            slot = 2 * h01 + vh
                            cb = P.op("vector", lambda e, slot=slot: e.tensor_copy(
                                bvt_sb[:, 256 * slot:256 * slot + 256],
                                ps_B[:, 256 * slot:256 * slot + 256]), inc="ve")
                        P.wait("vector", "ve", cb)
                    for h01 in range(2):
                        a = 2 * pr + h01
                        for u in range(4):
                            vh = u // 2
                            cr = P.op("vector", lambda e, h01=h01, u=u: e.reciprocal(
                                rcp_sb[:, 4 * h01 + u:4 * h01 + u + 1],
                                ps_O[:, 512 * h01 + 65 * u + 64:
                                     512 * h01 + 65 * u + 65]), inc="ve")
                            P.wait("vector", "ve", cr)
                            aff = any(c[0] == h01 and c[1] == vh for c in combos)
                            if aff:
                                slot = 2 * h01 + vh
                                in1 = bvt_sb[:, 256 * slot + 64 * (u % 2):
                                             256 * slot + 64 * (u % 2) + 64]
                            else:
                                in1 = w_bc[:, 65 * a:65 * a + 64]
                            P.wait("vector", "ve", asm_free_ve[u])
                            last = h01 == 1 and u == 3
                            stt_ve = P.op(
                                "vector", lambda e, h01=h01, u=u, a=a, in1=in1:
                                e.scalar_tensor_tensor(
                                    asm_sb[u][:, 64 * a:64 * a + 64],
                                    ps_O[:, 512 * h01 + 65 * u:
                                         512 * h01 + 65 * u + 64],
                                    rcp_sb[:, 4 * h01 + u:4 * h01 + u + 1],
                                    in1, ALU.mult, ALU.add),
                                inc="ve" if last else None)
                    psO_free_ve = stt_ve
                    psB_free_ve = stt_ve

                asm_ready = P.val("ve")
                for u in range(4):
                    sl = stats[u % 2]
                    g_tt = 4 * qc + u
                    hp = u % 2
                    P.wait("sync", "ve", x_freeB[hp])
                    lxb_cnt[hp] = P.op("sync", lambda e, g=g_tt, hp=hp: e.dma_start(
                        out=x_sb[hp][:], in_=d_x[128 * g:128 * g + 128, :]),
                        inc=f"lxb{hp}", inc_n=16)
                    c_ln1 = plan_ln(P, asm_sb[u][:], t1_sb[:], E, 1e-3, sl,
                                    ready_val=asm_ready)
                    asm_free_ve[u] = c_ln1
                    P.wait("vector", "ve", c_ln1)
                    c_g = P.op("vector", lambda e: e.scalar_tensor_tensor(
                        t2_sb[:], t1_sb[:], 0.0, gb_sb[:], ALU.bypass, ALU.mult),
                        inc="ve")
                    P.wait("vector", "ve", c_g)
                    c_b = P.op("vector", lambda e: e.scalar_tensor_tensor(
                        t2_sb[:], t2_sb[:], 0.0, bb_sb[:], ALU.bypass, ALU.add),
                        inc="ve")
                    c_ln2 = plan_ln(P, t2_sb[:], t1_sb[:], E, 1e-5, sl,
                                    ready_val=c_b)
                    P.wait("vector", f"lxb{hp}", lxb_cnt[hp])
                    P.wait("vector", "ve", c_ln2)
                    P.wait("vector", f"sx2{hp}", x2_free[hp])
                    c_x2 = P.op("vector", lambda e, hp=hp: e.tensor_tensor(
                        x2_sb[hp][:], x_sb[hp][:], t1_sb[:], ALU.add), inc="ve")
                    x_freeB[hp] = c_x2
                    P.wait("gpsimd", "ve", c_x2)
                    x2_free[hp] = P.op("gpsimd", lambda e, g=g_tt, hp=hp: e.dma_start(
                        out=d_x2[128 * g:128 * g + 128, :], in_=x2_sb[hp][:]),
                        inc=f"sx2{hp}", inc_n=16)
            b_x2_totals = {"sx20": x2_free[0], "sx21": x2_free[1]}
            if phases <= 2:
                for s_name, tot in b_x2_totals.items():
                    P.wait("sync", s_name, tot)
            P.flush(block)
        if phases <= 2:
            sAB.close()
            return nc

        sAB.close()
        with ExitStack() as sC:
            sbC = lambda n_, s_, d_: sC.enter_context(nc.sbuf_tensor(n_, s_, d_))
            xn2T = sbC("xn2T", [128, 8 * TFF], BF16)
            a_sb = sbC("a_sb", [128, 32 * TFF], BF16)
            h_sb = sbC("h_sb", [128, 32 * TFF], BF16)
            fT_sb = h_sb
            w12_sb = [sbC(f"w12_{i}", [128, 8 * 128], BF16) for i in range(4)]
            hsl_sb = [sbC(f"hsl{i}", [128, 32 * 128], BF16) for i in range(2)]
            x2t_sb = [sbC(f"x2t{i}", [128, E], F32) for i in range(2)]
            xn2_sb = [sbC(f"xn2_{i}", [128, E], BF16) for i in range(2)]
            silu_sb = [sbC(f"silu{i}", [128, 512], F32) for i in range(2)]
            ftm_sb = h_sb[:, 8 * TFF:8 * TFF + 2048].bitcast(F32)
            ps_M = [sC.enter_context(nc.psum_tensor(f"psM{i}", [128, 512], F32))
                    for i in range(4)]
            ps_tr2 = sC.enter_context(nc.psum_tensor("ps_tr2", [128, 2048], BF16))

            P.phase_barrier()

            lx2_cnt = [0, 0]
            x2t_free = [0, 0]
            w12_cnt = [0, 0, 0, 0]
            w12_free_te = [0, 0, 0, 0]
            hsl_cnt = [0, 0]
            hsl_free_te = [0, 0]
            silu_free_ve = [0, 0]
            psM_free = [0, 0, 0, 0]
            psM_free_eng = ["ve", "ve", "ve", "ve"]
            trp2 = [0]
            tr2_evac_ve = [0, 0]
            xn2_free_ve = [0, 0]
            out_free = [0, 0]

            def emit_xn2_unit(ft, u8):
                g_tt = 8 * ft + u8
                hp = u8 % 2
                tp2 = trp2[0]
                P.wait("sync", "ve", x2t_free[hp])
                lx2_cnt[hp] = P.op("sync", lambda e, g=g_tt, hp=hp: e.dma_start(
                    out=x2t_sb[hp][:], in_=d_x2[128 * g:128 * g + 128, :]),
                    inc=f"lx2{hp}", inc_n=16)
                P.wait("vector", f"lx2{hp}", lx2_cnt[hp])
                c_ln = plan_ln(P, x2t_sb[hp][:], xn2_sb[hp][:], E, 1e-5,
                               stats[hp], ready_val=xn2_free_ve[hp])
                x2t_free[hp] = c_ln
                P.wait("tensor", "ve", c_ln)
                P.wait("tensor", "ve", tr2_evac_ve[tp2])
                for f in range(8):
                    te_tr = P.op("tensor", lambda e, hp=hp, f=f, tp2=tp2:
                                 e.transpose(
                                     ps_tr2[:, 1024 * tp2 + 128 * f:
                                            1024 * tp2 + 128 * f + 128],
                                     xn2_sb[hp][:, 128 * f:128 * f + 128],
                                     idb_sb[:]), inc="te" if f == 7 else None)
                P.wait("vector", "te", te_tr)
                for f in range(8):
                    ev = P.op("vector", lambda e, f=f, u8=u8, tp2=tp2:
                              e.tensor_copy(
                                  xn2T[:, TFF * f + 128 * u8:
                                       TFF * f + 128 * u8 + 128],
                                  ps_tr2[:, 1024 * tp2 + 128 * f:
                                         1024 * tp2 + 128 * f + 128]),
                              inc="ve" if f == 7 else None)
                tr2_evac_ve[tp2] = ev
                xn2_free_ve[hp] = ev
                trp2[0] ^= 1

            for u8 in range(8):
                emit_xn2_unit(0, u8)

            for ft in range(4):
                xn2T_ve = P.val("ve")
                P.wait("tensor", "ve", xn2T_ve)

                for mc in range(32):
                    for wi, d_w in enumerate([d_w1b, d_w2b]):
                        slot = 2 * (mc % 2) + wi
                        P.wait("sync", "te", w12_free_te[slot])
                        w12_cnt[slot] = P.op("sync", lambda e, d_w=d_w, mc=mc,
                                             slot=slot: e.dma_start(
                                                 out=w12_sb[slot][:],
                                                 in_=d_w[mc, :, :]),
                                             inc=f"lw12_{slot}", inc_n=16)
                    for wi in range(2):
                        P.wait("tensor", f"lw12_{2 * (mc % 2) + wi}",
                               w12_cnt[2 * (mc % 2) + wi])
                    for nt in range(2):
                        for wi in range(2):
                            b = 2 * (mc % 2) + wi
                            slot = 2 * (mc % 2) + wi
                            P.wait("tensor", psM_free_eng[b], psM_free[b])
                            for ki in range(8):
                                te_w12 = P.op(
                                    "tensor", lambda e, slot=slot, ki=ki, nt=nt, b=b:
                                    e.matmul(
                                        ps_M[b][:],
                                        w12_sb[slot][:, 128 * ki:128 * ki + 128],
                                        xn2T[:, TFF * ki + 512 * nt:
                                             TFF * ki + 512 * nt + 512],
                                        start=(ki == 0), stop=(ki == 7)),
                                    inc="te" if ki == 7 else None)
                        b1, b2 = 2 * (mc % 2), 2 * (mc % 2) + 1
                        hp2 = nt
                        P.wait("scalar", "te", te_w12)
                        P.wait("scalar", "ve", silu_free_ve[hp2])
                        c_si = P.op("scalar", lambda e, b1=b1, mc=mc, hp2=hp2:
                                    e.activation(silu_sb[hp2][:], ps_M[b1][:],
                                                 AF.Silu,
                                                 scale=s1_sb[:, mc:mc + 1]),
                                    inc="se")
                        psM_free[b1] = c_si
                        psM_free_eng[b1] = "se"
                        P.wait("vector", "se", c_si)
                        c_a = P.op("vector", lambda e, b2=b2, mc=mc, nt=nt, hp2=hp2:
                                   e.scalar_tensor_tensor(
                                       a_sb[:, TFF * mc + 512 * nt:
                                            TFF * mc + 512 * nt + 512],
                                       ps_M[b2][:], s2_sb[:, mc:mc + 1],
                                       silu_sb[hp2][:], ALU.mult, ALU.mult),
                                   inc="ve")
                        silu_free_ve[hp2] = c_a
                        psM_free[b2] = c_a
                        psM_free_eng[b2] = "ve"
                        w12_free_te[b1] = te_w12
                        w12_free_te[b2] = te_w12

                def h_stage(d_ht, s_ap, src_sb, dst_sb, sname, interleave_xn2):
                    te_h = 0
                    for mc in range(32):
                        if interleave_xn2 and mc < 8:
                            emit_xn2_unit(ft + 1, mc)
                        slot = mc % 2
                        P.wait("sync", "te", hsl_free_te[slot])
                        hsl_cnt[slot] = P.op("sync", lambda e, d_ht=d_ht, mc=mc,
                                             slot=slot: e.dma_start(
                                                 out=hsl_sb[slot][:],
                                                 in_=d_ht[mc, :, :]),
                                             inc=f"{sname}{slot}", inc_n=16)
                        P.wait("tensor", f"{sname}{slot}", hsl_cnt[slot])
                        for nt in range(2):
                            b = 2 * slot + nt
                            P.wait("tensor", psM_free_eng[b], psM_free[b])
                            for ki in range(32):
                                te_h = P.op(
                                    "tensor", lambda e, slot=slot, ki=ki, nt=nt, b=b,
                                    src_sb=src_sb: e.matmul(
                                        ps_M[b][:],
                                        hsl_sb[slot][:, 128 * ki:128 * ki + 128],
                                        src_sb[:, TFF * ki + 512 * nt:
                                               TFF * ki + 512 * nt + 512],
                                        start=(ki == 0), stop=(ki == 31)),
                                    inc="te" if ki == 31 else None)
                            P.wait("scalar", "te", te_h)
                            c_g2 = P.op("scalar", lambda e, b=b, mc=mc, nt=nt,
                                        dst_sb=dst_sb, s_ap=s_ap: e.activation(
                                            dst_sb[:, TFF * mc + 512 * nt:
                                                   TFF * mc + 512 * nt + 512],
                                            ps_M[b][:], AF.Gelu,
                                            scale=s_ap[:, mc:mc + 1]), inc="se")
                            psM_free[b] = c_g2
                            psM_free_eng[b] = "se"
                        hsl_free_te[slot] = te_h
                    return P.val("se")

                a_ve = P.val("ve")
                P.wait("tensor", "ve", a_ve)
                se_h1 = h_stage(d_h1b, sh1_sb, a_sb, h_sb, "lh1_", ft < 3)
                P.wait("tensor", "se", se_h1)
                se_h2 = h_stage(d_h2b, sh2_sb, h_sb, a_sb, "lh2_", False)
                P.wait("tensor", "se", se_h2)

                for oc in range(8):
                    slot = oc % 2
                    P.wait("sync", "te", hsl_free_te[slot])
                    hsl_cnt[slot] = P.op("sync", lambda e, oc=oc, slot=slot:
                                         e.dma_start(out=hsl_sb[slot][:],
                                                     in_=d_w3b[oc, :, :]),
                                         inc=f"lw3_{slot}", inc_n=16)
                    P.wait("tensor", f"lw3_{slot}", hsl_cnt[slot])
                    for nt in range(2):
                        b = 2 * slot + nt
                        P.wait("tensor", psM_free_eng[b], psM_free[b])
                        for ki in range(32):
                            te_w3 = P.op(
                                "tensor", lambda e, slot=slot, ki=ki, nt=nt, b=b:
                                e.matmul(ps_M[b][:],
                                         hsl_sb[slot][:, 128 * ki:128 * ki + 128],
                                         a_sb[:, TFF * ki + 512 * nt:
                                              TFF * ki + 512 * nt + 512],
                                         start=(ki == 0), stop=(ki == 31)),
                                inc="te" if ki == 31 else None)
                        P.wait("vector", "te", te_w3)
                        c_f = P.op("vector", lambda e, b=b, oc=oc, nt=nt:
                                   e.tensor_scalar(
                                       fT_sb[:, TFF * oc + 512 * nt:
                                             TFF * oc + 512 * nt + 512],
                                       ps_M[b][:], s3_sb[:, oc:oc + 1], None,
                                       ALU.mult), inc="ve")
                        psM_free[b] = c_f
                        psM_free_eng[b] = "ve"
                    hsl_free_te[slot] = te_w3
                fT_ve = P.val("ve")
                P.wait("tensor", "ve", fT_ve)

                for u8 in range(8):
                    g_tt = 8 * ft + u8
                    hp = u8 % 2
                    tp2 = trp2[0]
                    P.wait("tensor", "ve", tr2_evac_ve[tp2])
                    for f in range(8):
                        te_tr = P.op("tensor", lambda e, f=f, u8=u8, tp2=tp2:
                                     e.transpose(
                                         ps_tr2[:, 1024 * tp2 + 128 * f:
                                                1024 * tp2 + 128 * f + 128],
                                         fT_sb[:, TFF * f + 128 * u8:
                                               TFF * f + 128 * u8 + 128],
                                         idb_sb[:]), inc="te" if f == 7 else None)
                    P.wait("vector", "te", te_tr)
                    c_ftm = P.op("vector", lambda e, tp2=tp2: e.tensor_copy(
                        ftm_sb, ps_tr2[:, 1024 * tp2:1024 * tp2 + 1024]),
                        inc="ve")
                    tr2_evac_ve[tp2] = c_ftm
                    trp2[0] ^= 1
                    P.wait("sync", "ve", x2t_free[hp])
                    lx2_cnt[hp] = P.op("sync", lambda e, g=g_tt, hp=hp: e.dma_start(
                        out=x2t_sb[hp][:], in_=d_x2[128 * g:128 * g + 128, :]),
                        inc=f"lx2{hp}", inc_n=16)
                    c_ln = plan_ln(P, ftm_sb, t1_sb[:], E, 1e-5, stats[2 + hp],
                                   ready_val=c_ftm)
                    P.wait("vector", f"lx2{hp}", lx2_cnt[hp])
                    P.wait("vector", "ve", c_ln)
                    c_o = P.op("vector", lambda e, hp=hp: e.tensor_tensor(
                        x2t_sb[hp][:], x2t_sb[hp][:], t1_sb[:], ALU.add), inc="ve")
                    x2t_free[hp] = c_o
                    P.wait("sync", "ve", c_o)
                    out_free[hp] = P.op("sync", lambda e, g=g_tt, hp=hp: e.dma_start(
                        out=d_out[128 * g:128 * g + 128, :], in_=x2t_sb[hp][:]),
                        inc=f"sout{hp}", inc_n=16)
                    P.wait("sync", f"sout{hp}", out_free[hp])

            P.wait("sync", "sout0", out_free[0])
            P.wait("sync", "sout1", out_free[1])
            P.flush(block)

    return nc


_PERM = None


def _perm():
    global _PERM
    if _PERM is None:
        p = np.arange(NT)
        _PERM = 16 * (p % 256) + p // 256
    return _PERM


def _prep_core_inputs(xb, d):
    perm = _perm()
    rel = d["L0_attn_rel_table"]
    jj = perm // 16
    cc = perm % 16
    gr = np.arange(2048)
    idx = np.minimum(2047 - gr[None, :] + jj[:, None], 2046)
    fvt = np.ascontiguousarray(rel[idx, cc[:, None]])
    cmat = np.ascontiguousarray(
        np.broadcast_to(np.repeat(rel[2046], 2)[None, :], (128, 32)))
    return {
        "x": np.ascontiguousarray(xb[perm]),
        "wq": d["L0_attn_Wq"], "wk": d["L0_attn_Wk"], "wv": d["L0_attn_Wv"],
        "fvt": fvt, "cmat": cmat,
        "gamma_b": np.ascontiguousarray(
            np.broadcast_to(d["L0_attn_gamma"][None, :], (128, E))),
        "beta_b": np.ascontiguousarray(
            np.broadcast_to(d["L0_attn_beta"][None, :], (128, E))),
        "w1t": np.ascontiguousarray(d["L0_ffn_w1_w"].T),
        "w2t": np.ascontiguousarray(d["L0_ffn_w2_w"].T),
        "h1t": np.ascontiguousarray(d["L0_ffn_h1_w"].T),
        "h2t": np.ascontiguousarray(d["L0_ffn_h2_w"].T),
        "w3t": np.ascontiguousarray(d["L0_ffn_w3_w"].T),
        "s1m": np.ascontiguousarray(d["L0_ffn_w1_s"].reshape(32, 128).T),
        "s2m": np.ascontiguousarray(d["L0_ffn_w2_s"].reshape(32, 128).T),
        "sh1m": np.ascontiguousarray(d["L0_ffn_h1_s"].reshape(32, 128).T),
        "sh2m": np.ascontiguousarray(d["L0_ffn_h2_s"].reshape(32, 128).T),
        "s3m": np.ascontiguousarray(d["L0_ffn_w3_s"].reshape(8, 128).T),
        "id_bf": np.eye(128, dtype=np.float32).astype(ml_dtypes.bfloat16),
        "id_f32": np.eye(128, dtype=np.float32),
    }


def kernel(**inputs):
    x = np.asarray(inputs["L0_attn_x"], dtype=np.float32)
    d = {k: np.asarray(v, dtype=np.float32) for k, v in inputs.items()}
    nc = build_program()
    in_maps = [_prep_core_inputs(x[b], d) for b in range(8)]
    res = run_bass_kernel_spmd(nc, in_maps, core_ids=list(range(8)))
    perm = _perm()
    out = np.empty((8, NT, E), dtype=np.float32)
    for b in range(8):
        out[b][perm] = res.results[b]["out"]
    return out


# version 08
# speedup vs baseline: 1.2734x; 1.1410x over previous
import numpy as np
import ml_dtypes
import concourse.bass as bass
import concourse.mybir as mybir
from concourse.bass_utils import run_bass_kernel_spmd
from contextlib import ExitStack

F32 = mybir.dt.float32
BF16 = mybir.dt.bfloat16
AF = mybir.ActivationFunctionType
ALU = mybir.AluOpType
AX = mybir.AxisListType

NT = 4096
E = 1024
H = 16
HID = 4096
NQC = 8
TFF = 1024

ENGINES = ("sync", "scalar", "vector", "tensor", "gpsimd")


class Plan:

    def __init__(self, nc, stack):
        self.nc = nc
        self.stack = stack
        self.thunks = {e: [] for e in ENGINES}
        self.sems = {}
        self.counts = {}

    def sem(self, name):
        if name not in self.sems:
            self.sems[name] = self.stack.enter_context(self.nc.semaphore(name))
            self.counts[name] = 0
        return self.sems[name]

    def op(self, eng, fn, inc=None, inc_n=1):
        if inc is not None:
            self.sem(inc)
            self.counts[inc] += inc_n
            target = self.counts[inc]
            sem = self.sems[inc]

            def thunk(e, fn=fn, sem=sem, inc_n=inc_n):
                fn(e).then_inc(sem, inc_n)
        else:
            target = None

            def thunk(e, fn=fn):
                fn(e)
        self.thunks[eng].append(thunk)
        return target

    def wait(self, eng, name, val):
        if val is None or val <= 0:
            return
        sem = self.sem(name)
        self.thunks[eng].append(lambda e, s=sem, v=val: e.wait_ge(s, v))

    def val(self, name):
        self.sem(name)
        return self.counts[name]

    def phase_barrier(self, exclude=()):
        snap = [(n, v) for n, v in self.counts.items()
                if v > 0 and n not in exclude]
        for eng in ENGINES:
            for n, v in snap:
                self.wait(eng, n, v)

    def flush(self, block):
        for eng in ENGINES:
            if not self.thunks[eng]:
                continue
            fns = self.thunks[eng]
            self.thunks[eng] = []
            getattr(block, eng)(lambda e, fns=fns: [f(e) for f in fns])


def plan_ln(P, x_ap, out_ap, n, eps, sl, ready_val=0):
    P.wait("vector", "ve", ready_val)
    c1 = P.op("vector", lambda e: e.reduce_sum(sl["rs"], x_ap, axis=AX.X), inc="ve")
    P.wait("vector", "ve", c1)
    c2 = P.op("vector", lambda e: e.tensor_scalar_mul(sl["nmu"], sl["rs"], -1.0 / n),
              inc="ve")
    P.wait("scalar", "ve", c2)
    c3 = P.op("scalar", lambda e: e.activation(sl["sq_scr"], x_ap, AF.Square,
                                               bias=sl["nmu"], scale=1.0,
                                               accum_out=sl["var"]), inc="se")
    P.wait("vector", "se", c3)
    c4 = P.op("vector", lambda e: e.tensor_scalar(sl["var"], sl["var"], 1.0 / n, eps,
                                                  ALU.mult, ALU.add), inc="ve")
    P.wait("scalar", "ve", c4)
    c5 = P.op("scalar", lambda e: e.activation(sl["sd"], sl["var"], AF.Sqrt), inc="se")
    P.wait("vector", "se", c5)
    c6 = P.op("vector", lambda e: e.reciprocal(sl["rstd"], sl["sd"]), inc="ve")
    P.wait("vector", "ve", c6)
    c7 = P.op("vector", lambda e: e.tensor_tensor(sl["nmur"], sl["nmu"], sl["rstd"],
                                                  ALU.mult), inc="ve")
    P.wait("vector", "ve", c7)
    c8 = P.op("vector", lambda e: e.tensor_scalar(out_ap, x_ap, sl["rstd"], sl["nmur"],
                                                  ALU.mult, ALU.add), inc="ve")
    return c8


def build_program(debug=False, phases=3, nqc=NQC, sub="all"):
    nc = bass.Bass("TRN2", target_bir_lowering=False)

    d_x = nc.dram_tensor("x", [NT, E], F32, kind="ExternalInput").ap()
    d_wq = nc.dram_tensor("wq", [E, E], F32, kind="ExternalInput").ap()
    d_wk = nc.dram_tensor("wk", [E, E], F32, kind="ExternalInput").ap()
    d_wv = nc.dram_tensor("wv", [E, E], F32, kind="ExternalInput").ap()
    d_fvt = nc.dram_tensor("fvt", [NT, 2048], F32, kind="ExternalInput").ap()
    d_cmat = nc.dram_tensor("cmat", [128, 32], F32, kind="ExternalInput").ap()
    d_gb = nc.dram_tensor("gamma_b", [128, E], F32, kind="ExternalInput").ap()
    d_bb = nc.dram_tensor("beta_b", [128, E], F32, kind="ExternalInput").ap()
    d_w1t = nc.dram_tensor("w1t", [E, HID], F32, kind="ExternalInput").ap()
    d_w2t = nc.dram_tensor("w2t", [E, HID], F32, kind="ExternalInput").ap()
    d_h1t = nc.dram_tensor("h1t", [HID, HID], F32, kind="ExternalInput").ap()
    d_h2t = nc.dram_tensor("h2t", [HID, HID], F32, kind="ExternalInput").ap()
    d_w3t = nc.dram_tensor("w3t", [HID, E], F32, kind="ExternalInput").ap()
    d_s1 = nc.dram_tensor("s1m", [128, 32], F32, kind="ExternalInput").ap()
    d_s2 = nc.dram_tensor("s2m", [128, 32], F32, kind="ExternalInput").ap()
    d_sh1 = nc.dram_tensor("sh1m", [128, 32], F32, kind="ExternalInput").ap()
    d_sh2 = nc.dram_tensor("sh2m", [128, 32], F32, kind="ExternalInput").ap()
    d_s3 = nc.dram_tensor("s3m", [128, 8], F32, kind="ExternalInput").ap()
    d_idb = nc.dram_tensor("id_bf", [128, 128], BF16, kind="ExternalInput").ap()
    d_idf = nc.dram_tensor("id_f32", [128, 128], F32, kind="ExternalInput").ap()
    d_out = nc.dram_tensor("out", [NT, E], F32, kind="ExternalOutput").ap()

    sk = "ExternalOutput" if debug else "Internal"
    d_qs = nc.dram_tensor("qs", [NQC, E, 512], BF16, kind=sk).ap()
    d_ks = nc.dram_tensor("ks", [NQC, E, 512], BF16, kind=sk).ap()
    d_vs = nc.dram_tensor("vs", [NT, H * 65], BF16, kind=sk).ap()
    d_x2 = nc.dram_tensor("x2s", [NT, E], F32, kind=sk).ap()
    d_fvtb = nc.dram_tensor("fvtb", [NT, 2048], BF16, kind=sk).ap()
    d_w1b = nc.dram_tensor("w1b", [32, 128, 1024], BF16, kind=sk).ap()
    d_w2b = nc.dram_tensor("w2b", [32, 128, 1024], BF16, kind="Internal").ap()
    d_h1b = nc.dram_tensor("h1b", [32, 128, 4096], BF16, kind="Internal").ap()
    d_h2b = nc.dram_tensor("h2b", [32, 128, 4096], BF16, kind="Internal").ap()
    d_w3b = nc.dram_tensor("w3b", [8, 128, 4096], BF16, kind="Internal").ap()

    stack = ExitStack()
    with stack:
        P = Plan(nc, stack)
        sb = lambda name, shape, dt: stack.enter_context(nc.sbuf_tensor(name, shape, dt))

        gb_sb = sb("gb", [128, E], F32)
        bb_sb = sb("bb", [128, E], F32)
        c_f32 = sb("c_f32", [128, 32], F32)
        c_bf = sb("c_bf", [128, 32], BF16)
        s1_sb = sb("s1", [128, 32], F32)
        s2_sb = sb("s2", [128, 32], F32)
        sh1_sb = sb("sh1", [128, 32], F32)
        sh2_sb = sb("sh2", [128, 32], F32)
        s3_sb = sb("s3", [128, 8], F32)
        idb_sb = sb("idb", [128, 128], BF16)
        idf_sb = sb("idf", [128, 128], F32)
        ones_f = sb("ones_f", [1, 128], F32)
        w_bc = sb("w_bc", [128, H * 65], F32)
        sq_scr = sb("sq_scr", [128, E], F32)
        t1_sb = sb("t1", [128, E], F32)
        st_t = sb("stats", [128, 32], F32)
        stats = []
        for s in range(4):
            stats.append({"rs": st_t[:, 8 * s:8 * s + 1],
                          "nmu": st_t[:, 8 * s + 1:8 * s + 2],
                          "var": st_t[:, 8 * s + 2:8 * s + 3],
                          "sd": st_t[:, 8 * s + 3:8 * s + 4],
                          "rstd": st_t[:, 8 * s + 4:8 * s + 5],
                          "nmur": st_t[:, 8 * s + 5:8 * s + 6],
                          "sq_scr": sq_scr[:]})

        sAB = ExitStack()
        pcst = [sAB.enter_context(nc.sbuf_tensor(f"pcst{i}", [128, 4096], BF16))
                for i in range(2)]

        block = stack.enter_context(nc.Block())

        with ExitStack() as sA:
            sbA = lambda n_, s_, d_: sA.enter_context(nc.sbuf_tensor(n_, s_, d_))
            wq_sb = sbA("wq_sb", [128, 8 * E], BF16)
            wk_sb = sbA("wk_sb", [128, 8 * E], BF16)
            wv_sb = sbA("wv_sb", [128, 8 * E], BF16)
            xn1_sb = [sbA(f"xn1_{i}", [128, E], BF16) for i in range(2)]
            xnT_sb = sbA("xnT", [128, 8 * 512], BF16)
            qe_sb = [sbA(f"qe{i}", [128, 512], BF16) for i in range(2)]
            ke_sb = [sbA(f"ke{i}", [128, 512], BF16) for i in range(2)]
            va_sb = [sbA(f"va{i}", [128, H * 65], BF16) for i in range(2)]
            w_row = sbA("w_row", [1, H * 65], F32)
            w_hi = sbA("w_hi", [1, H * 65], BF16)
            w_rem = sbA("w_rem", [1, H * 65], F32)
            w_lo = sbA("w_lo", [1, H * 65], BF16)
            ones_b = sbA("ones_b", [1, 128], BF16)
            x_sb = [sbA(f"xA{i}", [128, E], F32) for i in range(2)]
            ps_mm = sA.enter_context(nc.psum_tensor("ps_mm", [128, 1024], F32))
            ps_tr = sA.enter_context(nc.psum_tensor("ps_tr", [128, 2048], BF16))
            ps_w = sA.enter_context(nc.psum_tensor("ps_w", [1, 1536], F32))

            for dd, ss in [(d_gb, gb_sb), (d_bb, bb_sb), (d_cmat, c_f32),
                           (d_s1, s1_sb), (d_s2, s2_sb), (d_sh1, sh1_sb),
                           (d_sh2, sh2_sb), (d_s3, s3_sb), (d_idb, idb_sb),
                           (d_idf, idf_sb)]:
                l_set = P.op("sync", lambda e, dd=dd, ss=ss:
                             e.dma_start(out=ss[:], in_=dd[:]), inc="l_set", inc_n=16)
            for dd, ss in [(d_wq, wq_sb), (d_wk, wk_sb), (d_wv, wv_sb)]:
                src = dd.rearrange("(ki p) o -> p ki o", p=128)
                dst = ss[:].rearrange("p (ki o) -> p ki o", ki=8)
                l_wqkv = P.op("gpsimd", lambda e, s1=src, d1=dst:
                              e.dma_start(out=d1, in_=s1), inc="l_wqkv", inc_n=16)
            pc_ld = [0, 0]
            pc_st = [0, 0]
            pc_idx = 0
            for c in range(32):
                sl = pc_idx % 2
                pc_idx += 1
                P.wait("gpsimd", f"pcs{sl}", pc_st[sl])
                pc_ld[sl] = P.op("gpsimd", lambda e, c=c, sl=sl: e.dma_start(
                    out=pcst[sl][:, 0:2048],
                    in_=d_fvt[128 * c:128 * c + 128, :]), inc=f"pcl{sl}", inc_n=16)
                P.wait("gpsimd", f"pcl{sl}", pc_ld[sl])
                pc_st[sl] = P.op("gpsimd", lambda e, c=c, sl=sl: e.dma_start(
                    out=d_fvtb[128 * c:128 * c + 128, :],
                    in_=pcst[sl][:, 0:2048]), inc=f"pcs{sl}", inc_n=16)
            fvt_pc_done = (pc_st[0], pc_st[1])
            for dsrc, ddst, nsl, nki in [(d_w1t, d_w1b, 32, 8),
                                         (d_w2t, d_w2b, 32, 8),
                                         (d_h1t, d_h1b, 32, 32),
                                         (d_h2t, d_h2b, 32, 32),
                                         (d_w3t, d_w3b, 8, 32)]:
                wid = nki * 128
                for c in range(nsl):
                    sl = pc_idx % 2
                    pc_idx += 1
                    srcap = dsrc[:, 128 * c:128 * c + 128].rearrange(
                        "(ki p) o -> p ki o", p=128)
                    dstap = pcst[sl][:, 0:wid].rearrange(
                        "p (ki o) -> p ki o", ki=nki)
                    P.wait("gpsimd", f"pcs{sl}", pc_st[sl])
                    pc_ld[sl] = P.op("gpsimd", lambda e, srcap=srcap, dstap=dstap,
                                     sl=sl: e.dma_start(out=dstap, in_=srcap),
                                     inc=f"pcl{sl}", inc_n=16)
                    P.wait("gpsimd", f"pcl{sl}", pc_ld[sl])
                    pc_st[sl] = P.op("gpsimd", lambda e, ddst=ddst, c=c, sl=sl,
                                     wid=wid: e.dma_start(
                                         out=ddst[c, :, :],
                                         in_=pcst[sl][:, 0:wid]),
                                     inc=f"pcs{sl}", inc_n=16)
            P.wait("vector", "l_set", l_set)
            P.op("vector", lambda e: e.tensor_copy(c_bf[:], c_f32[:]), inc="ve")
            P.op("vector", lambda e: e.memset(ones_f[:], 1.0), inc="ve")
            setup_ve = P.op("vector", lambda e: e.memset(ones_b[:], 1.0), inc="ve")

            lx_cnt = [0, 0]
            xn_free_ve = [0, 0]
            x_freeA = [0, 0]
            stq_free = [0, 0]
            stk_free = [0, 0]
            stv_free = [0, 0]
            trp = 0
            tr_evac_ve = [0, 0]
            mm_free_ve = [0, 0]
            te_w = 0
            last_g = 4 * nqc - 1
            for qc in range(nqc):
                for t in range(4):
                    g_tt = 4 * qc + t
                    s = g_tt % 2
                    P.wait("sync", "ve", x_freeA[s])
                    lx_cnt[s] = P.op("sync", lambda e, g=g_tt, s=s: e.dma_start(
                        out=x_sb[s][:], in_=d_x[128 * g:128 * g + 128, :]),
                        inc=f"lx{s}", inc_n=16)
                    P.wait("vector", f"lx{s}", lx_cnt[s])
                    ln_ve = plan_ln(P, x_sb[s][:], xn1_sb[s][:], E, 1e-5, stats[s],
                                    ready_val=xn_free_ve[s])
                    x_freeA[s] = ln_ve
                    P.wait("tensor", "ve", ln_ve)
                    P.wait("tensor", "ve", tr_evac_ve[trp])
                    for f in range(8):
                        te_tr = P.op("tensor", lambda e, s=s, f=f, trp=trp: e.transpose(
                            ps_tr[:, 1024 * trp + 128 * f:1024 * trp + 128 * f + 128],
                            xn1_sb[s][:, 128 * f:128 * f + 128], idb_sb[:]),
                            inc="te" if f == 7 else None)
                    P.wait("vector", "te", te_tr)
                    for f in range(8):
                        ev = P.op("vector", lambda e, f=f, t=t, trp=trp: e.tensor_copy(
                            xnT_sb[:, 512 * f + 128 * t:512 * f + 128 * t + 128],
                            ps_tr[:, 1024 * trp + 128 * f:1024 * trp + 128 * f + 128]),
                            inc="ve" if f == 7 else None)
                    tr_evac_ve[trp] = ev
                    xn_free_ve[s] = ev
                    trp ^= 1
                if sub not in ("qk", "all"):
                    continue
                P.wait("tensor", "l_wqkv", l_wqkv)
                P.wait("tensor", "ve", setup_ve)
                xnT_ready = P.val("ve")
                P.wait("tensor", "ve", xnT_ready)
                for w_sb, ev_sb, ev_free, d_dst, sem_st in [
                        (wq_sb, qe_sb, stq_free, d_qs, "stq"),
                        (wk_sb, ke_sb, stk_free, d_ks, "stk")]:
                    for f in range(8):
                        hp = f % 2
                        P.wait("tensor", "ve", mm_free_ve[hp])
                        for ki in range(8):
                            te_mm = P.op(
                                "tensor", lambda e, w_sb=w_sb, f=f, ki=ki, hp=hp:
                                e.matmul(ps_mm[:, 512 * hp:512 * hp + 512],
                                         w_sb[:, E * ki + 128 * f:E * ki + 128 * f + 128],
                                         xnT_sb[:, 512 * ki:512 * ki + 512],
                                         start=(ki == 0), stop=(ki == 7)),
                                inc="te" if ki == 7 else None)
                        P.wait("vector", "te", te_mm)
                        P.wait("vector", f"{sem_st}{hp}", ev_free[hp])
                        cv = P.op("vector", lambda e, ev_sb=ev_sb, hp=hp: e.tensor_copy(
                            ev_sb[hp][:], ps_mm[:, 512 * hp:512 * hp + 512]), inc="ve")
                        mm_free_ve[hp] = cv
                        P.wait("sync", "ve", cv)
                        ev_free[hp] = P.op(
                            "sync", lambda e, ev_sb=ev_sb, hp=hp, qc=qc, f=f,
                            d_dst=d_dst: e.dma_start(
                                out=d_dst[qc, 128 * f:128 * f + 128, :],
                                in_=ev_sb[hp][:]),
                            inc=f"{sem_st}{hp}", inc_n=16)
                if sub not in ("v", "all", "vmm", "vst"):
                    continue
                for t in range(4):
                    g_tt = 4 * qc + t
                    hp = t % 2
                    P.wait("vector", f"stv{hp}", stv_free[hp])
                    mv = P.op("vector", lambda e, hp=hp: e.memset(va_sb[hp][:], 1.0),
                              inc="ve")
                    for nf in range(2):
                        P.wait("tensor", "ve", mm_free_ve[nf])
                        for ki in range(8):
                            te_mm = P.op("tensor", lambda e, t=t, ki=ki, nf=nf:
                                         e.matmul(
                                             ps_mm[:, 512 * nf:512 * nf + 512],
                                             xnT_sb[:, 512 * ki + 128 * t:
                                                    512 * ki + 128 * t + 128],
                                             wv_sb[:, E * ki + 512 * nf:
                                                   E * ki + 512 * nf + 512],
                                             start=(ki == 0), stop=(ki == 7)),
                                         inc="te" if ki == 7 else None)
                        P.wait("vector", "te", te_mm)
                        P.wait("vector", "ve", mv)
                        dst = va_sb[hp][:].rearrange("p (h c) -> p h c", c=65)[
                            :, 8 * nf:8 * nf + 8, 0:64]
                        src = ps_mm[:, 512 * nf:512 * nf + 512].rearrange(
                            "p (h c) -> p h c", c=64)
                        cv = P.op("vector", lambda e, dst=dst, src=src:
                                  e.tensor_copy(dst, src), inc="ve")
                        mm_free_ve[nf] = cv
                    va_ve = P.val("ve")
                    if sub != "vmm":
                        P.wait("sync", "ve", va_ve)
                        stv_free[hp] = P.op("sync", lambda e, hp=hp, g=g_tt: e.dma_start(
                            out=d_vs[128 * g:128 * g + 128, :], in_=va_sb[hp][:]),
                            inc=f"stv{hp}", inc_n=16)
                    if sub in ("vmm", "vst"):
                        continue
                    P.wait("tensor", "ve", va_ve)
                    for ch, (c0, c1) in enumerate([(0, 512), (512, 1024),
                                                   (1024, 1040)]):
                        te_w = P.op("tensor", lambda e, g=g_tt, hp=hp, c0=c0, c1=c1:
                                    e.matmul(ps_w[0:1, c0:c1], c_bf[:, g:g + 1],
                                             va_sb[hp][:, c0:c1],
                                             start=(g == 0), stop=(g == last_g),
                                             skip_group_check=True),
                                    inc="te" if (ch == 2 and g_tt == last_g) else None)
            w_ready_ve = 0
            if sub in ("v", "all"):
                P.wait("vector", "te", te_w)
                cw = P.op("vector", lambda e: e.tensor_copy(
                    w_row[:], ps_w[0:1, 0:H * 65]), inc="ve")
                P.wait("vector", "ve", cw)
                c_hi = P.op("vector", lambda e: e.tensor_copy(w_hi[:], w_row[:]),
                            inc="ve")
                P.wait("vector", "ve", c_hi)
                c_rm = P.op("vector", lambda e: e.tensor_tensor(
                    w_rem[:], w_row[:], w_hi[:], ALU.subtract), inc="ve")
                P.wait("vector", "ve", c_rm)
                c_lo = P.op("vector", lambda e: e.tensor_copy(w_lo[:], w_rem[:]),
                            inc="ve")
                P.wait("tensor", "ve", c_lo)
                P.wait("tensor", "ve", mm_free_ve[0])
                P.wait("tensor", "ve", mm_free_ve[1])
                for c0, c1 in [(0, 512), (512, 1024), (1024, 1040)]:
                    P.op("tensor", lambda e, c0=c0, c1=c1: e.matmul(
                        ps_mm[:, 0:c1 - c0], ones_b[:], w_hi[0:1, c0:c1],
                        start=True, stop=False))
                    tw = P.op("tensor", lambda e, c0=c0, c1=c1: e.matmul(
                        ps_mm[:, 0:c1 - c0], ones_b[:], w_lo[0:1, c0:c1],
                        start=False, stop=True), inc="te")
                    P.wait("vector", "te", tw)
                    cbc = P.op("vector", lambda e, c0=c0, c1=c1: e.tensor_copy(
                        w_bc[:, c0:c1], ps_mm[:, 0:c1 - c0]), inc="ve")
                    P.wait("tensor", "ve", cbc)
                    w_ready_ve = cbc
            a_store_totals = {"stq0": stq_free[0], "stq1": stq_free[1],
                              "stk0": stk_free[0], "stk1": stk_free[1],
                              "stv0": stv_free[0], "stv1": stv_free[1]}
            if phases <= 1:
                for s_name, tot in a_store_totals.items():
                    P.wait("sync", s_name, tot)
            P.flush(block)
        if phases <= 1:
            sAB.close()
            return nc

        with ExitStack() as sB:
            sbB = lambda n_, s_, d_: sB.enter_context(nc.sbuf_tensor(n_, s_, d_))
            kp_sb = [sbB(f"kp{i}", [128, NT], BF16) for i in range(2)]
            vp_sb = [sbB(f"vp{i}", [128, 32 * 130], BF16) for i in range(2)]
            qp_sb = [sbB(f"qp{i}", [128, 512], BF16) for i in range(2)]
            fvt_sb = [sbB(f"fvt{i}", [128, 32 * 256], BF16) for i in range(4)]
            p_sb = [sbB(f"p{i}", [128, 1024], BF16) for i in range(2)]
            of_sb = sbB("of", [128, 1024], F32)
            bvf_sb = sbB("bvf", [128, 1024], F32)
            bvt_sb = sbB("bvt", [128, 1024], F32)
            asm_sb = [sbB(f"asm{i}", [128, E], F32) for i in range(4)]
            x2_sb = [sbB(f"x2_{i}", [128, E], F32) for i in range(2)]
            t2_sb = sbB("t2", [128, E], F32)
            x_sb = [sbB(f"xB{i}", [128, E], F32) for i in range(2)]
            rcp_sb = sbB("rcp", [128, 8], F32)
            ps_S = [sB.enter_context(nc.psum_tensor(f"psS{i}", [128, 1024], F32))
                    for i in range(2)]
            ps_O = sB.enter_context(nc.psum_tensor("psO", [128, 1024], F32))
            ps_B = sB.enter_context(nc.psum_tensor("psB", [128, 1024], F32))

            P.phase_barrier(exclude=("pcl0", "pcl1", "pcs0", "pcs1"))
            P.wait("sync", "pcs0", fvt_pc_done[0])
            P.wait("sync", "pcs1", fvt_pc_done[1])

            kv_cnt = [0, 0]
            fvt_cnt = [0, 0, 0, 0]
            kv_free_te = [0, 0]
            fvt_free_te = [0, 0, 0, 0]
            psS_free_se = [0, 0]
            psO_free_ve = 0
            psB_free_ve = 0
            asm_free_ve = [0, 0, 0, 0]
            x2_free = [0, 0]
            x_freeB = [0, 0]
            lxb_cnt = [0, 0]
            pending_epi = [None]

            for qc in range(NQC):
                for pr in range(8):
                    s = pr % 2
                    combos = []
                    for h01 in range(2):
                        for vh in range(2):
                            a = 2 * pr + h01
                            v = 2 * qc + vh
                            if a - v >= 8:
                                combos.append((h01, vh, 256 * (a - v) - 2048))
                    P.wait("sync", "te", kv_free_te[s])
                    src = d_ks[:, 128 * pr:128 * pr + 128, :].rearrange(
                        "b p c -> p b c")
                    dst = kp_sb[s][:].rearrange("p (b c) -> p b c", b=8)
                    P.op("sync", lambda e, s1=src, d1=dst: e.dma_start(out=d1, in_=s1),
                         inc=f"lkv{s}", inc_n=16)
                    src = d_vs[:, 130 * pr:130 * pr + 130].rearrange(
                        "(kc p) c -> p kc c", p=128)
                    dst = vp_sb[s][:].rearrange("p (kc c) -> p kc c", kc=32)
                    P.op("sync", lambda e, s1=src, d1=dst: e.dma_start(out=d1, in_=s1),
                         inc=f"lkv{s}", inc_n=16)
                    kv_cnt[s] = P.op("sync", lambda e, qc=qc, pr=pr, s=s: e.dma_start(
                        out=qp_sb[s][:], in_=d_qs[qc, 128 * pr:128 * pr + 128, :]),
                        inc=f"lkv{s}", inc_n=16)
                    for (h01, vh, gr0) in combos:
                        slot = 2 * h01 + vh
                        P.wait("sync", "te", fvt_free_te[slot])
                        src = d_fvtb[:, gr0:gr0 + 256].rearrange(
                            "(kc p) c -> p kc c", p=128)
                        dst = fvt_sb[slot][:].rearrange("p (kc c) -> p kc c", kc=32)
                        fvt_cnt[slot] = P.op("sync", lambda e, s1=src, d1=dst:
                                             e.dma_start(out=d1, in_=s1),
                                             inc=f"lfv{slot}", inc_n=16)

                    P.wait("tensor", f"lkv{s}", kv_cnt[s])
                    s_te = {}
                    exp_se = {}
                    pv_te = {}

                    def emit_S(kc, s=s):
                        b = kc % 2
                        P.wait("tensor", "se", psS_free_se[b])
                        P.op("tensor", lambda e, kc=kc, b=b, s=s: e.matmul(
                            ps_S[b][:, 0:512],
                            kp_sb[s][0:64, 128 * kc:128 * kc + 128],
                            qp_sb[s][0:64, :], start=True, stop=True))
                        s_te[kc] = P.op("tensor", lambda e, kc=kc, b=b, s=s: e.matmul(
                            ps_S[b][:, 512:1024],
                            kp_sb[s][64:128, 128 * kc:128 * kc + 128],
                            qp_sb[s][64:128, :], start=True, stop=True), inc="te")

                    def emit_exp(kc):
                        b = kc % 2
                        gate = pv_te[kc - 2] if kc >= 2 else s_te[kc]
                        P.wait("scalar", "te", gate)
                        c = P.op("scalar", lambda e, b=b: e.activation(
                            p_sb[b][:], ps_S[b][:], AF.Exp, scale=1.0 / 32.0),
                            inc="se")
                        exp_se[kc] = c
                        psS_free_se[b] = c

                    def emit_PV(kc, s=s, combos=combos):
                        b = kc % 2
                        P.wait("tensor", "se", exp_se[kc])
                        if kc == 0:
                            P.wait("tensor", "ve", psO_free_ve)
                            if combos:
                                P.wait("tensor", "ve", psB_free_ve)
                                for (h01, vh, gr0) in combos:
                                    slot = 2 * h01 + vh
                                    P.wait("tensor", f"lfv{slot}", fvt_cnt[slot])
                        n_ops = 2 + len(combos)
                        i = 0
                        t = None
                        for h01 in range(2):
                            i += 1
                            t = P.op("tensor", lambda e, kc=kc, b=b, s=s, h01=h01:
                                     e.matmul(
                                         ps_O[0:65, 512 * h01:512 * h01 + 512],
                                         vp_sb[s][:, 130 * kc + 65 * h01:
                                                   130 * kc + 65 * h01 + 65],
                                         p_sb[b][:, 512 * h01:512 * h01 + 512],
                                         start=(kc == 0), stop=(kc == 31),
                                         skip_group_check=True),
                                     inc="te" if i == n_ops else None)
                        seen_banks = set()
                        for (h01, vh, gr0) in combos:
                            i += 1
                            slot = 2 * h01 + vh
                            bank = slot // 2
                            st_flag = (kc == 0) and bank not in seen_banks
                            if kc == 0:
                                seen_banks.add(bank)
                            t = P.op("tensor", lambda e, kc=kc, b=b, s=s, h01=h01,
                                     slot=slot, st_flag=st_flag: e.matmul(
                                         ps_B[0:64, 256 * slot:256 * slot + 256],
                                         vp_sb[s][:, 130 * kc + 65 * h01:
                                                   130 * kc + 65 * h01 + 64],
                                         fvt_sb[slot][:, 256 * kc:256 * kc + 256],
                                         start=st_flag, stop=(kc == 31),
                                         skip_group_check=True),
                                     inc="te" if i == n_ops else None)
                        pv_te[kc] = t

                    emit_S(0)
                    emit_exp(0)
                    emit_S(1)
                    emit_exp(1)
                    if pending_epi[0] is not None:
                        pending_epi[0]()
                        pending_epi[0] = None
                    for kc in range(32):
                        if kc + 2 < 32:
                            emit_S(kc + 2)
                        emit_PV(kc)
                        if kc + 2 < 32:
                            emit_exp(kc + 2)
                    last_te = pv_te[31]
                    kv_free_te[s] = last_te
                    for (h01, vh, gr0) in combos:
                        fvt_free_te[2 * h01 + vh] = last_te

                    def epi(pr=pr, combos=combos, last_te=last_te):
                        nonlocal psO_free_ve, psB_free_ve
                        P.wait("vector", "te", last_te)
                        evac_ve = P.op("vector", lambda e: e.tensor_copy(
                            of_sb[0:65, :], ps_O[0:65, :]),
                            inc="ve" if not combos else None)
                        for ci, (h01, vh, gr0) in enumerate(combos):
                            slot = 2 * h01 + vh
                            evac_ve = P.op("vector", lambda e, slot=slot:
                                           e.tensor_copy(
                                               bvf_sb[0:64,
                                                      256 * slot:256 * slot + 256],
                                               ps_B[0:64,
                                                    256 * slot:256 * slot + 256]),
                                           inc="ve" if ci == len(combos) - 1
                                           else None)
                        P.wait("tensor", "ve", evac_ve)
                        n_tr = 8 + 2 * len(combos)
                        i = 0
                        for h01 in range(2):
                            for u in range(4):
                                i += 1
                                tr_te = P.op("tensor", lambda e, h01=h01, u=u:
                                             e.transpose(
                                                 ps_O[:, 512 * h01 + 65 * u:
                                                      512 * h01 + 65 * u + 65],
                                                 of_sb[0:65, 512 * h01 + 128 * u:
                                                       512 * h01 + 128 * u + 128],
                                                 idf_sb[0:65, 0:65]),
                                             inc="te" if i == n_tr else None)
                        for (h01, vh, gr0) in combos:
                            slot = 2 * h01 + vh
                            for uu in range(2):
                                i += 1
                                tr_te = P.op("tensor", lambda e, slot=slot, uu=uu:
                                             e.transpose(
                                                 ps_B[:, 256 * slot + 64 * uu:
                                                      256 * slot + 64 * uu + 64],
                                                 bvf_sb[0:64,
                                                        256 * slot + 128 * uu:
                                                        256 * slot + 128 * uu + 128],
                                                 idf_sb[0:64, 0:64]),
                                             inc="te" if i == n_tr else None)
                        P.wait("vector", "te", tr_te)
                        if combos:
                            for (h01, vh, gr0) in combos:
                                slot = 2 * h01 + vh
                                P.op("vector", lambda e, slot=slot: e.tensor_copy(
                                    bvt_sb[:, 256 * slot:256 * slot + 256],
                                    ps_B[:, 256 * slot:256 * slot + 256]), inc=None)
                        den = bass.AP(ps_O[:].tensor, 64, [[1024, 128], [512, 2],
                                                           [65, 4]])
                        rout = rcp_sb[:, 0:8].rearrange("p (a b) -> p a b", a=2)
                        cr = P.op("vector", lambda e, den=den, rout=rout:
                                  e.reciprocal(rout, den), inc="ve")
                        P.wait("vector", "ve", cr)
                        for u in range(4):
                            P.wait("vector", "ve", asm_free_ve[u])
                        for h01 in range(2):
                            a = 2 * pr + h01
                            for u in range(4):
                                vh = u // 2
                                aff = any(c[0] == h01 and c[1] == vh
                                          for c in combos)
                                if aff:
                                    slot = 2 * h01 + vh
                                    in1 = bvt_sb[:, 256 * slot + 64 * (u % 2):
                                                 256 * slot + 64 * (u % 2) + 64]
                                else:
                                    in1 = w_bc[:, 65 * a:65 * a + 64]
                                last = h01 == 1 and u == 3
                                stt_ve = P.op(
                                    "vector", lambda e, h01=h01, u=u, a=a, in1=in1:
                                    e.scalar_tensor_tensor(
                                        asm_sb[u][:, 64 * a:64 * a + 64],
                                        ps_O[:, 512 * h01 + 65 * u:
                                             512 * h01 + 65 * u + 64],
                                        rcp_sb[:, 4 * h01 + u:4 * h01 + u + 1],
                                        in1, ALU.mult, ALU.add),
                                    inc="ve" if last else None)
                        psO_free_ve = stt_ve
                        psB_free_ve = stt_ve

                    pending_epi[0] = epi

                if pending_epi[0] is not None:
                    pending_epi[0]()
                    pending_epi[0] = None

                asm_ready = P.val("ve")
                for u in range(4):
                    sl = stats[u % 2]
                    g_tt = 4 * qc + u
                    hp = u % 2
                    P.wait("sync", "ve", x_freeB[hp])
                    lxb_cnt[hp] = P.op("sync", lambda e, g=g_tt, hp=hp: e.dma_start(
                        out=x_sb[hp][:], in_=d_x[128 * g:128 * g + 128, :]),
                        inc=f"lxb{hp}", inc_n=16)
                    c_ln1 = plan_ln(P, asm_sb[u][:], t1_sb[:], E, 1e-3, sl,
                                    ready_val=asm_ready)
                    asm_free_ve[u] = c_ln1
                    P.wait("vector", "ve", c_ln1)
                    c_g = P.op("vector", lambda e: e.scalar_tensor_tensor(
                        t2_sb[:], t1_sb[:], 0.0, gb_sb[:], ALU.bypass, ALU.mult),
                        inc="ve")
                    P.wait("vector", "ve", c_g)
                    c_b = P.op("vector", lambda e: e.scalar_tensor_tensor(
                        t2_sb[:], t2_sb[:], 0.0, bb_sb[:], ALU.bypass, ALU.add),
                        inc="ve")
                    c_ln2 = plan_ln(P, t2_sb[:], t1_sb[:], E, 1e-5, sl,
                                    ready_val=c_b)
                    P.wait("vector", f"lxb{hp}", lxb_cnt[hp])
                    P.wait("vector", "ve", c_ln2)
                    P.wait("vector", f"sx2{hp}", x2_free[hp])
                    c_x2 = P.op("vector", lambda e, hp=hp: e.tensor_tensor(
                        x2_sb[hp][:], x_sb[hp][:], t1_sb[:], ALU.add), inc="ve")
                    x_freeB[hp] = c_x2
                    P.wait("gpsimd", "ve", c_x2)
                    x2_free[hp] = P.op("gpsimd", lambda e, g=g_tt, hp=hp: e.dma_start(
                        out=d_x2[128 * g:128 * g + 128, :], in_=x2_sb[hp][:]),
                        inc=f"sx2{hp}", inc_n=16)
            b_x2_totals = {"sx20": x2_free[0], "sx21": x2_free[1]}
            if phases <= 2:
                for s_name, tot in b_x2_totals.items():
                    P.wait("sync", s_name, tot)
            P.flush(block)
        if phases <= 2:
            sAB.close()
            return nc

        sAB.close()
        with ExitStack() as sC:
            sbC = lambda n_, s_, d_: sC.enter_context(nc.sbuf_tensor(n_, s_, d_))
            xn2T = sbC("xn2T", [128, 8 * TFF], BF16)
            a_sb = sbC("a_sb", [128, 32 * TFF], BF16)
            h_sb = sbC("h_sb", [128, 32 * TFF], BF16)
            fT_sb = h_sb
            w12_sb = [sbC(f"w12_{i}", [128, 8 * 128], BF16) for i in range(4)]
            hsl_sb = [sbC(f"hsl{i}", [128, 32 * 128], BF16) for i in range(2)]
            x2t_sb = [sbC(f"x2t{i}", [128, E], F32) for i in range(2)]
            xn2_sb = [sbC(f"xn2_{i}", [128, E], BF16) for i in range(2)]
            silu_sb = [sbC(f"silu{i}", [128, 512], F32) for i in range(2)]
            ftm_sb = h_sb[:, 8 * TFF:8 * TFF + 2048].bitcast(F32)
            ps_M = [sC.enter_context(nc.psum_tensor(f"psM{i}", [128, 512], F32))
                    for i in range(4)]
            ps_tr2 = sC.enter_context(nc.psum_tensor("ps_tr2", [128, 2048], BF16))

            P.phase_barrier()

            lx2_cnt = [0, 0]
            x2t_free = [0, 0]
            w12_cnt = [0, 0, 0, 0]
            w12_free_te = [0, 0, 0, 0]
            hsl_cnt = [0, 0]
            hsl_free_te = [0, 0]
            silu_free_ve = [0, 0]
            psM_free = [0, 0, 0, 0]
            psM_free_eng = ["ve", "ve", "ve", "ve"]
            trp2 = [0]
            tr2_evac_ve = [0, 0]
            xn2_free_ve = [0, 0]
            out_free = [0, 0]

            def emit_xn2_unit(ft, u8):
                g_tt = 8 * ft + u8
                hp = u8 % 2
                tp2 = trp2[0]
                P.wait("sync", "ve", x2t_free[hp])
                lx2_cnt[hp] = P.op("sync", lambda e, g=g_tt, hp=hp: e.dma_start(
                    out=x2t_sb[hp][:], in_=d_x2[128 * g:128 * g + 128, :]),
                    inc=f"lx2{hp}", inc_n=16)
                P.wait("vector", f"lx2{hp}", lx2_cnt[hp])
                c_ln = plan_ln(P, x2t_sb[hp][:], xn2_sb[hp][:], E, 1e-5,
                               stats[hp], ready_val=xn2_free_ve[hp])
                x2t_free[hp] = c_ln
                P.wait("tensor", "ve", c_ln)
                P.wait("tensor", "ve", tr2_evac_ve[tp2])
                for f in range(8):
                    te_tr = P.op("tensor", lambda e, hp=hp, f=f, tp2=tp2:
                                 e.transpose(
                                     ps_tr2[:, 1024 * tp2 + 128 * f:
                                            1024 * tp2 + 128 * f + 128],
                                     xn2_sb[hp][:, 128 * f:128 * f + 128],
                                     idb_sb[:]), inc="te" if f == 7 else None)
                P.wait("vector", "te", te_tr)
                for f in range(8):
                    ev = P.op("vector", lambda e, f=f, u8=u8, tp2=tp2:
                              e.tensor_copy(
                                  xn2T[:, TFF * f + 128 * u8:
                                       TFF * f + 128 * u8 + 128],
                                  ps_tr2[:, 1024 * tp2 + 128 * f:
                                         1024 * tp2 + 128 * f + 128]),
                              inc="ve" if f == 7 else None)
                tr2_evac_ve[tp2] = ev
                xn2_free_ve[hp] = ev
                trp2[0] ^= 1

            for u8 in range(8):
                emit_xn2_unit(0, u8)

            for ft in range(4):
                xn2T_ve = P.val("ve")
                P.wait("tensor", "ve", xn2T_ve)

                for mc in range(32):
                    for wi, d_w in enumerate([d_w1b, d_w2b]):
                        slot = 2 * (mc % 2) + wi
                        P.wait("sync", "te", w12_free_te[slot])
                        w12_cnt[slot] = P.op("sync", lambda e, d_w=d_w, mc=mc,
                                             slot=slot: e.dma_start(
                                                 out=w12_sb[slot][:],
                                                 in_=d_w[mc, :, :]),
                                             inc=f"lw12_{slot}", inc_n=16)
                    for wi in range(2):
                        P.wait("tensor", f"lw12_{2 * (mc % 2) + wi}",
                               w12_cnt[2 * (mc % 2) + wi])
                    for nt in range(2):
                        for wi in range(2):
                            b = 2 * (mc % 2) + wi
                            slot = 2 * (mc % 2) + wi
                            P.wait("tensor", psM_free_eng[b], psM_free[b])
                            for ki in range(8):
                                te_w12 = P.op(
                                    "tensor", lambda e, slot=slot, ki=ki, nt=nt, b=b:
                                    e.matmul(
                                        ps_M[b][:],
                                        w12_sb[slot][:, 128 * ki:128 * ki + 128],
                                        xn2T[:, TFF * ki + 512 * nt:
                                             TFF * ki + 512 * nt + 512],
                                        start=(ki == 0), stop=(ki == 7)),
                                    inc="te" if ki == 7 else None)
                        b1, b2 = 2 * (mc % 2), 2 * (mc % 2) + 1
                        hp2 = nt
                        P.wait("scalar", "te", te_w12)
                        P.wait("scalar", "ve", silu_free_ve[hp2])
                        c_si = P.op("scalar", lambda e, b1=b1, mc=mc, hp2=hp2:
                                    e.activation(silu_sb[hp2][:], ps_M[b1][:],
                                                 AF.Silu,
                                                 scale=s1_sb[:, mc:mc + 1]),
                                    inc="se")
                        psM_free[b1] = c_si
                        psM_free_eng[b1] = "se"
                        P.wait("vector", "se", c_si)
                        c_a = P.op("vector", lambda e, b2=b2, mc=mc, nt=nt, hp2=hp2:
                                   e.scalar_tensor_tensor(
                                       a_sb[:, TFF * mc + 512 * nt:
                                            TFF * mc + 512 * nt + 512],
                                       ps_M[b2][:], s2_sb[:, mc:mc + 1],
                                       silu_sb[hp2][:], ALU.mult, ALU.mult),
                                   inc="ve")
                        silu_free_ve[hp2] = c_a
                        psM_free[b2] = c_a
                        psM_free_eng[b2] = "ve"
                        w12_free_te[b1] = te_w12
                        w12_free_te[b2] = te_w12

                def h_stage(d_ht, s_ap, src_sb, dst_sb, sname, interleave_xn2):
                    te_h = 0
                    for mc in range(32):
                        if interleave_xn2 and mc < 8:
                            emit_xn2_unit(ft + 1, mc)
                        slot = mc % 2
                        P.wait("sync", "te", hsl_free_te[slot])
                        hsl_cnt[slot] = P.op("sync", lambda e, d_ht=d_ht, mc=mc,
                                             slot=slot: e.dma_start(
                                                 out=hsl_sb[slot][:],
                                                 in_=d_ht[mc, :, :]),
                                             inc=f"{sname}{slot}", inc_n=16)
                        P.wait("tensor", f"{sname}{slot}", hsl_cnt[slot])
                        for nt in range(2):
                            b = 2 * slot + nt
                            P.wait("tensor", psM_free_eng[b], psM_free[b])
                            for ki in range(32):
                                te_h = P.op(
                                    "tensor", lambda e, slot=slot, ki=ki, nt=nt, b=b,
                                    src_sb=src_sb: e.matmul(
                                        ps_M[b][:],
                                        hsl_sb[slot][:, 128 * ki:128 * ki + 128],
                                        src_sb[:, TFF * ki + 512 * nt:
                                               TFF * ki + 512 * nt + 512],
                                        start=(ki == 0), stop=(ki == 31)),
                                    inc="te" if ki == 31 else None)
                            P.wait("scalar", "te", te_h)
                            c_g2 = P.op("scalar", lambda e, b=b, mc=mc, nt=nt,
                                        dst_sb=dst_sb, s_ap=s_ap: e.activation(
                                            dst_sb[:, TFF * mc + 512 * nt:
                                                   TFF * mc + 512 * nt + 512],
                                            ps_M[b][:], AF.Gelu,
                                            scale=s_ap[:, mc:mc + 1]), inc="se")
                            psM_free[b] = c_g2
                            psM_free_eng[b] = "se"
                        hsl_free_te[slot] = te_h
                    return P.val("se")

                a_ve = P.val("ve")
                P.wait("tensor", "ve", a_ve)
                se_h1 = h_stage(d_h1b, sh1_sb, a_sb, h_sb, "lh1_", ft < 3)
                P.wait("tensor", "se", se_h1)
                se_h2 = h_stage(d_h2b, sh2_sb, h_sb, a_sb, "lh2_", False)
                P.wait("tensor", "se", se_h2)

                for oc in range(8):
                    slot = oc % 2
                    P.wait("sync", "te", hsl_free_te[slot])
                    hsl_cnt[slot] = P.op("sync", lambda e, oc=oc, slot=slot:
                                         e.dma_start(out=hsl_sb[slot][:],
                                                     in_=d_w3b[oc, :, :]),
                                         inc=f"lw3_{slot}", inc_n=16)
                    P.wait("tensor", f"lw3_{slot}", hsl_cnt[slot])
                    for nt in range(2):
                        b = 2 * slot + nt
                        P.wait("tensor", psM_free_eng[b], psM_free[b])
                        for ki in range(32):
                            te_w3 = P.op(
                                "tensor", lambda e, slot=slot, ki=ki, nt=nt, b=b:
                                e.matmul(ps_M[b][:],
                                         hsl_sb[slot][:, 128 * ki:128 * ki + 128],
                                         a_sb[:, TFF * ki + 512 * nt:
                                              TFF * ki + 512 * nt + 512],
                                         start=(ki == 0), stop=(ki == 31)),
                                inc="te" if ki == 31 else None)
                        P.wait("vector", "te", te_w3)
                        c_f = P.op("vector", lambda e, b=b, oc=oc, nt=nt:
                                   e.tensor_scalar(
                                       fT_sb[:, TFF * oc + 512 * nt:
                                             TFF * oc + 512 * nt + 512],
                                       ps_M[b][:], s3_sb[:, oc:oc + 1], None,
                                       ALU.mult), inc="ve")
                        psM_free[b] = c_f
                        psM_free_eng[b] = "ve"
                    hsl_free_te[slot] = te_w3
                fT_ve = P.val("ve")
                P.wait("tensor", "ve", fT_ve)

                for u8 in range(8):
                    g_tt = 8 * ft + u8
                    hp = u8 % 2
                    tp2 = trp2[0]
                    P.wait("tensor", "ve", tr2_evac_ve[tp2])
                    for f in range(8):
                        te_tr = P.op("tensor", lambda e, f=f, u8=u8, tp2=tp2:
                                     e.transpose(
                                         ps_tr2[:, 1024 * tp2 + 128 * f:
                                                1024 * tp2 + 128 * f + 128],
                                         fT_sb[:, TFF * f + 128 * u8:
                                               TFF * f + 128 * u8 + 128],
                                         idb_sb[:]), inc="te" if f == 7 else None)
                    P.wait("vector", "te", te_tr)
                    c_ftm = P.op("vector", lambda e, tp2=tp2: e.tensor_copy(
                        ftm_sb, ps_tr2[:, 1024 * tp2:1024 * tp2 + 1024]),
                        inc="ve")
                    tr2_evac_ve[tp2] = c_ftm
                    trp2[0] ^= 1
                    P.wait("sync", "ve", x2t_free[hp])
                    lx2_cnt[hp] = P.op("sync", lambda e, g=g_tt, hp=hp: e.dma_start(
                        out=x2t_sb[hp][:], in_=d_x2[128 * g:128 * g + 128, :]),
                        inc=f"lx2{hp}", inc_n=16)
                    c_ln = plan_ln(P, ftm_sb, t1_sb[:], E, 1e-5, stats[2 + hp],
                                   ready_val=c_ftm)
                    P.wait("vector", f"lx2{hp}", lx2_cnt[hp])
                    P.wait("vector", "ve", c_ln)
                    c_o = P.op("vector", lambda e, hp=hp: e.tensor_tensor(
                        x2t_sb[hp][:], x2t_sb[hp][:], t1_sb[:], ALU.add), inc="ve")
                    x2t_free[hp] = c_o
                    P.wait("sync", "ve", c_o)
                    out_free[hp] = P.op("sync", lambda e, g=g_tt, hp=hp: e.dma_start(
                        out=d_out[128 * g:128 * g + 128, :], in_=x2t_sb[hp][:]),
                        inc=f"sout{hp}", inc_n=16)
                    P.wait("sync", f"sout{hp}", out_free[hp])

            P.wait("sync", "sout0", out_free[0])
            P.wait("sync", "sout1", out_free[1])
            P.flush(block)

    return nc


_PERM = None


def _perm():
    global _PERM
    if _PERM is None:
        p = np.arange(NT)
        _PERM = 16 * (p % 256) + p // 256
    return _PERM


def _prep_core_inputs(xb, d):
    perm = _perm()
    rel = d["L0_attn_rel_table"]
    jj = perm // 16
    cc = perm % 16
    gr = np.arange(2048)
    idx = np.minimum(2047 - gr[None, :] + jj[:, None], 2046)
    fvt = np.ascontiguousarray(rel[idx, cc[:, None]])
    cmat = np.ascontiguousarray(
        np.broadcast_to(np.repeat(rel[2046], 2)[None, :], (128, 32)))
    return {
        "x": np.ascontiguousarray(xb[perm]),
        "wq": d["L0_attn_Wq"], "wk": d["L0_attn_Wk"], "wv": d["L0_attn_Wv"],
        "fvt": fvt, "cmat": cmat,
        "gamma_b": np.ascontiguousarray(
            np.broadcast_to(d["L0_attn_gamma"][None, :], (128, E))),
        "beta_b": np.ascontiguousarray(
            np.broadcast_to(d["L0_attn_beta"][None, :], (128, E))),
        "w1t": np.ascontiguousarray(d["L0_ffn_w1_w"].T),
        "w2t": np.ascontiguousarray(d["L0_ffn_w2_w"].T),
        "h1t": np.ascontiguousarray(d["L0_ffn_h1_w"].T),
        "h2t": np.ascontiguousarray(d["L0_ffn_h2_w"].T),
        "w3t": np.ascontiguousarray(d["L0_ffn_w3_w"].T),
        "s1m": np.ascontiguousarray(d["L0_ffn_w1_s"].reshape(32, 128).T),
        "s2m": np.ascontiguousarray(d["L0_ffn_w2_s"].reshape(32, 128).T),
        "sh1m": np.ascontiguousarray(d["L0_ffn_h1_s"].reshape(32, 128).T),
        "sh2m": np.ascontiguousarray(d["L0_ffn_h2_s"].reshape(32, 128).T),
        "s3m": np.ascontiguousarray(d["L0_ffn_w3_s"].reshape(8, 128).T),
        "id_bf": np.eye(128, dtype=np.float32).astype(ml_dtypes.bfloat16),
        "id_f32": np.eye(128, dtype=np.float32),
    }


def kernel(**inputs):
    x = np.asarray(inputs["L0_attn_x"], dtype=np.float32)
    d = {k: np.asarray(v, dtype=np.float32) for k, v in inputs.items()}
    nc = build_program()
    in_maps = [_prep_core_inputs(x[b], d) for b in range(8)]
    res = run_bass_kernel_spmd(nc, in_maps, core_ids=list(range(8)))
    perm = _perm()
    out = np.empty((8, NT, E), dtype=np.float32)
    for b in range(8):
        out[b][perm] = res.results[b]["out"]
    return out


# version 10
# speedup vs baseline: 1.3977x; 1.0976x over previous
import numpy as np
import ml_dtypes
import concourse.bass as bass
import concourse.mybir as mybir
from concourse.bass_utils import run_bass_kernel_spmd
from contextlib import ExitStack

F32 = mybir.dt.float32
BF16 = mybir.dt.bfloat16
AF = mybir.ActivationFunctionType
ALU = mybir.AluOpType
AX = mybir.AxisListType

NT = 4096
E = 1024
H = 16
HID = 4096
NQC = 8
TFF = 1024

ENGINES = ("sync", "scalar", "vector", "tensor", "gpsimd")


class Plan:

    def __init__(self, nc, stack):
        self.nc = nc
        self.stack = stack
        self.thunks = {e: [] for e in ENGINES}
        self.sems = {}
        self.counts = {}

    def sem(self, name):
        if name not in self.sems:
            self.sems[name] = self.stack.enter_context(self.nc.semaphore(name))
            self.counts[name] = 0
        return self.sems[name]

    def op(self, eng, fn, inc=None, inc_n=1):
        if inc is not None:
            self.sem(inc)
            self.counts[inc] += inc_n
            target = self.counts[inc]
            sem = self.sems[inc]

            def thunk(e, fn=fn, sem=sem, inc_n=inc_n):
                fn(e).then_inc(sem, inc_n)
        else:
            target = None

            def thunk(e, fn=fn):
                fn(e)
        self.thunks[eng].append(thunk)
        return target

    def wait(self, eng, name, val):
        if val is None or val <= 0:
            return
        sem = self.sem(name)
        self.thunks[eng].append(lambda e, s=sem, v=val: e.wait_ge(s, v))

    def val(self, name):
        self.sem(name)
        return self.counts[name]

    def phase_barrier(self, exclude=()):
        snap = [(n, v) for n, v in self.counts.items()
                if v > 0 and n not in exclude]
        for eng in ENGINES:
            for n, v in snap:
                self.wait(eng, n, v)

    def flush(self, block):
        for eng in ENGINES:
            if not self.thunks[eng]:
                continue
            fns = self.thunks[eng]
            self.thunks[eng] = []
            getattr(block, eng)(lambda e, fns=fns: [f(e) for f in fns])


def plan_ln(P, x_ap, out_ap, n, eps, sl, ready_val=0):
    P.wait("vector", "ve", ready_val)
    c1 = P.op("vector", lambda e: e.reduce_sum(sl["rs"], x_ap, axis=AX.X), inc="ve")
    P.wait("vector", "ve", c1)
    c2 = P.op("vector", lambda e: e.tensor_scalar_mul(sl["nmu"], sl["rs"], -1.0 / n),
              inc="ve")
    P.wait("scalar", "ve", c2)
    c3 = P.op("scalar", lambda e: e.activation(sl["sq_scr"], x_ap, AF.Square,
                                               bias=sl["nmu"], scale=1.0,
                                               accum_out=sl["var"]), inc="se")
    P.wait("vector", "se", c3)
    c4 = P.op("vector", lambda e: e.tensor_scalar(sl["var"], sl["var"], 1.0 / n, eps,
                                                  ALU.mult, ALU.add), inc="ve")
    P.wait("scalar", "ve", c4)
    c5 = P.op("scalar", lambda e: e.activation(sl["sd"], sl["var"], AF.Sqrt), inc="se")
    P.wait("vector", "se", c5)
    c6 = P.op("vector", lambda e: e.reciprocal(sl["rstd"], sl["sd"]), inc="ve")
    P.wait("vector", "ve", c6)
    c7 = P.op("vector", lambda e: e.tensor_tensor(sl["nmur"], sl["nmu"], sl["rstd"],
                                                  ALU.mult), inc="ve")
    P.wait("vector", "ve", c7)
    c8 = P.op("vector", lambda e: e.tensor_scalar(out_ap, x_ap, sl["rstd"], sl["nmur"],
                                                  ALU.mult, ALU.add), inc="ve")
    return c8


def build_program(debug=False, phases=3, nqc=NQC, sub="all"):
    nc = bass.Bass("TRN2", target_bir_lowering=False)

    d_x = nc.dram_tensor("x", [NT, E], F32, kind="ExternalInput").ap()
    d_wq = nc.dram_tensor("wq", [E, E], F32, kind="ExternalInput").ap()
    d_wk = nc.dram_tensor("wk", [E, E], F32, kind="ExternalInput").ap()
    d_wv = nc.dram_tensor("wv", [E, E], F32, kind="ExternalInput").ap()
    d_fvt = nc.dram_tensor("fvt", [NT, 2048], F32, kind="ExternalInput").ap()
    d_cmat = nc.dram_tensor("cmat", [128, 32], F32, kind="ExternalInput").ap()
    d_gb = nc.dram_tensor("gamma_b", [128, E], F32, kind="ExternalInput").ap()
    d_bb = nc.dram_tensor("beta_b", [128, E], F32, kind="ExternalInput").ap()
    d_w1t = nc.dram_tensor("w1t", [E, HID], F32, kind="ExternalInput").ap()
    d_w2t = nc.dram_tensor("w2t", [E, HID], F32, kind="ExternalInput").ap()
    d_h1t = nc.dram_tensor("h1t", [HID, HID], F32, kind="ExternalInput").ap()
    d_h2t = nc.dram_tensor("h2t", [HID, HID], F32, kind="ExternalInput").ap()
    d_w3t = nc.dram_tensor("w3t", [HID, E], F32, kind="ExternalInput").ap()
    d_s1 = nc.dram_tensor("s1m", [128, 32], F32, kind="ExternalInput").ap()
    d_s2 = nc.dram_tensor("s2m", [128, 32], F32, kind="ExternalInput").ap()
    d_sh1 = nc.dram_tensor("sh1m", [128, 32], F32, kind="ExternalInput").ap()
    d_sh2 = nc.dram_tensor("sh2m", [128, 32], F32, kind="ExternalInput").ap()
    d_s3 = nc.dram_tensor("s3m", [128, 8], F32, kind="ExternalInput").ap()
    d_idb = nc.dram_tensor("id_bf", [128, 128], BF16, kind="ExternalInput").ap()
    d_idf = nc.dram_tensor("id_f32", [128, 128], F32, kind="ExternalInput").ap()
    d_out = nc.dram_tensor("out", [NT, E], F32, kind="ExternalOutput").ap()

    sk = "ExternalOutput" if debug else "Internal"
    d_qs = nc.dram_tensor("qs", [NQC, E, 512], BF16, kind=sk).ap()
    d_ks = nc.dram_tensor("ks", [NQC, E, 512], BF16, kind=sk).ap()
    d_vs = nc.dram_tensor("vs", [NT, H * 65], BF16, kind=sk).ap()
    d_x2 = nc.dram_tensor("x2s", [NT, E], F32, kind=sk).ap()
    d_fvtb = nc.dram_tensor("fvtb", [NT, 2048], BF16, kind=sk).ap()
    d_w1b = nc.dram_tensor("w1b", [32, 128, 1024], BF16, kind=sk).ap()
    d_w2b = nc.dram_tensor("w2b", [32, 128, 1024], BF16, kind="Internal").ap()
    d_h1b = nc.dram_tensor("h1b", [32, 128, 4096], BF16, kind="Internal").ap()
    d_h2b = nc.dram_tensor("h2b", [32, 128, 4096], BF16, kind="Internal").ap()
    d_w3b = nc.dram_tensor("w3b", [8, 128, 4096], BF16, kind="Internal").ap()

    stack = ExitStack()
    with stack:
        P = Plan(nc, stack)
        sb = lambda name, shape, dt: stack.enter_context(nc.sbuf_tensor(name, shape, dt))

        gb_sb = sb("gb", [128, E], F32)
        bb_sb = sb("bb", [128, E], F32)
        c_f32 = sb("c_f32", [128, 32], F32)
        c_bf = sb("c_bf", [128, 32], BF16)
        s1_sb = sb("s1", [128, 32], F32)
        s2_sb = sb("s2", [128, 32], F32)
        sh1_sb = sb("sh1", [128, 32], F32)
        sh2_sb = sb("sh2", [128, 32], F32)
        s3_sb = sb("s3", [128, 8], F32)
        idb_sb = sb("idb", [128, 128], BF16)
        idf_sb = sb("idf", [128, 128], F32)
        ones_f = sb("ones_f", [1, 128], F32)
        w_bc = sb("w_bc", [128, H * 65], F32)
        sq_scr = sb("sq_scr", [128, E], F32)
        t1_sb = sb("t1", [128, E], F32)
        st_t = sb("stats", [128, 32], F32)
        stats = []
        for s in range(4):
            stats.append({"rs": st_t[:, 8 * s:8 * s + 1],
                          "nmu": st_t[:, 8 * s + 1:8 * s + 2],
                          "var": st_t[:, 8 * s + 2:8 * s + 3],
                          "sd": st_t[:, 8 * s + 3:8 * s + 4],
                          "rstd": st_t[:, 8 * s + 4:8 * s + 5],
                          "nmur": st_t[:, 8 * s + 5:8 * s + 6],
                          "sq_scr": sq_scr[:]})

        sAB = ExitStack()
        pcst = [sAB.enter_context(nc.sbuf_tensor(f"pcst{i}", [128, 4096], BF16))
                for i in range(2)]

        block = stack.enter_context(nc.Block())

        with ExitStack() as sA:
            sbA = lambda n_, s_, d_: sA.enter_context(nc.sbuf_tensor(n_, s_, d_))
            wq_sb = sbA("wq_sb", [128, 8 * E], BF16)
            wk_sb = sbA("wk_sb", [128, 8 * E], BF16)
            wv_sb = sbA("wv_sb", [128, 8 * E], BF16)
            xn1_sb = [sbA(f"xn1_{i}", [128, E], BF16) for i in range(2)]
            xnT_sb = sbA("xnT", [128, 8 * 512], BF16)
            qe_sb = [sbA(f"qe{i}", [128, 512], BF16) for i in range(2)]
            ke_sb = [sbA(f"ke{i}", [128, 512], BF16) for i in range(2)]
            va_sb = [sbA(f"va{i}", [128, H * 65], BF16) for i in range(2)]
            w_row = sbA("w_row", [1, H * 65], F32)
            w_hi = sbA("w_hi", [1, H * 65], BF16)
            w_rem = sbA("w_rem", [1, H * 65], F32)
            w_lo = sbA("w_lo", [1, H * 65], BF16)
            ones_b = sbA("ones_b", [1, 128], BF16)
            x_sb = [sbA(f"xA{i}", [128, E], F32) for i in range(2)]
            ps_mm = sA.enter_context(nc.psum_tensor("ps_mm", [128, 1024], F32))
            ps_tr = sA.enter_context(nc.psum_tensor("ps_tr", [128, 2048], BF16))
            ps_w = sA.enter_context(nc.psum_tensor("ps_w", [1, 1536], F32))

            for dd, ss in [(d_gb, gb_sb), (d_bb, bb_sb), (d_cmat, c_f32),
                           (d_s1, s1_sb), (d_s2, s2_sb), (d_sh1, sh1_sb),
                           (d_sh2, sh2_sb), (d_s3, s3_sb), (d_idb, idb_sb),
                           (d_idf, idf_sb)]:
                l_set = P.op("sync", lambda e, dd=dd, ss=ss:
                             e.dma_start(out=ss[:], in_=dd[:]), inc="l_set", inc_n=16)
            for dd, ss in [(d_wq, wq_sb), (d_wk, wk_sb), (d_wv, wv_sb)]:
                src = dd.rearrange("(ki p) o -> p ki o", p=128)
                dst = ss[:].rearrange("p (ki o) -> p ki o", ki=8)
                l_wqkv = P.op("gpsimd", lambda e, s1=src, d1=dst:
                              e.dma_start(out=d1, in_=s1), inc="l_wqkv", inc_n=16)
            pc_ld = [0, 0]
            pc_st = [0, 0]
            pc_idx = 0
            for c in range(32):
                sl = pc_idx % 2
                pc_idx += 1
                P.wait("gpsimd", f"pcs{sl}", pc_st[sl])
                pc_ld[sl] = P.op("gpsimd", lambda e, c=c, sl=sl: e.dma_start(
                    out=pcst[sl][:, 0:2048],
                    in_=d_fvt[128 * c:128 * c + 128, :]), inc=f"pcl{sl}", inc_n=16)
                P.wait("gpsimd", f"pcl{sl}", pc_ld[sl])
                pc_st[sl] = P.op("gpsimd", lambda e, c=c, sl=sl: e.dma_start(
                    out=d_fvtb[128 * c:128 * c + 128, :],
                    in_=pcst[sl][:, 0:2048]), inc=f"pcs{sl}", inc_n=16)
            fvt_pc_done = (pc_st[0], pc_st[1])
            for dsrc, ddst, nsl, nki in [(d_w1t, d_w1b, 32, 8),
                                         (d_w2t, d_w2b, 32, 8),
                                         (d_h1t, d_h1b, 32, 32),
                                         (d_h2t, d_h2b, 32, 32),
                                         (d_w3t, d_w3b, 8, 32)]:
                wid = nki * 128
                for c in range(nsl):
                    sl = pc_idx % 2
                    pc_idx += 1
                    srcap = dsrc[:, 128 * c:128 * c + 128].rearrange(
                        "(ki p) o -> p ki o", p=128)
                    dstap = pcst[sl][:, 0:wid].rearrange(
                        "p (ki o) -> p ki o", ki=nki)
                    P.wait("gpsimd", f"pcs{sl}", pc_st[sl])
                    pc_ld[sl] = P.op("gpsimd", lambda e, srcap=srcap, dstap=dstap,
                                     sl=sl: e.dma_start(out=dstap, in_=srcap),
                                     inc=f"pcl{sl}", inc_n=16)
                    P.wait("gpsimd", f"pcl{sl}", pc_ld[sl])
                    pc_st[sl] = P.op("gpsimd", lambda e, ddst=ddst, c=c, sl=sl,
                                     wid=wid: e.dma_start(
                                         out=ddst[c, :, :],
                                         in_=pcst[sl][:, 0:wid]),
                                     inc=f"pcs{sl}", inc_n=16)
            P.wait("vector", "l_set", l_set)
            P.op("vector", lambda e: e.tensor_copy(c_bf[:], c_f32[:]), inc="ve")
            P.op("vector", lambda e: e.memset(ones_f[:], 1.0), inc="ve")
            setup_ve = P.op("vector", lambda e: e.memset(ones_b[:], 1.0), inc="ve")

            lx_cnt = [0, 0]
            xn_free_ve = [0, 0]
            x_freeA = [0, 0]
            stq_free = [0, 0]
            stk_free = [0, 0]
            stv_free = [0, 0]
            trp = 0
            tr_evac_ve = [0, 0]
            mm_free_ve = [0, 0]
            te_w = 0
            last_g = 4 * nqc - 1
            for qc in range(nqc):
                for t in range(4):
                    g_tt = 4 * qc + t
                    s = g_tt % 2
                    P.wait("sync", "ve", x_freeA[s])
                    lx_cnt[s] = P.op("sync", lambda e, g=g_tt, s=s: e.dma_start(
                        out=x_sb[s][:], in_=d_x[128 * g:128 * g + 128, :]),
                        inc=f"lx{s}", inc_n=16)
                    P.wait("vector", f"lx{s}", lx_cnt[s])
                    ln_ve = plan_ln(P, x_sb[s][:], xn1_sb[s][:], E, 1e-5, stats[s],
                                    ready_val=xn_free_ve[s])
                    x_freeA[s] = ln_ve
                    P.wait("tensor", "ve", ln_ve)
                    P.wait("tensor", "ve", tr_evac_ve[trp])
                    for f in range(8):
                        te_tr = P.op("tensor", lambda e, s=s, f=f, trp=trp: e.transpose(
                            ps_tr[:, 1024 * trp + 128 * f:1024 * trp + 128 * f + 128],
                            xn1_sb[s][:, 128 * f:128 * f + 128], idb_sb[:]),
                            inc="te" if f == 7 else None)
                    P.wait("vector", "te", te_tr)
                    for f in range(8):
                        ev = P.op("vector", lambda e, f=f, t=t, trp=trp: e.tensor_copy(
                            xnT_sb[:, 512 * f + 128 * t:512 * f + 128 * t + 128],
                            ps_tr[:, 1024 * trp + 128 * f:1024 * trp + 128 * f + 128]),
                            inc="ve" if f == 7 else None)
                    tr_evac_ve[trp] = ev
                    xn_free_ve[s] = ev
                    trp ^= 1
                if sub not in ("qk", "all"):
                    continue
                P.wait("tensor", "l_wqkv", l_wqkv)
                P.wait("tensor", "ve", setup_ve)
                xnT_ready = P.val("ve")
                P.wait("tensor", "ve", xnT_ready)
                for w_sb, ev_sb, ev_free, d_dst, sem_st in [
                        (wq_sb, qe_sb, stq_free, d_qs, "stq"),
                        (wk_sb, ke_sb, stk_free, d_ks, "stk")]:
                    for f in range(8):
                        hp = f % 2
                        P.wait("tensor", "ve", mm_free_ve[hp])
                        for ki in range(8):
                            te_mm = P.op(
                                "tensor", lambda e, w_sb=w_sb, f=f, ki=ki, hp=hp:
                                e.matmul(ps_mm[:, 512 * hp:512 * hp + 512],
                                         w_sb[:, E * ki + 128 * f:E * ki + 128 * f + 128],
                                         xnT_sb[:, 512 * ki:512 * ki + 512],
                                         start=(ki == 0), stop=(ki == 7)),
                                inc="te" if ki == 7 else None)
                        P.wait("vector", "te", te_mm)
                        P.wait("vector", f"{sem_st}{hp}", ev_free[hp])
                        cv = P.op("vector", lambda e, ev_sb=ev_sb, hp=hp: e.tensor_copy(
                            ev_sb[hp][:], ps_mm[:, 512 * hp:512 * hp + 512]), inc="ve")
                        mm_free_ve[hp] = cv
                        P.wait("sync", "ve", cv)
                        ev_free[hp] = P.op(
                            "sync", lambda e, ev_sb=ev_sb, hp=hp, qc=qc, f=f,
                            d_dst=d_dst: e.dma_start(
                                out=d_dst[qc, 128 * f:128 * f + 128, :],
                                in_=ev_sb[hp][:]),
                            inc=f"{sem_st}{hp}", inc_n=16)
                if sub not in ("v", "all", "vmm", "vst"):
                    continue
                for t in range(4):
                    g_tt = 4 * qc + t
                    hp = t % 2
                    P.wait("vector", f"stv{hp}", stv_free[hp])
                    mv = P.op("vector", lambda e, hp=hp: e.memset(va_sb[hp][:], 1.0),
                              inc="ve")
                    for nf in range(2):
                        P.wait("tensor", "ve", mm_free_ve[nf])
                        for ki in range(8):
                            te_mm = P.op("tensor", lambda e, t=t, ki=ki, nf=nf:
                                         e.matmul(
                                             ps_mm[:, 512 * nf:512 * nf + 512],
                                             xnT_sb[:, 512 * ki + 128 * t:
                                                    512 * ki + 128 * t + 128],
                                             wv_sb[:, E * ki + 512 * nf:
                                                   E * ki + 512 * nf + 512],
                                             start=(ki == 0), stop=(ki == 7)),
                                         inc="te" if ki == 7 else None)
                        P.wait("vector", "te", te_mm)
                        P.wait("vector", "ve", mv)
                        dst = va_sb[hp][:].rearrange("p (h c) -> p h c", c=65)[
                            :, 8 * nf:8 * nf + 8, 0:64]
                        src = ps_mm[:, 512 * nf:512 * nf + 512].rearrange(
                            "p (h c) -> p h c", c=64)
                        cv = P.op("vector", lambda e, dst=dst, src=src:
                                  e.tensor_copy(dst, src), inc="ve")
                        mm_free_ve[nf] = cv
                    va_ve = P.val("ve")
                    if sub != "vmm":
                        P.wait("sync", "ve", va_ve)
                        stv_free[hp] = P.op("sync", lambda e, hp=hp, g=g_tt: e.dma_start(
                            out=d_vs[128 * g:128 * g + 128, :], in_=va_sb[hp][:]),
                            inc=f"stv{hp}", inc_n=16)
                    if sub in ("vmm", "vst"):
                        continue
                    P.wait("tensor", "ve", va_ve)
                    for ch, (c0, c1) in enumerate([(0, 512), (512, 1024),
                                                   (1024, 1040)]):
                        te_w = P.op("tensor", lambda e, g=g_tt, hp=hp, c0=c0, c1=c1:
                                    e.matmul(ps_w[0:1, c0:c1], c_bf[:, g:g + 1],
                                             va_sb[hp][:, c0:c1],
                                             start=(g == 0), stop=(g == last_g),
                                             skip_group_check=True),
                                    inc="te" if (ch == 2 and g_tt == last_g) else None)
            w_ready_ve = 0
            if sub in ("v", "all"):
                P.wait("vector", "te", te_w)
                cw = P.op("vector", lambda e: e.tensor_copy(
                    w_row[:], ps_w[0:1, 0:H * 65]), inc="ve")
                P.wait("vector", "ve", cw)
                c_hi = P.op("vector", lambda e: e.tensor_copy(w_hi[:], w_row[:]),
                            inc="ve")
                P.wait("vector", "ve", c_hi)
                c_rm = P.op("vector", lambda e: e.tensor_tensor(
                    w_rem[:], w_row[:], w_hi[:], ALU.subtract), inc="ve")
                P.wait("vector", "ve", c_rm)
                c_lo = P.op("vector", lambda e: e.tensor_copy(w_lo[:], w_rem[:]),
                            inc="ve")
                P.wait("tensor", "ve", c_lo)
                P.wait("tensor", "ve", mm_free_ve[0])
                P.wait("tensor", "ve", mm_free_ve[1])
                for c0, c1 in [(0, 512), (512, 1024), (1024, 1040)]:
                    P.op("tensor", lambda e, c0=c0, c1=c1: e.matmul(
                        ps_mm[:, 0:c1 - c0], ones_b[:], w_hi[0:1, c0:c1],
                        start=True, stop=False))
                    tw = P.op("tensor", lambda e, c0=c0, c1=c1: e.matmul(
                        ps_mm[:, 0:c1 - c0], ones_b[:], w_lo[0:1, c0:c1],
                        start=False, stop=True), inc="te")
                    P.wait("vector", "te", tw)
                    cbc = P.op("vector", lambda e, c0=c0, c1=c1: e.tensor_copy(
                        w_bc[:, c0:c1], ps_mm[:, 0:c1 - c0]), inc="ve")
                    P.wait("tensor", "ve", cbc)
                    w_ready_ve = cbc
            a_store_totals = {"stq0": stq_free[0], "stq1": stq_free[1],
                              "stk0": stk_free[0], "stk1": stk_free[1],
                              "stv0": stv_free[0], "stv1": stv_free[1]}
            if phases <= 1:
                for s_name, tot in a_store_totals.items():
                    P.wait("sync", s_name, tot)
            P.flush(block)
        if phases <= 1:
            sAB.close()
            return nc

        with ExitStack() as sB:
            sbB = lambda n_, s_, d_: sB.enter_context(nc.sbuf_tensor(n_, s_, d_))
            kp_sb = [sbB(f"kp{i}", [128, NT], BF16) for i in range(2)]
            vp_sb = [sbB(f"vp{i}", [128, 32 * 130], BF16) for i in range(2)]
            qp_sb = [sbB(f"qp{i}", [128, 512], BF16) for i in range(2)]
            fvt_sb = [sbB(f"fvt{i}", [128, 32 * 256], BF16) for i in range(4)]
            p_sb = [sbB(f"p{i}", [128, 1024], BF16) for i in range(2)]
            of_sb = sbB("of", [128, 1024], F32)
            bvf_sb = sbB("bvf", [128, 1024], F32)
            bvt_sb = sbB("bvt", [128, 1024], F32)
            asm_sb = [[sbB(f"asm{b_}_{i}", [128, E], F32) for i in range(4)]
                      for b_ in range(2)]
            x2_sb = [sbB(f"x2_{i}", [128, E], F32) for i in range(2)]
            t2_sb = sbB("t2", [128, E], F32)
            x_sb = [sbB(f"xB{i}", [128, E], F32) for i in range(2)]
            rcp_sb = sbB("rcp", [128, 8], F32)
            ps_S = [sB.enter_context(nc.psum_tensor(f"psS{i}", [128, 1024], F32))
                    for i in range(2)]
            ps_O = sB.enter_context(nc.psum_tensor("psO", [128, 1024], F32))
            ps_B = sB.enter_context(nc.psum_tensor("psB", [128, 1024], F32))

            P.phase_barrier(exclude=("pcl0", "pcl1", "pcs0", "pcs1"))
            P.wait("sync", "pcs0", fvt_pc_done[0])
            P.wait("sync", "pcs1", fvt_pc_done[1])

            kv_cnt = [0, 0]
            fvt_cnt = [0, 0, 0, 0]
            kv_free_te = [0, 0]
            fvt_free_te = [0, 0, 0, 0]
            psS_free_se = [0, 0]
            psO_free_ve = 0
            psB_free_ve = 0
            asm_free_ve = [[0, 0, 0, 0], [0, 0, 0, 0]]
            x2_free = [0, 0]
            x_freeB = [0, 0]
            lxb_cnt = [0, 0]
            pending_epi = [None]
            pending_qcln = []
            asm_ready = [0, 0]

            for qc in range(NQC):
                for pr in range(8):
                    s = pr % 2
                    combos = []
                    for h01 in range(2):
                        for vh in range(2):
                            a = 2 * pr + h01
                            v = 2 * qc + vh
                            if a - v >= 8:
                                combos.append((h01, vh, 256 * (a - v) - 2048))
                    P.wait("sync", "te", kv_free_te[s])
                    src = d_ks[:, 128 * pr:128 * pr + 128, :].rearrange(
                        "b p c -> p b c")
                    dst = kp_sb[s][:].rearrange("p (b c) -> p b c", b=8)
                    P.op("sync", lambda e, s1=src, d1=dst: e.dma_start(out=d1, in_=s1),
                         inc=f"lkv{s}", inc_n=16)
                    src = d_vs[:, 130 * pr:130 * pr + 130].rearrange(
                        "(kc p) c -> p kc c", p=128)
                    dst = vp_sb[s][:].rearrange("p (kc c) -> p kc c", kc=32)
                    P.op("sync", lambda e, s1=src, d1=dst: e.dma_start(out=d1, in_=s1),
                         inc=f"lkv{s}", inc_n=16)
                    kv_cnt[s] = P.op("sync", lambda e, qc=qc, pr=pr, s=s: e.dma_start(
                        out=qp_sb[s][:], in_=d_qs[qc, 128 * pr:128 * pr + 128, :]),
                        inc=f"lkv{s}", inc_n=16)
                    for (h01, vh, gr0) in combos:
                        slot = 2 * h01 + vh
                        P.wait("sync", "te", fvt_free_te[slot])
                        src = d_fvtb[:, gr0:gr0 + 256].rearrange(
                            "(kc p) c -> p kc c", p=128)
                        dst = fvt_sb[slot][:].rearrange("p (kc c) -> p kc c", kc=32)
                        fvt_cnt[slot] = P.op("sync", lambda e, s1=src, d1=dst:
                                             e.dma_start(out=d1, in_=s1),
                                             inc=f"lfv{slot}", inc_n=16)

                    P.wait("tensor", f"lkv{s}", kv_cnt[s])
                    s_te = {}
                    exp_se = {}
                    pv_te = {}

                    def emit_S(kc, s=s):
                        b = kc % 2
                        P.wait("tensor", "se", psS_free_se[b])
                        P.op("tensor", lambda e, kc=kc, b=b, s=s: e.matmul(
                            ps_S[b][:, 0:512],
                            kp_sb[s][0:64, 128 * kc:128 * kc + 128],
                            qp_sb[s][0:64, :], start=True, stop=True))
                        s_te[kc] = P.op("tensor", lambda e, kc=kc, b=b, s=s: e.matmul(
                            ps_S[b][:, 512:1024],
                            kp_sb[s][64:128, 128 * kc:128 * kc + 128],
                            qp_sb[s][64:128, :], start=True, stop=True), inc="te")

                    def emit_exp(kc):
                        b = kc % 2
                        gate = pv_te[kc - 2] if kc >= 2 else s_te[kc]
                        P.wait("scalar", "te", gate)
                        c = P.op("scalar", lambda e, b=b: e.activation(
                            p_sb[b][:], ps_S[b][:], AF.Exp, scale=1.0 / 32.0),
                            inc="se")
                        exp_se[kc] = c
                        psS_free_se[b] = c

                    def emit_PV(kc, s=s, combos=combos):
                        b = kc % 2
                        P.wait("tensor", "se", exp_se[kc])
                        if kc == 0:
                            P.wait("tensor", "ve", psO_free_ve)
                            if combos:
                                P.wait("tensor", "ve", psB_free_ve)
                                for (h01, vh, gr0) in combos:
                                    slot = 2 * h01 + vh
                                    P.wait("tensor", f"lfv{slot}", fvt_cnt[slot])
                        n_ops = 2 + len(combos)
                        i = 0
                        t = None
                        for h01 in range(2):
                            i += 1
                            t = P.op("tensor", lambda e, kc=kc, b=b, s=s, h01=h01:
                                     e.matmul(
                                         ps_O[0:65, 512 * h01:512 * h01 + 512],
                                         vp_sb[s][:, 130 * kc + 65 * h01:
                                                   130 * kc + 65 * h01 + 65],
                                         p_sb[b][:, 512 * h01:512 * h01 + 512],
                                         start=(kc == 0), stop=(kc == 31),
                                         skip_group_check=True),
                                     inc="te" if i == n_ops else None)
                        seen_banks = set()
                        for (h01, vh, gr0) in combos:
                            i += 1
                            slot = 2 * h01 + vh
                            bank = slot // 2
                            st_flag = (kc == 0) and bank not in seen_banks
                            if kc == 0:
                                seen_banks.add(bank)
                            t = P.op("tensor", lambda e, kc=kc, b=b, s=s, h01=h01,
                                     slot=slot, st_flag=st_flag: e.matmul(
                                         ps_B[0:64, 256 * slot:256 * slot + 256],
                                         vp_sb[s][:, 130 * kc + 65 * h01:
                                                   130 * kc + 65 * h01 + 64],
                                         fvt_sb[slot][:, 256 * kc:256 * kc + 256],
                                         start=st_flag, stop=(kc == 31),
                                         skip_group_check=True),
                                     inc="te" if i == n_ops else None)
                        pv_te[kc] = t

                    emit_S(0)
                    emit_exp(0)
                    emit_S(1)
                    emit_exp(1)
                    if pending_epi[0] is not None:
                        pending_epi[0]()
                        pending_epi[0] = None
                    if pending_qcln and pr < 4:
                        pending_qcln[0](pr)
                        if pr == 3:
                            pending_qcln.pop(0)
                    for kc in range(32):
                        if kc + 2 < 32:
                            emit_S(kc + 2)
                        emit_PV(kc)
                        if kc + 2 < 32:
                            emit_exp(kc + 2)
                    last_te = pv_te[31]
                    kv_free_te[s] = last_te
                    for (h01, vh, gr0) in combos:
                        fvt_free_te[2 * h01 + vh] = last_te

                    def epi(pr=pr, combos=combos, last_te=last_te, qb=qc % 2):
                        nonlocal psO_free_ve, psB_free_ve
                        P.wait("vector", "te", last_te)
                        evac_ve = P.op("vector", lambda e: e.tensor_copy(
                            of_sb[0:65, :], ps_O[0:65, :]),
                            inc="ve" if not combos else None)
                        for ci, (h01, vh, gr0) in enumerate(combos):
                            slot = 2 * h01 + vh
                            evac_ve = P.op("vector", lambda e, slot=slot:
                                           e.tensor_copy(
                                               bvf_sb[0:64,
                                                      256 * slot:256 * slot + 256],
                                               ps_B[0:64,
                                                    256 * slot:256 * slot + 256]),
                                           inc="ve" if ci == len(combos) - 1
                                           else None)
                        P.wait("tensor", "ve", evac_ve)
                        n_tr = 8 + 2 * len(combos)
                        i = 0
                        for h01 in range(2):
                            for u in range(4):
                                i += 1
                                tr_te = P.op("tensor", lambda e, h01=h01, u=u:
                                             e.transpose(
                                                 ps_O[:, 512 * h01 + 65 * u:
                                                      512 * h01 + 65 * u + 65],
                                                 of_sb[0:65, 512 * h01 + 128 * u:
                                                       512 * h01 + 128 * u + 128],
                                                 idf_sb[0:65, 0:65]),
                                             inc="te" if i == n_tr else None)
                        for (h01, vh, gr0) in combos:
                            slot = 2 * h01 + vh
                            for uu in range(2):
                                i += 1
                                tr_te = P.op("tensor", lambda e, slot=slot, uu=uu:
                                             e.transpose(
                                                 ps_B[:, 256 * slot + 64 * uu:
                                                      256 * slot + 64 * uu + 64],
                                                 bvf_sb[0:64,
                                                        256 * slot + 128 * uu:
                                                        256 * slot + 128 * uu + 128],
                                                 idf_sb[0:64, 0:64]),
                                             inc="te" if i == n_tr else None)
                        P.wait("vector", "te", tr_te)
                        if combos:
                            for (h01, vh, gr0) in combos:
                                slot = 2 * h01 + vh
                                P.op("vector", lambda e, slot=slot: e.tensor_copy(
                                    bvt_sb[:, 256 * slot:256 * slot + 256],
                                    ps_B[:, 256 * slot:256 * slot + 256]), inc=None)
                        den = bass.AP(ps_O[:].tensor, 64, [[1024, 128], [512, 2],
                                                           [65, 4]])
                        rout = rcp_sb[:, 0:8].rearrange("p (a b) -> p a b", a=2)
                        cr = P.op("vector", lambda e, den=den, rout=rout:
                                  e.reciprocal(rout, den), inc="ve")
                        P.wait("vector", "ve", cr)
                        for u in range(4):
                            P.wait("vector", "ve", asm_free_ve[qb][u])
                        for h01 in range(2):
                            a = 2 * pr + h01
                            for u in range(4):
                                vh = u // 2
                                aff = any(c[0] == h01 and c[1] == vh
                                          for c in combos)
                                if aff:
                                    slot = 2 * h01 + vh
                                    in1 = bvt_sb[:, 256 * slot + 64 * (u % 2):
                                                 256 * slot + 64 * (u % 2) + 64]
                                else:
                                    in1 = w_bc[:, 65 * a:65 * a + 64]
                                last = h01 == 1 and u == 3
                                stt_ve = P.op(
                                    "vector", lambda e, h01=h01, u=u, a=a, in1=in1:
                                    e.scalar_tensor_tensor(
                                        asm_sb[qb][u][:, 64 * a:64 * a + 64],
                                        ps_O[:, 512 * h01 + 65 * u:
                                             512 * h01 + 65 * u + 64],
                                        rcp_sb[:, 4 * h01 + u:4 * h01 + u + 1],
                                        in1, ALU.mult, ALU.add),
                                    inc="ve" if last else None)
                        psO_free_ve = stt_ve
                        psB_free_ve = stt_ve

                    pending_epi[0] = epi

                if pending_epi[0] is not None:
                    pending_epi[0]()
                    pending_epi[0] = None

                def mk_qcln(qc=qc):
                    qb = qc % 2

                    def unit(u):
                        sl = stats[u % 2]
                        g_tt = 4 * qc + u
                        hp = u % 2
                        P.wait("sync", "ve", x_freeB[hp])
                        lxb_cnt[hp] = P.op("sync", lambda e, g=g_tt, hp=hp:
                                           e.dma_start(out=x_sb[hp][:],
                                                       in_=d_x[128 * g:
                                                               128 * g + 128, :]),
                                           inc=f"lxb{hp}", inc_n=16)
                        c_ln1 = plan_ln(P, asm_sb[qb][u][:], t1_sb[:], E, 1e-3, sl,
                                        ready_val=asm_ready[qb])
                        asm_free_ve[qb][u] = c_ln1
                        P.wait("vector", "ve", c_ln1)
                        c_g = P.op("vector", lambda e: e.scalar_tensor_tensor(
                            t2_sb[:], t1_sb[:], 0.0, gb_sb[:], ALU.bypass,
                            ALU.mult), inc="ve")
                        P.wait("vector", "ve", c_g)
                        c_b = P.op("vector", lambda e: e.scalar_tensor_tensor(
                            t2_sb[:], t2_sb[:], 0.0, bb_sb[:], ALU.bypass,
                            ALU.add), inc="ve")
                        c_ln2 = plan_ln(P, t2_sb[:], t1_sb[:], E, 1e-5, sl,
                                        ready_val=c_b)
                        P.wait("vector", f"lxb{hp}", lxb_cnt[hp])
                        P.wait("vector", "ve", c_ln2)
                        P.wait("vector", f"sx2{hp}", x2_free[hp])
                        c_x2 = P.op("vector", lambda e, hp=hp: e.tensor_tensor(
                            x2_sb[hp][:], x_sb[hp][:], t1_sb[:], ALU.add),
                            inc="ve")
                        x_freeB[hp] = c_x2
                        P.wait("sync", "ve", c_x2)
                        x2_free[hp] = P.op("sync", lambda e, g=g_tt, hp=hp:
                                           e.dma_start(
                                               out=d_x2[128 * g:128 * g + 128, :],
                                               in_=x2_sb[hp][:]),
                                           inc=f"sx2{hp}", inc_n=16)
                    return unit

                asm_ready[qc % 2] = P.val("ve")
                pending_qcln.append(mk_qcln())

            if pending_epi[0] is not None:
                pending_epi[0]()
                pending_epi[0] = None
            while pending_qcln:
                unit = pending_qcln.pop(0)
                for u in range(4):
                    unit(u)
            b_x2_totals = {"sx20": x2_free[0], "sx21": x2_free[1]}
            if phases <= 2:
                for s_name, tot in b_x2_totals.items():
                    P.wait("sync", s_name, tot)
            P.flush(block)
        if phases <= 2:
            sAB.close()
            return nc

        sAB.close()
        with ExitStack() as sC:
            sbC = lambda n_, s_, d_: sC.enter_context(nc.sbuf_tensor(n_, s_, d_))
            xn2T = sbC("xn2T", [128, 8 * TFF], BF16)
            a_sb = sbC("a_sb", [128, 32 * TFF], BF16)
            h_sb = sbC("h_sb", [128, 32 * TFF], BF16)
            fT_sb = h_sb
            w12_sb = [sbC(f"w12_{i}", [128, 8 * 128], BF16) for i in range(4)]
            hsl_sb = [sbC(f"hsl{i}", [128, 32 * 128], BF16) for i in range(2)]
            x2t_sb = [sbC(f"x2t{i}", [128, E], F32) for i in range(2)]
            xn2_sb = [sbC(f"xn2_{i}", [128, E], BF16) for i in range(2)]
            silu_sb = [sbC(f"silu{i}", [128, 512], F32) for i in range(2)]
            ftm_sb = h_sb[:, 8 * TFF:8 * TFF + 2048].bitcast(F32)
            ps_M = [sC.enter_context(nc.psum_tensor(f"psM{i}", [128, 512], F32))
                    for i in range(4)]
            ps_tr2 = sC.enter_context(nc.psum_tensor("ps_tr2", [128, 2048], BF16))

            P.phase_barrier()

            lx2_cnt = [0, 0]
            x2t_free = [0, 0]
            w12_cnt = [0, 0, 0, 0]
            w12_free_te = [0, 0, 0, 0]
            hsl_cnt = [0, 0]
            hsl_free_te = [0, 0]
            silu_free_ve = [0, 0]
            psM_free = [0, 0, 0, 0]
            psM_free_eng = ["ve", "ve", "ve", "ve"]
            trp2 = [0]
            tr2_evac_ve = [0, 0]
            xn2_free_ve = [0, 0]
            out_free = [0, 0]

            def emit_xn2_unit(ft, u8):
                g_tt = 8 * ft + u8
                hp = u8 % 2
                tp2 = trp2[0]
                P.wait("sync", "ve", x2t_free[hp])
                lx2_cnt[hp] = P.op("sync", lambda e, g=g_tt, hp=hp: e.dma_start(
                    out=x2t_sb[hp][:], in_=d_x2[128 * g:128 * g + 128, :]),
                    inc=f"lx2{hp}", inc_n=16)
                P.wait("vector", f"lx2{hp}", lx2_cnt[hp])
                c_ln = plan_ln(P, x2t_sb[hp][:], xn2_sb[hp][:], E, 1e-5,
                               stats[hp], ready_val=xn2_free_ve[hp])
                x2t_free[hp] = c_ln
                P.wait("tensor", "ve", c_ln)
                P.wait("tensor", "ve", tr2_evac_ve[tp2])
                for f in range(8):
                    te_tr = P.op("tensor", lambda e, hp=hp, f=f, tp2=tp2:
                                 e.transpose(
                                     ps_tr2[:, 1024 * tp2 + 128 * f:
                                            1024 * tp2 + 128 * f + 128],
                                     xn2_sb[hp][:, 128 * f:128 * f + 128],
                                     idb_sb[:]), inc="te" if f == 7 else None)
                P.wait("vector", "te", te_tr)
                for f in range(8):
                    ev = P.op("vector", lambda e, f=f, u8=u8, tp2=tp2:
                              e.tensor_copy(
                                  xn2T[:, TFF * f + 128 * u8:
                                       TFF * f + 128 * u8 + 128],
                                  ps_tr2[:, 1024 * tp2 + 128 * f:
                                         1024 * tp2 + 128 * f + 128]),
                              inc="ve" if f == 7 else None)
                tr2_evac_ve[tp2] = ev
                xn2_free_ve[hp] = ev
                trp2[0] ^= 1

            for u8 in range(8):
                emit_xn2_unit(0, u8)

            for ft in range(4):
                xn2T_ve = P.val("ve")
                P.wait("tensor", "ve", xn2T_ve)

                for mc in range(32):
                    for wi, d_w in enumerate([d_w1b, d_w2b]):
                        slot = 2 * (mc % 2) + wi
                        P.wait("sync", "te", w12_free_te[slot])
                        w12_cnt[slot] = P.op("sync", lambda e, d_w=d_w, mc=mc,
                                             slot=slot: e.dma_start(
                                                 out=w12_sb[slot][:],
                                                 in_=d_w[mc, :, :]),
                                             inc=f"lw12_{slot}", inc_n=16)
                    for wi in range(2):
                        P.wait("tensor", f"lw12_{2 * (mc % 2) + wi}",
                               w12_cnt[2 * (mc % 2) + wi])
                    for nt in range(2):
                        for wi in range(2):
                            b = 2 * (mc % 2) + wi
                            slot = 2 * (mc % 2) + wi
                            P.wait("tensor", psM_free_eng[b], psM_free[b])
                            for ki in range(8):
                                te_w12 = P.op(
                                    "tensor", lambda e, slot=slot, ki=ki, nt=nt, b=b:
                                    e.matmul(
                                        ps_M[b][:],
                                        w12_sb[slot][:, 128 * ki:128 * ki + 128],
                                        xn2T[:, TFF * ki + 512 * nt:
                                             TFF * ki + 512 * nt + 512],
                                        start=(ki == 0), stop=(ki == 7)),
                                    inc="te" if ki == 7 else None)
                        b1, b2 = 2 * (mc % 2), 2 * (mc % 2) + 1
                        hp2 = nt
                        P.wait("scalar", "te", te_w12)
                        P.wait("scalar", "ve", silu_free_ve[hp2])
                        c_si = P.op("scalar", lambda e, b1=b1, mc=mc, hp2=hp2:
                                    e.activation(silu_sb[hp2][:], ps_M[b1][:],
                                                 AF.Silu,
                                                 scale=s1_sb[:, mc:mc + 1]),
                                    inc="se")
                        psM_free[b1] = c_si
                        psM_free_eng[b1] = "se"
                        P.wait("vector", "se", c_si)
                        c_a = P.op("vector", lambda e, b2=b2, mc=mc, nt=nt, hp2=hp2:
                                   e.scalar_tensor_tensor(
                                       a_sb[:, TFF * mc + 512 * nt:
                                            TFF * mc + 512 * nt + 512],
                                       ps_M[b2][:], s2_sb[:, mc:mc + 1],
                                       silu_sb[hp2][:], ALU.mult, ALU.mult),
                                   inc="ve")
                        silu_free_ve[hp2] = c_a
                        psM_free[b2] = c_a
                        psM_free_eng[b2] = "ve"
                        w12_free_te[b1] = te_w12
                        w12_free_te[b2] = te_w12

                def h_stage(d_ht, s_ap, src_sb, dst_sb, sname, interleave_xn2):
                    te_h = 0
                    for mc in range(32):
                        if interleave_xn2 and mc >= 2 and mc % 2 == 0 \
                                and mc // 2 - 1 < 8:
                            emit_xn2_unit(ft + 1, mc // 2 - 1)
                        slot = mc % 2
                        P.wait("sync", "te", hsl_free_te[slot])
                        hsl_cnt[slot] = P.op("sync", lambda e, d_ht=d_ht, mc=mc,
                                             slot=slot: e.dma_start(
                                                 out=hsl_sb[slot][:],
                                                 in_=d_ht[mc, :, :]),
                                             inc=f"{sname}{slot}", inc_n=16)
                        P.wait("tensor", f"{sname}{slot}", hsl_cnt[slot])
                        for nt in range(2):
                            b = 2 * slot + nt
                            P.wait("tensor", psM_free_eng[b], psM_free[b])
                            for ki in range(32):
                                te_h = P.op(
                                    "tensor", lambda e, slot=slot, ki=ki, nt=nt, b=b,
                                    src_sb=src_sb: e.matmul(
                                        ps_M[b][:],
                                        hsl_sb[slot][:, 128 * ki:128 * ki + 128],
                                        src_sb[:, TFF * ki + 512 * nt:
                                               TFF * ki + 512 * nt + 512],
                                        start=(ki == 0), stop=(ki == 31)),
                                    inc="te" if ki == 31 else None)
                            P.wait("scalar", "te", te_h)
                            c_g2 = P.op("scalar", lambda e, b=b, mc=mc, nt=nt,
                                        dst_sb=dst_sb, s_ap=s_ap: e.activation(
                                            dst_sb[:, TFF * mc + 512 * nt:
                                                   TFF * mc + 512 * nt + 512],
                                            ps_M[b][:], AF.Gelu,
                                            scale=s_ap[:, mc:mc + 1]), inc="se")
                            psM_free[b] = c_g2
                            psM_free_eng[b] = "se"
                        hsl_free_te[slot] = te_h
                    return P.val("se")

                a_ve = P.val("ve")
                P.wait("tensor", "ve", a_ve)
                se_h1 = h_stage(d_h1b, sh1_sb, a_sb, h_sb, "lh1_", ft < 3)
                P.wait("tensor", "se", se_h1)
                se_h2 = h_stage(d_h2b, sh2_sb, h_sb, a_sb, "lh2_", False)
                P.wait("tensor", "se", se_h2)

                for oc in range(8):
                    slot = oc % 2
                    P.wait("sync", "te", hsl_free_te[slot])
                    hsl_cnt[slot] = P.op("sync", lambda e, oc=oc, slot=slot:
                                         e.dma_start(out=hsl_sb[slot][:],
                                                     in_=d_w3b[oc, :, :]),
                                         inc=f"lw3_{slot}", inc_n=16)
                    P.wait("tensor", f"lw3_{slot}", hsl_cnt[slot])
                    for nt in range(2):
                        b = 2 * slot + nt
                        P.wait("tensor", psM_free_eng[b], psM_free[b])
                        for ki in range(32):
                            te_w3 = P.op(
                                "tensor", lambda e, slot=slot, ki=ki, nt=nt, b=b:
                                e.matmul(ps_M[b][:],
                                         hsl_sb[slot][:, 128 * ki:128 * ki + 128],
                                         a_sb[:, TFF * ki + 512 * nt:
                                              TFF * ki + 512 * nt + 512],
                                         start=(ki == 0), stop=(ki == 31)),
                                inc="te" if ki == 31 else None)
                        P.wait("vector", "te", te_w3)
                        c_f = P.op("vector", lambda e, b=b, oc=oc, nt=nt:
                                   e.tensor_scalar(
                                       fT_sb[:, TFF * oc + 512 * nt:
                                             TFF * oc + 512 * nt + 512],
                                       ps_M[b][:], s3_sb[:, oc:oc + 1], None,
                                       ALU.mult), inc="ve")
                        psM_free[b] = c_f
                        psM_free_eng[b] = "ve"
                    hsl_free_te[slot] = te_w3
                fT_ve = P.val("ve")
                P.wait("tensor", "ve", fT_ve)

                for u8 in range(8):
                    g_tt = 8 * ft + u8
                    hp = u8 % 2
                    tp2 = trp2[0]
                    P.wait("tensor", "ve", tr2_evac_ve[tp2])
                    for f in range(8):
                        te_tr = P.op("tensor", lambda e, f=f, u8=u8, tp2=tp2:
                                     e.transpose(
                                         ps_tr2[:, 1024 * tp2 + 128 * f:
                                                1024 * tp2 + 128 * f + 128],
                                         fT_sb[:, TFF * f + 128 * u8:
                                               TFF * f + 128 * u8 + 128],
                                         idb_sb[:]), inc="te" if f == 7 else None)
                    P.wait("vector", "te", te_tr)
                    c_ftm = P.op("vector", lambda e, tp2=tp2: e.tensor_copy(
                        ftm_sb, ps_tr2[:, 1024 * tp2:1024 * tp2 + 1024]),
                        inc="ve")
                    tr2_evac_ve[tp2] = c_ftm
                    trp2[0] ^= 1
                    P.wait("sync", "ve", x2t_free[hp])
                    lx2_cnt[hp] = P.op("sync", lambda e, g=g_tt, hp=hp: e.dma_start(
                        out=x2t_sb[hp][:], in_=d_x2[128 * g:128 * g + 128, :]),
                        inc=f"lx2{hp}", inc_n=16)
                    c_ln = plan_ln(P, ftm_sb, t1_sb[:], E, 1e-5, stats[2 + hp],
                                   ready_val=c_ftm)
                    P.wait("vector", f"lx2{hp}", lx2_cnt[hp])
                    P.wait("vector", "ve", c_ln)
                    c_o = P.op("vector", lambda e, hp=hp: e.tensor_tensor(
                        x2t_sb[hp][:], x2t_sb[hp][:], t1_sb[:], ALU.add), inc="ve")
                    x2t_free[hp] = c_o
                    P.wait("sync", "ve", c_o)
                    out_free[hp] = P.op("sync", lambda e, g=g_tt, hp=hp: e.dma_start(
                        out=d_out[128 * g:128 * g + 128, :], in_=x2t_sb[hp][:]),
                        inc=f"sout{hp}", inc_n=16)
                    P.wait("sync", f"sout{hp}", out_free[hp])

            P.wait("sync", "sout0", out_free[0])
            P.wait("sync", "sout1", out_free[1])
            P.flush(block)

    return nc


_PERM = None


def _perm():
    global _PERM
    if _PERM is None:
        p = np.arange(NT)
        _PERM = 16 * (p % 256) + p // 256
    return _PERM


def _prep_core_inputs(xb, d):
    perm = _perm()
    rel = d["L0_attn_rel_table"]
    jj = perm // 16
    cc = perm % 16
    gr = np.arange(2048)
    idx = np.minimum(2047 - gr[None, :] + jj[:, None], 2046)
    fvt = np.ascontiguousarray(rel[idx, cc[:, None]])
    cmat = np.ascontiguousarray(
        np.broadcast_to(np.repeat(rel[2046], 2)[None, :], (128, 32)))
    return {
        "x": np.ascontiguousarray(xb[perm]),
        "wq": d["L0_attn_Wq"], "wk": d["L0_attn_Wk"], "wv": d["L0_attn_Wv"],
        "fvt": fvt, "cmat": cmat,
        "gamma_b": np.ascontiguousarray(
            np.broadcast_to(d["L0_attn_gamma"][None, :], (128, E))),
        "beta_b": np.ascontiguousarray(
            np.broadcast_to(d["L0_attn_beta"][None, :], (128, E))),
        "w1t": np.ascontiguousarray(d["L0_ffn_w1_w"].T),
        "w2t": np.ascontiguousarray(d["L0_ffn_w2_w"].T),
        "h1t": np.ascontiguousarray(d["L0_ffn_h1_w"].T),
        "h2t": np.ascontiguousarray(d["L0_ffn_h2_w"].T),
        "w3t": np.ascontiguousarray(d["L0_ffn_w3_w"].T),
        "s1m": np.ascontiguousarray(d["L0_ffn_w1_s"].reshape(32, 128).T),
        "s2m": np.ascontiguousarray(d["L0_ffn_w2_s"].reshape(32, 128).T),
        "sh1m": np.ascontiguousarray(d["L0_ffn_h1_s"].reshape(32, 128).T),
        "sh2m": np.ascontiguousarray(d["L0_ffn_h2_s"].reshape(32, 128).T),
        "s3m": np.ascontiguousarray(d["L0_ffn_w3_s"].reshape(8, 128).T),
        "id_bf": np.eye(128, dtype=np.float32).astype(ml_dtypes.bfloat16),
        "id_f32": np.eye(128, dtype=np.float32),
    }


def kernel(**inputs):
    x = np.asarray(inputs["L0_attn_x"], dtype=np.float32)
    d = {k: np.asarray(v, dtype=np.float32) for k, v in inputs.items()}
    nc = build_program()
    in_maps = [_prep_core_inputs(x[b], d) for b in range(8)]
    res = run_bass_kernel_spmd(nc, in_maps, core_ids=list(range(8)))
    perm = _perm()
    out = np.empty((8, NT, E), dtype=np.float32)
    for b in range(8):
        out[b][perm] = res.results[b]["out"]
    return out


# version 12
# speedup vs baseline: 1.4606x; 1.0450x over previous
import numpy as np
import ml_dtypes
import concourse.bass as bass
import concourse.mybir as mybir
from concourse.bass_utils import run_bass_kernel_spmd
from contextlib import ExitStack

F32 = mybir.dt.float32
BF16 = mybir.dt.bfloat16
AF = mybir.ActivationFunctionType
ALU = mybir.AluOpType
AX = mybir.AxisListType

NT = 4096
E = 1024
H = 16
HID = 4096
NQC = 8
TFF = 1024

ENGINES = ("sync", "scalar", "vector", "tensor", "gpsimd")


class Plan:

    def __init__(self, nc, stack):
        self.nc = nc
        self.stack = stack
        self.thunks = {e: [] for e in ENGINES}
        self.sems = {}
        self.counts = {}

    def sem(self, name):
        if name not in self.sems:
            self.sems[name] = self.stack.enter_context(self.nc.semaphore(name))
            self.counts[name] = 0
        return self.sems[name]

    def op(self, eng, fn, inc=None, inc_n=1):
        if inc is not None:
            self.sem(inc)
            self.counts[inc] += inc_n
            target = self.counts[inc]
            sem = self.sems[inc]

            def thunk(e, fn=fn, sem=sem, inc_n=inc_n):
                fn(e).then_inc(sem, inc_n)
        else:
            target = None

            def thunk(e, fn=fn):
                fn(e)
        self.thunks[eng].append(thunk)
        return target

    def wait(self, eng, name, val):
        if val is None or val <= 0:
            return
        sem = self.sem(name)
        self.thunks[eng].append(lambda e, s=sem, v=val: e.wait_ge(s, v))

    def val(self, name):
        self.sem(name)
        return self.counts[name]

    def phase_barrier(self, exclude=()):
        snap = [(n, v) for n, v in self.counts.items()
                if v > 0 and n not in exclude]
        for eng in ENGINES:
            for n, v in snap:
                self.wait(eng, n, v)

    def flush(self, block):
        for eng in ENGINES:
            if not self.thunks[eng]:
                continue
            fns = self.thunks[eng]
            self.thunks[eng] = []
            getattr(block, eng)(lambda e, fns=fns: [f(e) for f in fns])


def plan_ln(P, x_ap, out_ap, n, eps, sl, ready_val=0):
    P.wait("vector", "ve", ready_val)
    c1 = P.op("vector", lambda e: e.reduce_sum(sl["rs"], x_ap, axis=AX.X), inc="ve")
    P.wait("vector", "ve", c1)
    c2 = P.op("vector", lambda e: e.tensor_scalar_mul(sl["nmu"], sl["rs"], -1.0 / n),
              inc="ve")
    P.wait("scalar", "ve", c2)
    c3 = P.op("scalar", lambda e: e.activation(sl["sq_scr"], x_ap, AF.Square,
                                               bias=sl["nmu"], scale=1.0,
                                               accum_out=sl["var"]), inc="se")
    P.wait("vector", "se", c3)
    c4 = P.op("vector", lambda e: e.tensor_scalar(sl["var"], sl["var"], 1.0 / n, eps,
                                                  ALU.mult, ALU.add), inc="ve")
    P.wait("scalar", "ve", c4)
    c5 = P.op("scalar", lambda e: e.activation(sl["sd"], sl["var"], AF.Sqrt), inc="se")
    P.wait("vector", "se", c5)
    c6 = P.op("vector", lambda e: e.reciprocal(sl["rstd"], sl["sd"]), inc="ve")
    P.wait("vector", "ve", c6)
    c7 = P.op("vector", lambda e: e.tensor_tensor(sl["nmur"], sl["nmu"], sl["rstd"],
                                                  ALU.mult), inc="ve")
    P.wait("vector", "ve", c7)
    c8 = P.op("vector", lambda e: e.tensor_scalar(out_ap, x_ap, sl["rstd"], sl["nmur"],
                                                  ALU.mult, ALU.add), inc="ve")
    return c8


def build_program(debug=False, phases=3, nqc=NQC, sub="all"):
    nc = bass.Bass("TRN2", target_bir_lowering=False)

    d_x = nc.dram_tensor("x", [NT, E], F32, kind="ExternalInput").ap()
    d_wq = nc.dram_tensor("wq", [E, E], F32, kind="ExternalInput").ap()
    d_wk = nc.dram_tensor("wk", [E, E], F32, kind="ExternalInput").ap()
    d_wv = nc.dram_tensor("wv", [E, E], F32, kind="ExternalInput").ap()
    d_fvt = nc.dram_tensor("fvt", [NT, 2048], F32, kind="ExternalInput").ap()
    d_cmat = nc.dram_tensor("cmat", [128, 32], F32, kind="ExternalInput").ap()
    d_gb = nc.dram_tensor("gamma_b", [128, E], F32, kind="ExternalInput").ap()
    d_bb = nc.dram_tensor("beta_b", [128, E], F32, kind="ExternalInput").ap()
    d_w1t = nc.dram_tensor("w1t", [E, HID], F32, kind="ExternalInput").ap()
    d_w2t = nc.dram_tensor("w2t", [E, HID], F32, kind="ExternalInput").ap()
    d_h1t = nc.dram_tensor("h1t", [HID, HID], F32, kind="ExternalInput").ap()
    d_h2t = nc.dram_tensor("h2t", [HID, HID], F32, kind="ExternalInput").ap()
    d_w3t = nc.dram_tensor("w3t", [HID, E], F32, kind="ExternalInput").ap()
    d_s1 = nc.dram_tensor("s1m", [128, 32], F32, kind="ExternalInput").ap()
    d_s2 = nc.dram_tensor("s2m", [128, 32], F32, kind="ExternalInput").ap()
    d_sh1 = nc.dram_tensor("sh1m", [128, 32], F32, kind="ExternalInput").ap()
    d_sh2 = nc.dram_tensor("sh2m", [128, 32], F32, kind="ExternalInput").ap()
    d_s3 = nc.dram_tensor("s3m", [128, 8], F32, kind="ExternalInput").ap()
    d_idb = nc.dram_tensor("id_bf", [128, 128], BF16, kind="ExternalInput").ap()
    d_idf = nc.dram_tensor("id_f32", [128, 128], F32, kind="ExternalInput").ap()
    d_out = nc.dram_tensor("out", [NT, E], F32, kind="ExternalOutput").ap()

    sk = "ExternalOutput" if debug else "Internal"
    d_qs = nc.dram_tensor("qs", [NQC, E, 512], BF16, kind=sk).ap()
    d_ks = nc.dram_tensor("ks", [NQC, E, 512], BF16, kind=sk).ap()
    d_vs = nc.dram_tensor("vs", [NT, H * 65], BF16, kind=sk).ap()
    d_x2 = nc.dram_tensor("x2s", [NT, E], F32, kind=sk).ap()
    d_fvtb = nc.dram_tensor("fvtb", [NT, 2048], BF16, kind=sk).ap()
    d_w1b = nc.dram_tensor("w1b", [32, 128, 1024], BF16, kind=sk).ap()
    d_w2b = nc.dram_tensor("w2b", [32, 128, 1024], BF16, kind="Internal").ap()
    d_h1b = nc.dram_tensor("h1b", [32, 128, 4096], BF16, kind="Internal").ap()
    d_h2b = nc.dram_tensor("h2b", [32, 128, 4096], BF16, kind="Internal").ap()
    d_w3b = nc.dram_tensor("w3b", [8, 128, 4096], BF16, kind="Internal").ap()

    stack = ExitStack()
    with stack:
        P = Plan(nc, stack)
        sb = lambda name, shape, dt: stack.enter_context(nc.sbuf_tensor(name, shape, dt))

        gb_sb = sb("gb", [128, E], F32)
        bb_sb = sb("bb", [128, E], F32)
        c_f32 = sb("c_f32", [128, 32], F32)
        c_bf = sb("c_bf", [128, 32], BF16)
        s1_sb = sb("s1", [128, 32], F32)
        s2_sb = sb("s2", [128, 32], F32)
        sh1_sb = sb("sh1", [128, 32], F32)
        sh2_sb = sb("sh2", [128, 32], F32)
        s3_sb = sb("s3", [128, 8], F32)
        idb_sb = sb("idb", [128, 128], BF16)
        idf_sb = sb("idf", [128, 128], F32)
        ones_f = sb("ones_f", [1, 128], F32)
        w_bc = sb("w_bc", [128, H * 65], F32)
        sq_scr = sb("sq_scr", [128, E], F32)
        t1_sb = sb("t1", [128, E], F32)
        st_t = sb("stats", [128, 32], F32)
        stats = []
        for s in range(4):
            stats.append({"rs": st_t[:, 8 * s:8 * s + 1],
                          "nmu": st_t[:, 8 * s + 1:8 * s + 2],
                          "var": st_t[:, 8 * s + 2:8 * s + 3],
                          "sd": st_t[:, 8 * s + 3:8 * s + 4],
                          "rstd": st_t[:, 8 * s + 4:8 * s + 5],
                          "nmur": st_t[:, 8 * s + 5:8 * s + 6],
                          "sq_scr": sq_scr[:]})

        sAB = ExitStack()
        pcst = [sAB.enter_context(nc.sbuf_tensor(f"pcst{i}", [128, 4096], BF16))
                for i in range(2)]

        block = stack.enter_context(nc.Block())

        with ExitStack() as sA:
            sbA = lambda n_, s_, d_: sA.enter_context(nc.sbuf_tensor(n_, s_, d_))
            wq_sb = sbA("wq_sb", [128, 8 * E], BF16)
            wk_sb = sbA("wk_sb", [128, 8 * E], BF16)
            wv_sb = sbA("wv_sb", [128, 8 * E], BF16)
            xn1_sb = [sbA(f"xn1_{i}", [128, E], BF16) for i in range(2)]
            xnT_sb = sbA("xnT", [128, 8 * 512], BF16)
            qe_sb = [sbA(f"qe{i}", [128, 512], BF16) for i in range(2)]
            ke_sb = [sbA(f"ke{i}", [128, 512], BF16) for i in range(2)]
            va_sb = [sbA(f"va{i}", [128, H * 65], BF16) for i in range(2)]
            w_row = sbA("w_row", [1, H * 65], F32)
            w_hi = sbA("w_hi", [1, H * 65], BF16)
            w_rem = sbA("w_rem", [1, H * 65], F32)
            w_lo = sbA("w_lo", [1, H * 65], BF16)
            ones_b = sbA("ones_b", [1, 128], BF16)
            x_sb = [sbA(f"xA{i}", [128, E], F32) for i in range(2)]
            ps_mm = sA.enter_context(nc.psum_tensor("ps_mm", [128, 1024], F32))
            ps_tr = sA.enter_context(nc.psum_tensor("ps_tr", [128, 2048], BF16))
            ps_w = sA.enter_context(nc.psum_tensor("ps_w", [1, 1536], F32))

            for dd, ss in [(d_gb, gb_sb), (d_bb, bb_sb), (d_cmat, c_f32),
                           (d_s1, s1_sb), (d_s2, s2_sb), (d_sh1, sh1_sb),
                           (d_sh2, sh2_sb), (d_s3, s3_sb), (d_idb, idb_sb),
                           (d_idf, idf_sb)]:
                l_set = P.op("sync", lambda e, dd=dd, ss=ss:
                             e.dma_start(out=ss[:], in_=dd[:]), inc="l_set", inc_n=16)
            for dd, ss in [(d_wq, wq_sb), (d_wk, wk_sb), (d_wv, wv_sb)]:
                src = dd.rearrange("(ki p) o -> p ki o", p=128)
                dst = ss[:].rearrange("p (ki o) -> p ki o", ki=8)
                l_wqkv = P.op("gpsimd", lambda e, s1=src, d1=dst:
                              e.dma_start(out=d1, in_=s1), inc="l_wqkv", inc_n=16)
            pc_ld = [0, 0]
            pc_st = [0, 0]
            pc_idx = 0
            for c in range(32):
                sl = pc_idx % 2
                pc_idx += 1
                P.wait("gpsimd", f"pcs{sl}", pc_st[sl])
                pc_ld[sl] = P.op("gpsimd", lambda e, c=c, sl=sl: e.dma_start(
                    out=pcst[sl][:, 0:2048],
                    in_=d_fvt[128 * c:128 * c + 128, :]), inc=f"pcl{sl}", inc_n=16)
                P.wait("gpsimd", f"pcl{sl}", pc_ld[sl])
                pc_st[sl] = P.op("gpsimd", lambda e, c=c, sl=sl: e.dma_start(
                    out=d_fvtb[128 * c:128 * c + 128, :],
                    in_=pcst[sl][:, 0:2048]), inc=f"pcs{sl}", inc_n=16)
            fvt_pc_done = (pc_st[0], pc_st[1])
            for dsrc, ddst, nsl, nki in [(d_w1t, d_w1b, 32, 8),
                                         (d_w2t, d_w2b, 32, 8),
                                         (d_h1t, d_h1b, 32, 32),
                                         (d_h2t, d_h2b, 32, 32),
                                         (d_w3t, d_w3b, 8, 32)]:
                wid = nki * 128
                for c in range(nsl):
                    sl = pc_idx % 2
                    pc_idx += 1
                    srcap = dsrc[:, 128 * c:128 * c + 128].rearrange(
                        "(ki p) o -> p ki o", p=128)
                    dstap = pcst[sl][:, 0:wid].rearrange(
                        "p (ki o) -> p ki o", ki=nki)
                    P.wait("gpsimd", f"pcs{sl}", pc_st[sl])
                    pc_ld[sl] = P.op("gpsimd", lambda e, srcap=srcap, dstap=dstap,
                                     sl=sl: e.dma_start(out=dstap, in_=srcap),
                                     inc=f"pcl{sl}", inc_n=16)
                    P.wait("gpsimd", f"pcl{sl}", pc_ld[sl])
                    pc_st[sl] = P.op("gpsimd", lambda e, ddst=ddst, c=c, sl=sl,
                                     wid=wid: e.dma_start(
                                         out=ddst[c, :, :],
                                         in_=pcst[sl][:, 0:wid]),
                                     inc=f"pcs{sl}", inc_n=16)
            P.wait("vector", "l_set", l_set)
            P.op("vector", lambda e: e.tensor_copy(c_bf[:], c_f32[:]), inc="ve")
            P.op("vector", lambda e: e.memset(ones_f[:], 1.0), inc="ve")
            setup_ve = P.op("vector", lambda e: e.memset(ones_b[:], 1.0), inc="ve")

            lx_cnt = [0, 0]
            xn_free_ve = [0, 0]
            x_freeA = [0, 0]
            stq_free = [0, 0]
            stk_free = [0, 0]
            stv_free = [0, 0]
            trp = 0
            tr_evac_ve = [0, 0]
            mm_free_ve = [0, 0]
            te_w = 0
            last_g = 4 * nqc - 1
            lnA_ve = {}

            def emit_A_ln(g_tt):
                s = g_tt % 2
                P.wait("sync", "ve", x_freeA[s])
                lx_cnt[s] = P.op("sync", lambda e, g=g_tt, s=s: e.dma_start(
                    out=x_sb[s][:], in_=d_x[128 * g:128 * g + 128, :]),
                    inc=f"lx{s}", inc_n=16)
                P.wait("vector", f"lx{s}", lx_cnt[s])
                ln_ve = plan_ln(P, x_sb[s][:], xn1_sb[s][:], E, 1e-5, stats[s],
                                ready_val=xn_free_ve[s])
                x_freeA[s] = ln_ve
                lnA_ve[g_tt] = ln_ve

            def emit_A_tr(g_tt, trp, t):
                s = g_tt % 2
                P.wait("tensor", "ve", lnA_ve[g_tt])
                P.wait("tensor", "ve", tr_evac_ve[trp])
                for f in range(8):
                    te_tr = P.op("tensor", lambda e, s=s, f=f, trp=trp: e.transpose(
                        ps_tr[:, 1024 * trp + 128 * f:1024 * trp + 128 * f + 128],
                        xn1_sb[s][:, 128 * f:128 * f + 128], idb_sb[:]),
                        inc="te" if f == 7 else None)
                P.wait("vector", "te", te_tr)
                for f in range(8):
                    ev = P.op("vector", lambda e, f=f, t=t, trp=trp: e.tensor_copy(
                        xnT_sb[:, 512 * f + 128 * t:512 * f + 128 * t + 128],
                        ps_tr[:, 1024 * trp + 128 * f:1024 * trp + 128 * f + 128]),
                        inc="ve" if f == 7 else None)
                tr_evac_ve[trp] = ev
                xn_free_ve[s] = ev

            emit_A_ln(0)
            for qc in range(nqc):
                for t in range(4):
                    g_tt = 4 * qc + t
                    if g_tt + 1 < 4 * nqc:
                        emit_A_ln(g_tt + 1)
                    emit_A_tr(g_tt, trp, t)
                    trp ^= 1
                if sub not in ("qk", "all"):
                    continue
                P.wait("tensor", "l_wqkv", l_wqkv)
                P.wait("tensor", "ve", setup_ve)
                xnT_ready = P.val("ve")
                P.wait("tensor", "ve", xnT_ready)
                for w_sb, ev_sb, ev_free, d_dst, sem_st in [
                        (wq_sb, qe_sb, stq_free, d_qs, "stq"),
                        (wk_sb, ke_sb, stk_free, d_ks, "stk")]:
                    for f in range(8):
                        hp = f % 2
                        P.wait("tensor", "ve", mm_free_ve[hp])
                        for ki in range(8):
                            te_mm = P.op(
                                "tensor", lambda e, w_sb=w_sb, f=f, ki=ki, hp=hp:
                                e.matmul(ps_mm[:, 512 * hp:512 * hp + 512],
                                         w_sb[:, E * ki + 128 * f:E * ki + 128 * f + 128],
                                         xnT_sb[:, 512 * ki:512 * ki + 512],
                                         start=(ki == 0), stop=(ki == 7)),
                                inc="te" if ki == 7 else None)
                        P.wait("vector", "te", te_mm)
                        P.wait("vector", f"{sem_st}{hp}", ev_free[hp])
                        cv = P.op("vector", lambda e, ev_sb=ev_sb, hp=hp: e.tensor_copy(
                            ev_sb[hp][:], ps_mm[:, 512 * hp:512 * hp + 512]), inc="ve")
                        mm_free_ve[hp] = cv
                        P.wait("sync", "ve", cv)
                        ev_free[hp] = P.op(
                            "sync", lambda e, ev_sb=ev_sb, hp=hp, qc=qc, f=f,
                            d_dst=d_dst: e.dma_start(
                                out=d_dst[qc, 128 * f:128 * f + 128, :],
                                in_=ev_sb[hp][:]),
                            inc=f"{sem_st}{hp}", inc_n=16)
                if sub not in ("v", "all", "vmm", "vst"):
                    continue
                for t in range(4):
                    g_tt = 4 * qc + t
                    hp = t % 2
                    P.wait("vector", f"stv{hp}", stv_free[hp])
                    mv = P.op("vector", lambda e, hp=hp: e.memset(va_sb[hp][:], 1.0),
                              inc="ve")
                    for nf in range(2):
                        P.wait("tensor", "ve", mm_free_ve[nf])
                        for ki in range(8):
                            te_mm = P.op("tensor", lambda e, t=t, ki=ki, nf=nf:
                                         e.matmul(
                                             ps_mm[:, 512 * nf:512 * nf + 512],
                                             xnT_sb[:, 512 * ki + 128 * t:
                                                    512 * ki + 128 * t + 128],
                                             wv_sb[:, E * ki + 512 * nf:
                                                   E * ki + 512 * nf + 512],
                                             start=(ki == 0), stop=(ki == 7)),
                                         inc="te" if ki == 7 else None)
                        P.wait("vector", "te", te_mm)
                        P.wait("vector", "ve", mv)
                        dst = va_sb[hp][:].rearrange("p (h c) -> p h c", c=65)[
                            :, 8 * nf:8 * nf + 8, 0:64]
                        src = ps_mm[:, 512 * nf:512 * nf + 512].rearrange(
                            "p (h c) -> p h c", c=64)
                        cv = P.op("vector", lambda e, dst=dst, src=src:
                                  e.tensor_copy(dst, src), inc="ve")
                        mm_free_ve[nf] = cv
                    va_ve = P.val("ve")
                    if sub != "vmm":
                        P.wait("sync", "ve", va_ve)
                        stv_free[hp] = P.op("sync", lambda e, hp=hp, g=g_tt: e.dma_start(
                            out=d_vs[128 * g:128 * g + 128, :], in_=va_sb[hp][:]),
                            inc=f"stv{hp}", inc_n=16)
                    if sub in ("vmm", "vst"):
                        continue
                    P.wait("tensor", "ve", va_ve)
                    for ch, (c0, c1) in enumerate([(0, 512), (512, 1024),
                                                   (1024, 1040)]):
                        te_w = P.op("tensor", lambda e, g=g_tt, hp=hp, c0=c0, c1=c1:
                                    e.matmul(ps_w[0:1, c0:c1], c_bf[:, g:g + 1],
                                             va_sb[hp][:, c0:c1],
                                             start=(g == 0), stop=(g == last_g),
                                             skip_group_check=True),
                                    inc="te" if (ch == 2 and g_tt == last_g) else None)
            w_ready_ve = 0
            if sub in ("v", "all"):
                P.wait("vector", "te", te_w)
                cw = P.op("vector", lambda e: e.tensor_copy(
                    w_row[:], ps_w[0:1, 0:H * 65]), inc="ve")
                P.wait("vector", "ve", cw)
                c_hi = P.op("vector", lambda e: e.tensor_copy(w_hi[:], w_row[:]),
                            inc="ve")
                P.wait("vector", "ve", c_hi)
                c_rm = P.op("vector", lambda e: e.tensor_tensor(
                    w_rem[:], w_row[:], w_hi[:], ALU.subtract), inc="ve")
                P.wait("vector", "ve", c_rm)
                c_lo = P.op("vector", lambda e: e.tensor_copy(w_lo[:], w_rem[:]),
                            inc="ve")
                P.wait("tensor", "ve", c_lo)
                P.wait("tensor", "ve", mm_free_ve[0])
                P.wait("tensor", "ve", mm_free_ve[1])
                for c0, c1 in [(0, 512), (512, 1024), (1024, 1040)]:
                    P.op("tensor", lambda e, c0=c0, c1=c1: e.matmul(
                        ps_mm[:, 0:c1 - c0], ones_b[:], w_hi[0:1, c0:c1],
                        start=True, stop=False))
                    tw = P.op("tensor", lambda e, c0=c0, c1=c1: e.matmul(
                        ps_mm[:, 0:c1 - c0], ones_b[:], w_lo[0:1, c0:c1],
                        start=False, stop=True), inc="te")
                    P.wait("vector", "te", tw)
                    cbc = P.op("vector", lambda e, c0=c0, c1=c1: e.tensor_copy(
                        w_bc[:, c0:c1], ps_mm[:, 0:c1 - c0]), inc="ve")
                    P.wait("tensor", "ve", cbc)
                    w_ready_ve = cbc
            a_store_totals = {"stq0": stq_free[0], "stq1": stq_free[1],
                              "stk0": stk_free[0], "stk1": stk_free[1],
                              "stv0": stv_free[0], "stv1": stv_free[1]}
            if phases <= 1:
                for s_name, tot in a_store_totals.items():
                    P.wait("sync", s_name, tot)
            P.flush(block)
        if phases <= 1:
            sAB.close()
            return nc

        with ExitStack() as sB:
            sbB = lambda n_, s_, d_: sB.enter_context(nc.sbuf_tensor(n_, s_, d_))
            kp_sb = [sbB(f"kp{i}", [128, NT], BF16) for i in range(2)]
            vp_sb = [sbB(f"vp{i}", [128, 32 * 130], BF16) for i in range(2)]
            qp_sb = [sbB(f"qp{i}", [128, 512], BF16) for i in range(2)]
            fvt_sb = [sbB(f"fvt{i}", [128, 32 * 256], BF16) for i in range(4)]
            p_sb = [sbB(f"p{i}", [128, 1024], BF16) for i in range(4)]
            of_sb = sbB("of", [128, 1024], F32)
            bvf_sb = sbB("bvf", [128, 1024], F32)
            bvt_sb = bvf_sb
            asm_sb = [[sbB(f"asm{b_}_{i}", [128, E], F32) for i in range(4)]
                      for b_ in range(2)]
            x2_sb = [sbB(f"x2_{i}", [128, E], F32) for i in range(2)]
            t2_sb = sbB("t2", [128, E], F32)
            x_sb = [sbB(f"xB{i}", [128, E], F32) for i in range(2)]
            rcp_sb = sbB("rcp", [128, 8], F32)
            ps_S = [sB.enter_context(nc.psum_tensor(f"psS{i}", [128, 1024], F32))
                    for i in range(2)]
            ps_O = sB.enter_context(nc.psum_tensor("psO", [128, 1024], F32))
            ps_B = sB.enter_context(nc.psum_tensor("psB", [128, 1024], F32))

            P.phase_barrier(exclude=("pcl0", "pcl1", "pcs0", "pcs1"))
            P.wait("sync", "pcs0", fvt_pc_done[0])
            P.wait("sync", "pcs1", fvt_pc_done[1])

            kv_cnt = [0, 0]
            fvt_cnt = [0, 0, 0, 0]
            kv_free_te = [0, 0]
            fvt_free_te = [0, 0, 0, 0]
            psS_free_se = [0, 0]
            psO_free_ve = 0
            psB_free_ve = 0
            asm_free_ve = [[0, 0, 0, 0], [0, 0, 0, 0]]
            x2_free = [0, 0]
            x_freeB = [0, 0]
            lxb_cnt = [0, 0]
            pending_epi = [None]
            pending_qcln = []
            asm_ready = [0, 0]

            for qc in range(NQC):
                for pr in range(8):
                    s = pr % 2
                    combos = []
                    for h01 in range(2):
                        for vh in range(2):
                            a = 2 * pr + h01
                            v = 2 * qc + vh
                            if a - v >= 8:
                                combos.append((h01, vh, 256 * (a - v) - 2048))
                    P.wait("sync", "te", kv_free_te[s])
                    src = d_ks[:, 128 * pr:128 * pr + 128, :].rearrange(
                        "b p c -> p b c")
                    dst = kp_sb[s][:].rearrange("p (b c) -> p b c", b=8)
                    P.op("sync", lambda e, s1=src, d1=dst: e.dma_start(out=d1, in_=s1),
                         inc=f"lkv{s}", inc_n=16)
                    src = d_vs[:, 130 * pr:130 * pr + 130].rearrange(
                        "(kc p) c -> p kc c", p=128)
                    dst = vp_sb[s][:].rearrange("p (kc c) -> p kc c", kc=32)
                    P.op("sync", lambda e, s1=src, d1=dst: e.dma_start(out=d1, in_=s1),
                         inc=f"lkv{s}", inc_n=16)
                    kv_cnt[s] = P.op("sync", lambda e, qc=qc, pr=pr, s=s: e.dma_start(
                        out=qp_sb[s][:], in_=d_qs[qc, 128 * pr:128 * pr + 128, :]),
                        inc=f"lkv{s}", inc_n=16)
                    for (h01, vh, gr0) in combos:
                        slot = 2 * h01 + vh
                        P.wait("sync", "te", fvt_free_te[slot])
                        src = d_fvtb[:, gr0:gr0 + 256].rearrange(
                            "(kc p) c -> p kc c", p=128)
                        dst = fvt_sb[slot][:].rearrange("p (kc c) -> p kc c", kc=32)
                        fvt_cnt[slot] = P.op("sync", lambda e, s1=src, d1=dst:
                                             e.dma_start(out=d1, in_=s1),
                                             inc=f"lfv{slot}", inc_n=16)

                    P.wait("tensor", f"lkv{s}", kv_cnt[s])
                    s_te = {}
                    exp_se = {}
                    pv_te = {}

                    def emit_S(kc, s=s):
                        b = kc % 2
                        P.wait("tensor", "se", psS_free_se[b])
                        P.op("tensor", lambda e, kc=kc, b=b, s=s: e.matmul(
                            ps_S[b][:, 0:512],
                            kp_sb[s][0:64, 128 * kc:128 * kc + 128],
                            qp_sb[s][0:64, :], start=True, stop=True))
                        s_te[kc] = P.op("tensor", lambda e, kc=kc, b=b, s=s: e.matmul(
                            ps_S[b][:, 512:1024],
                            kp_sb[s][64:128, 128 * kc:128 * kc + 128],
                            qp_sb[s][64:128, :], start=True, stop=True), inc="te")

                    def emit_exp(kc):
                        bs = kc % 2
                        bp = kc % 4
                        P.wait("scalar", "te", s_te[kc])
                        c = P.op("scalar", lambda e, bs=bs, bp=bp: e.activation(
                            p_sb[bp][:], ps_S[bs][:], AF.Exp, scale=1.0 / 32.0),
                            inc="se")
                        exp_se[kc] = c
                        psS_free_se[bs] = c

                    def emit_PV(kc, s=s, combos=combos):
                        b = kc % 4
                        P.wait("tensor", "se", exp_se[kc])
                        if kc == 0:
                            P.wait("tensor", "ve", psO_free_ve)
                            if combos:
                                P.wait("tensor", "ve", psB_free_ve)
                                for (h01, vh, gr0) in combos:
                                    slot = 2 * h01 + vh
                                    P.wait("tensor", f"lfv{slot}", fvt_cnt[slot])
                        n_ops = 2 + len(combos)
                        i = 0
                        t = None
                        for h01 in range(2):
                            i += 1
                            t = P.op("tensor", lambda e, kc=kc, b=b, s=s, h01=h01:
                                     e.matmul(
                                         ps_O[0:65, 512 * h01:512 * h01 + 512],
                                         vp_sb[s][:, 130 * kc + 65 * h01:
                                                   130 * kc + 65 * h01 + 65],
                                         p_sb[b][:, 512 * h01:512 * h01 + 512],
                                         start=(kc == 0), stop=(kc == 31),
                                         skip_group_check=True),
                                     inc="te" if i == n_ops else None)
                        seen_banks = set()
                        for (h01, vh, gr0) in combos:
                            i += 1
                            slot = 2 * h01 + vh
                            bank = slot // 2
                            st_flag = (kc == 0) and bank not in seen_banks
                            if kc == 0:
                                seen_banks.add(bank)
                            t = P.op("tensor", lambda e, kc=kc, b=b, s=s, h01=h01,
                                     slot=slot, st_flag=st_flag: e.matmul(
                                         ps_B[0:64, 256 * slot:256 * slot + 256],
                                         vp_sb[s][:, 130 * kc + 65 * h01:
                                                   130 * kc + 65 * h01 + 64],
                                         fvt_sb[slot][:, 256 * kc:256 * kc + 256],
                                         start=st_flag, stop=(kc == 31),
                                         skip_group_check=True),
                                     inc="te" if i == n_ops else None)
                        pv_te[kc] = t

                    emit_S(0)
                    emit_exp(0)
                    emit_S(1)
                    emit_exp(1)
                    if pending_epi[0] is not None:
                        pending_epi[0]()
                        pending_epi[0] = None
                    if pending_qcln and pr < 4:
                        pending_qcln[0](pr)
                        if pr == 3:
                            pending_qcln.pop(0)
                    for kc in range(32):
                        if kc + 2 < 32:
                            emit_S(kc + 2)
                        emit_PV(kc)
                        if kc + 2 < 32:
                            emit_exp(kc + 2)
                    last_te = pv_te[31]
                    kv_free_te[s] = last_te
                    for (h01, vh, gr0) in combos:
                        fvt_free_te[2 * h01 + vh] = last_te

                    def epi(pr=pr, combos=combos, last_te=last_te, qb=qc % 2):
                        nonlocal psO_free_ve, psB_free_ve
                        P.wait("vector", "te", last_te)
                        evac_ve = P.op("vector", lambda e: e.tensor_copy(
                            of_sb[0:65, :], ps_O[0:65, :]),
                            inc="ve" if not combos else None)
                        for ci, (h01, vh, gr0) in enumerate(combos):
                            slot = 2 * h01 + vh
                            evac_ve = P.op("vector", lambda e, slot=slot:
                                           e.tensor_copy(
                                               bvf_sb[0:64,
                                                      256 * slot:256 * slot + 256],
                                               ps_B[0:64,
                                                    256 * slot:256 * slot + 256]),
                                           inc="ve" if ci == len(combos) - 1
                                           else None)
                        P.wait("tensor", "ve", evac_ve)
                        n_tr = 8 + 2 * len(combos)
                        i = 0
                        for h01 in range(2):
                            for u in range(4):
                                i += 1
                                tr_te = P.op("tensor", lambda e, h01=h01, u=u:
                                             e.transpose(
                                                 ps_O[:, 512 * h01 + 65 * u:
                                                      512 * h01 + 65 * u + 65],
                                                 of_sb[0:65, 512 * h01 + 128 * u:
                                                       512 * h01 + 128 * u + 128],
                                                 idf_sb[0:65, 0:65]),
                                             inc="te" if i == n_tr else None)
                        for (h01, vh, gr0) in combos:
                            slot = 2 * h01 + vh
                            for uu in range(2):
                                i += 1
                                tr_te = P.op("tensor", lambda e, slot=slot, uu=uu:
                                             e.transpose(
                                                 ps_B[:, 256 * slot + 64 * uu:
                                                      256 * slot + 64 * uu + 64],
                                                 bvf_sb[0:64,
                                                        256 * slot + 128 * uu:
                                                        256 * slot + 128 * uu + 128],
                                                 idf_sb[0:64, 0:64]),
                                             inc="te" if i == n_tr else None)
                        P.wait("vector", "te", tr_te)
                        if combos:
                            for (h01, vh, gr0) in combos:
                                slot = 2 * h01 + vh
                                P.op("vector", lambda e, slot=slot: e.tensor_copy(
                                    bvt_sb[:, 256 * slot:256 * slot + 256],
                                    ps_B[:, 256 * slot:256 * slot + 256]), inc=None)
                        den = bass.AP(ps_O[:].tensor, 64, [[1024, 128], [512, 2],
                                                           [65, 4]])
                        rout = rcp_sb[:, 0:8].rearrange("p (a b) -> p a b", a=2)
                        cr = P.op("vector", lambda e, den=den, rout=rout:
                                  e.reciprocal(rout, den), inc="ve")
                        P.wait("vector", "ve", cr)
                        for u in range(4):
                            P.wait("vector", "ve", asm_free_ve[qb][u])
                        for h01 in range(2):
                            a = 2 * pr + h01
                            for u in range(4):
                                vh = u // 2
                                aff = any(c[0] == h01 and c[1] == vh
                                          for c in combos)
                                if aff:
                                    slot = 2 * h01 + vh
                                    in1 = bvt_sb[:, 256 * slot + 64 * (u % 2):
                                                 256 * slot + 64 * (u % 2) + 64]
                                else:
                                    in1 = w_bc[:, 65 * a:65 * a + 64]
                                last = h01 == 1 and u == 3
                                stt_ve = P.op(
                                    "vector", lambda e, h01=h01, u=u, a=a, in1=in1:
                                    e.scalar_tensor_tensor(
                                        asm_sb[qb][u][:, 64 * a:64 * a + 64],
                                        ps_O[:, 512 * h01 + 65 * u:
                                             512 * h01 + 65 * u + 64],
                                        rcp_sb[:, 4 * h01 + u:4 * h01 + u + 1],
                                        in1, ALU.mult, ALU.add),
                                    inc="ve" if last else None)
                        psO_free_ve = stt_ve
                        psB_free_ve = stt_ve

                    pending_epi[0] = epi

                if pending_epi[0] is not None:
                    pending_epi[0]()
                    pending_epi[0] = None

                def mk_qcln(qc=qc):
                    qb = qc % 2

                    def unit(u):
                        sl = stats[u % 2]
                        g_tt = 4 * qc + u
                        hp = u % 2
                        P.wait("sync", "ve", x_freeB[hp])
                        lxb_cnt[hp] = P.op("sync", lambda e, g=g_tt, hp=hp:
                                           e.dma_start(out=x_sb[hp][:],
                                                       in_=d_x[128 * g:
                                                               128 * g + 128, :]),
                                           inc=f"lxb{hp}", inc_n=16)
                        c_ln1 = plan_ln(P, asm_sb[qb][u][:], t1_sb[:], E, 1e-3, sl,
                                        ready_val=asm_ready[qb])
                        asm_free_ve[qb][u] = c_ln1
                        P.wait("vector", "ve", c_ln1)
                        c_g = P.op("vector", lambda e: e.scalar_tensor_tensor(
                            t2_sb[:], t1_sb[:], 0.0, gb_sb[:], ALU.bypass,
                            ALU.mult), inc="ve")
                        P.wait("vector", "ve", c_g)
                        c_b = P.op("vector", lambda e: e.scalar_tensor_tensor(
                            t2_sb[:], t2_sb[:], 0.0, bb_sb[:], ALU.bypass,
                            ALU.add), inc="ve")
                        c_ln2 = plan_ln(P, t2_sb[:], t1_sb[:], E, 1e-5, sl,
                                        ready_val=c_b)
                        P.wait("vector", f"lxb{hp}", lxb_cnt[hp])
                        P.wait("vector", "ve", c_ln2)
                        P.wait("vector", f"sx2{hp}", x2_free[hp])
                        c_x2 = P.op("vector", lambda e, hp=hp: e.tensor_tensor(
                            x2_sb[hp][:], x_sb[hp][:], t1_sb[:], ALU.add),
                            inc="ve")
                        x_freeB[hp] = c_x2
                        P.wait("sync", "ve", c_x2)
                        x2_free[hp] = P.op("sync", lambda e, g=g_tt, hp=hp:
                                           e.dma_start(
                                               out=d_x2[128 * g:128 * g + 128, :],
                                               in_=x2_sb[hp][:]),
                                           inc=f"sx2{hp}", inc_n=16)
                    return unit

                asm_ready[qc % 2] = P.val("ve")
                pending_qcln.append(mk_qcln())

            if pending_epi[0] is not None:
                pending_epi[0]()
                pending_epi[0] = None
            while pending_qcln:
                unit = pending_qcln.pop(0)
                for u in range(4):
                    unit(u)
            b_x2_totals = {"sx20": x2_free[0], "sx21": x2_free[1]}
            if phases <= 2:
                for s_name, tot in b_x2_totals.items():
                    P.wait("sync", s_name, tot)
            P.flush(block)
        if phases <= 2:
            sAB.close()
            return nc

        sAB.close()
        with ExitStack() as sC:
            sbC = lambda n_, s_, d_: sC.enter_context(nc.sbuf_tensor(n_, s_, d_))
            xn2T = sbC("xn2T", [128, 8 * TFF], BF16)
            a_sb = sbC("a_sb", [128, 32 * TFF], BF16)
            h_sb = sbC("h_sb", [128, 32 * TFF], BF16)
            fT_sb = h_sb
            w12_sb = [sbC(f"w12_{i}", [128, 8 * 128], BF16) for i in range(4)]
            hsl_sb = [sbC(f"hsl{i}", [128, 32 * 128], BF16) for i in range(2)]
            x2t_sb = [sbC(f"x2t{i}", [128, E], F32) for i in range(2)]
            xn2_sb = [sbC(f"xn2_{i}", [128, E], BF16) for i in range(2)]
            silu_sb = [sbC(f"silu{i}", [128, 512], F32) for i in range(2)]
            ftm_sb = h_sb[:, 8 * TFF:8 * TFF + 2048].bitcast(F32)
            ps_M = [sC.enter_context(nc.psum_tensor(f"psM{i}", [128, 512], F32))
                    for i in range(4)]
            ps_tr2 = sC.enter_context(nc.psum_tensor("ps_tr2", [128, 2048], BF16))

            P.phase_barrier()

            lx2_cnt = [0, 0]
            x2t_free = [0, 0]
            w12_cnt = [0, 0, 0, 0]
            w12_free_te = [0, 0, 0, 0]
            hsl_cnt = [0, 0]
            hsl_free_te = [0, 0]
            silu_free_ve = [0, 0]
            psM_free = [0, 0, 0, 0]
            psM_free_eng = ["ve", "ve", "ve", "ve"]
            trp2 = [0]
            tr2_evac_ve = [0, 0]
            xn2_free_ve = [0, 0]
            out_free = [0, 0]

            xn2_cln = {}

            def emit_xn2_ln(ft, u8):
                g_tt = 8 * ft + u8
                hp = u8 % 2
                P.wait("sync", "ve", x2t_free[hp])
                lx2_cnt[hp] = P.op("sync", lambda e, g=g_tt, hp=hp: e.dma_start(
                    out=x2t_sb[hp][:], in_=d_x2[128 * g:128 * g + 128, :]),
                    inc=f"lx2{hp}", inc_n=16)
                P.wait("vector", f"lx2{hp}", lx2_cnt[hp])
                c_ln = plan_ln(P, x2t_sb[hp][:], xn2_sb[hp][:], E, 1e-5,
                               stats[hp], ready_val=xn2_free_ve[hp])
                x2t_free[hp] = c_ln
                xn2_cln[u8] = c_ln

            def emit_xn2_tr(ft, u8):
                hp = u8 % 2
                tp2 = trp2[0]
                P.wait("tensor", "ve", xn2_cln[u8])
                P.wait("tensor", "ve", tr2_evac_ve[tp2])
                for f in range(8):
                    te_tr = P.op("tensor", lambda e, hp=hp, f=f, tp2=tp2:
                                 e.transpose(
                                     ps_tr2[:, 1024 * tp2 + 128 * f:
                                            1024 * tp2 + 128 * f + 128],
                                     xn2_sb[hp][:, 128 * f:128 * f + 128],
                                     idb_sb[:]), inc="te" if f == 7 else None)
                P.wait("vector", "te", te_tr)
                for f in range(8):
                    ev = P.op("vector", lambda e, f=f, u8=u8, tp2=tp2:
                              e.tensor_copy(
                                  xn2T[:, TFF * f + 128 * u8:
                                       TFF * f + 128 * u8 + 128],
                                  ps_tr2[:, 1024 * tp2 + 128 * f:
                                         1024 * tp2 + 128 * f + 128]),
                              inc="ve" if f == 7 else None)
                tr2_evac_ve[tp2] = ev
                xn2_free_ve[hp] = ev
                trp2[0] ^= 1

            def emit_xn2_unit(ft, u8):
                emit_xn2_ln(ft, u8)
                emit_xn2_tr(ft, u8)

            emit_xn2_ln(0, 0)
            for u8 in range(8):
                if u8 + 1 < 8:
                    emit_xn2_ln(0, u8 + 1)
                emit_xn2_tr(0, u8)

            for ft in range(4):
                xn2T_ve = P.val("ve")
                P.wait("tensor", "ve", xn2T_ve)

                for mc in range(32):
                    for wi, d_w in enumerate([d_w1b, d_w2b]):
                        slot = 2 * (mc % 2) + wi
                        P.wait("sync", "te", w12_free_te[slot])
                        w12_cnt[slot] = P.op("sync", lambda e, d_w=d_w, mc=mc,
                                             slot=slot: e.dma_start(
                                                 out=w12_sb[slot][:],
                                                 in_=d_w[mc, :, :]),
                                             inc=f"lw12_{slot}", inc_n=16)
                    for wi in range(2):
                        P.wait("tensor", f"lw12_{2 * (mc % 2) + wi}",
                               w12_cnt[2 * (mc % 2) + wi])
                    for nt in range(2):
                        for wi in range(2):
                            b = 2 * (mc % 2) + wi
                            slot = 2 * (mc % 2) + wi
                            P.wait("tensor", psM_free_eng[b], psM_free[b])
                            for ki in range(8):
                                te_w12 = P.op(
                                    "tensor", lambda e, slot=slot, ki=ki, nt=nt, b=b:
                                    e.matmul(
                                        ps_M[b][:],
                                        w12_sb[slot][:, 128 * ki:128 * ki + 128],
                                        xn2T[:, TFF * ki + 512 * nt:
                                             TFF * ki + 512 * nt + 512],
                                        start=(ki == 0), stop=(ki == 7)),
                                    inc="te" if ki == 7 else None)
                        b1, b2 = 2 * (mc % 2), 2 * (mc % 2) + 1
                        hp2 = nt
                        P.wait("scalar", "te", te_w12)
                        P.wait("scalar", "ve", silu_free_ve[hp2])
                        c_si = P.op("scalar", lambda e, b1=b1, mc=mc, hp2=hp2:
                                    e.activation(silu_sb[hp2][:], ps_M[b1][:],
                                                 AF.Silu,
                                                 scale=s1_sb[:, mc:mc + 1]),
                                    inc="se")
                        psM_free[b1] = c_si
                        psM_free_eng[b1] = "se"
                        P.wait("vector", "se", c_si)
                        c_a = P.op("vector", lambda e, b2=b2, mc=mc, nt=nt, hp2=hp2:
                                   e.scalar_tensor_tensor(
                                       a_sb[:, TFF * mc + 512 * nt:
                                            TFF * mc + 512 * nt + 512],
                                       ps_M[b2][:], s2_sb[:, mc:mc + 1],
                                       silu_sb[hp2][:], ALU.mult, ALU.mult),
                                   inc="ve")
                        silu_free_ve[hp2] = c_a
                        psM_free[b2] = c_a
                        psM_free_eng[b2] = "ve"
                        w12_free_te[b1] = te_w12
                        w12_free_te[b2] = te_w12

                def h_stage(d_ht, s_ap, src_sb, dst_sb, sname, interleave_xn2):
                    te_h = 0
                    for mc in range(32):
                        if interleave_xn2 and mc % 2 == 0:
                            uln = mc // 2 - 1
                            utr = mc // 2 - 2
                            if 0 <= uln < 8:
                                emit_xn2_ln(ft + 1, uln)
                            if 0 <= utr < 8:
                                emit_xn2_tr(ft + 1, utr)
                        slot = mc % 2
                        P.wait("sync", "te", hsl_free_te[slot])
                        hsl_cnt[slot] = P.op("sync", lambda e, d_ht=d_ht, mc=mc,
                                             slot=slot: e.dma_start(
                                                 out=hsl_sb[slot][:],
                                                 in_=d_ht[mc, :, :]),
                                             inc=f"{sname}{slot}", inc_n=16)
                        P.wait("tensor", f"{sname}{slot}", hsl_cnt[slot])
                        for nt in range(2):
                            b = 2 * slot + nt
                            P.wait("tensor", psM_free_eng[b], psM_free[b])
                            for ki in range(32):
                                te_h = P.op(
                                    "tensor", lambda e, slot=slot, ki=ki, nt=nt, b=b,
                                    src_sb=src_sb: e.matmul(
                                        ps_M[b][:],
                                        hsl_sb[slot][:, 128 * ki:128 * ki + 128],
                                        src_sb[:, TFF * ki + 512 * nt:
                                               TFF * ki + 512 * nt + 512],
                                        start=(ki == 0), stop=(ki == 31)),
                                    inc="te" if ki == 31 else None)
                            P.wait("scalar", "te", te_h)
                            c_g2 = P.op("scalar", lambda e, b=b, mc=mc, nt=nt,
                                        dst_sb=dst_sb, s_ap=s_ap: e.activation(
                                            dst_sb[:, TFF * mc + 512 * nt:
                                                   TFF * mc + 512 * nt + 512],
                                            ps_M[b][:], AF.Gelu,
                                            scale=s_ap[:, mc:mc + 1]), inc="se")
                            psM_free[b] = c_g2
                            psM_free_eng[b] = "se"
                        hsl_free_te[slot] = te_h
                    return P.val("se")

                a_ve = P.val("ve")
                P.wait("tensor", "ve", a_ve)
                se_h1 = h_stage(d_h1b, sh1_sb, a_sb, h_sb, "lh1_", ft < 3)
                if ft < 3 and 7 in xn2_cln and trp2 is not None:
                    pass
                P.wait("tensor", "se", se_h1)
                se_h2 = h_stage(d_h2b, sh2_sb, h_sb, a_sb, "lh2_", False)
                P.wait("tensor", "se", se_h2)

                for oc in range(8):
                    slot = oc % 2
                    P.wait("sync", "te", hsl_free_te[slot])
                    hsl_cnt[slot] = P.op("sync", lambda e, oc=oc, slot=slot:
                                         e.dma_start(out=hsl_sb[slot][:],
                                                     in_=d_w3b[oc, :, :]),
                                         inc=f"lw3_{slot}", inc_n=16)
                    P.wait("tensor", f"lw3_{slot}", hsl_cnt[slot])
                    for nt in range(2):
                        b = 2 * slot + nt
                        P.wait("tensor", psM_free_eng[b], psM_free[b])
                        for ki in range(32):
                            te_w3 = P.op(
                                "tensor", lambda e, slot=slot, ki=ki, nt=nt, b=b:
                                e.matmul(ps_M[b][:],
                                         hsl_sb[slot][:, 128 * ki:128 * ki + 128],
                                         a_sb[:, TFF * ki + 512 * nt:
                                              TFF * ki + 512 * nt + 512],
                                         start=(ki == 0), stop=(ki == 31)),
                                inc="te" if ki == 31 else None)
                        P.wait("vector", "te", te_w3)
                        c_f = P.op("vector", lambda e, b=b, oc=oc, nt=nt:
                                   e.tensor_scalar(
                                       fT_sb[:, TFF * oc + 512 * nt:
                                             TFF * oc + 512 * nt + 512],
                                       ps_M[b][:], s3_sb[:, oc:oc + 1], None,
                                       ALU.mult), inc="ve")
                        psM_free[b] = c_f
                        psM_free_eng[b] = "ve"
                    hsl_free_te[slot] = te_w3
                fT_ve = P.val("ve")
                P.wait("tensor", "ve", fT_ve)

                for u8 in range(8):
                    g_tt = 8 * ft + u8
                    hp = u8 % 2
                    tp2 = trp2[0]
                    P.wait("tensor", "ve", tr2_evac_ve[tp2])
                    for f in range(8):
                        te_tr = P.op("tensor", lambda e, f=f, u8=u8, tp2=tp2:
                                     e.transpose(
                                         ps_tr2[:, 1024 * tp2 + 128 * f:
                                                1024 * tp2 + 128 * f + 128],
                                         fT_sb[:, TFF * f + 128 * u8:
                                               TFF * f + 128 * u8 + 128],
                                         idb_sb[:]), inc="te" if f == 7 else None)
                    P.wait("vector", "te", te_tr)
                    c_ftm = P.op("vector", lambda e, tp2=tp2: e.tensor_copy(
                        ftm_sb, ps_tr2[:, 1024 * tp2:1024 * tp2 + 1024]),
                        inc="ve")
                    tr2_evac_ve[tp2] = c_ftm
                    trp2[0] ^= 1
                    P.wait("sync", "ve", x2t_free[hp])
                    lx2_cnt[hp] = P.op("sync", lambda e, g=g_tt, hp=hp: e.dma_start(
                        out=x2t_sb[hp][:], in_=d_x2[128 * g:128 * g + 128, :]),
                        inc=f"lx2{hp}", inc_n=16)
                    c_ln = plan_ln(P, ftm_sb, t1_sb[:], E, 1e-5, stats[2 + hp],
                                   ready_val=c_ftm)
                    P.wait("vector", f"lx2{hp}", lx2_cnt[hp])
                    P.wait("vector", "ve", c_ln)
                    c_o = P.op("vector", lambda e, hp=hp: e.tensor_tensor(
                        x2t_sb[hp][:], x2t_sb[hp][:], t1_sb[:], ALU.add), inc="ve")
                    x2t_free[hp] = c_o
                    P.wait("sync", "ve", c_o)
                    out_free[hp] = P.op("sync", lambda e, g=g_tt, hp=hp: e.dma_start(
                        out=d_out[128 * g:128 * g + 128, :], in_=x2t_sb[hp][:]),
                        inc=f"sout{hp}", inc_n=16)
                    P.wait("sync", f"sout{hp}", out_free[hp])

            P.wait("sync", "sout0", out_free[0])
            P.wait("sync", "sout1", out_free[1])
            P.flush(block)

    return nc


_PERM = None


def _perm():
    global _PERM
    if _PERM is None:
        p = np.arange(NT)
        _PERM = 16 * (p % 256) + p // 256
    return _PERM


def _prep_core_inputs(xb, d):
    perm = _perm()
    rel = d["L0_attn_rel_table"]
    jj = perm // 16
    cc = perm % 16
    gr = np.arange(2048)
    idx = np.minimum(2047 - gr[None, :] + jj[:, None], 2046)
    fvt = np.ascontiguousarray(rel[idx, cc[:, None]])
    cmat = np.ascontiguousarray(
        np.broadcast_to(np.repeat(rel[2046], 2)[None, :], (128, 32)))
    return {
        "x": np.ascontiguousarray(xb[perm]),
        "wq": d["L0_attn_Wq"], "wk": d["L0_attn_Wk"], "wv": d["L0_attn_Wv"],
        "fvt": fvt, "cmat": cmat,
        "gamma_b": np.ascontiguousarray(
            np.broadcast_to(d["L0_attn_gamma"][None, :], (128, E))),
        "beta_b": np.ascontiguousarray(
            np.broadcast_to(d["L0_attn_beta"][None, :], (128, E))),
        "w1t": np.ascontiguousarray(d["L0_ffn_w1_w"].T),
        "w2t": np.ascontiguousarray(d["L0_ffn_w2_w"].T),
        "h1t": np.ascontiguousarray(d["L0_ffn_h1_w"].T),
        "h2t": np.ascontiguousarray(d["L0_ffn_h2_w"].T),
        "w3t": np.ascontiguousarray(d["L0_ffn_w3_w"].T),
        "s1m": np.ascontiguousarray(d["L0_ffn_w1_s"].reshape(32, 128).T),
        "s2m": np.ascontiguousarray(d["L0_ffn_w2_s"].reshape(32, 128).T),
        "sh1m": np.ascontiguousarray(d["L0_ffn_h1_s"].reshape(32, 128).T),
        "sh2m": np.ascontiguousarray(d["L0_ffn_h2_s"].reshape(32, 128).T),
        "s3m": np.ascontiguousarray(d["L0_ffn_w3_s"].reshape(8, 128).T),
        "id_bf": np.eye(128, dtype=np.float32).astype(ml_dtypes.bfloat16),
        "id_f32": np.eye(128, dtype=np.float32),
    }


def kernel(**inputs):
    x = np.asarray(inputs["L0_attn_x"], dtype=np.float32)
    d = {k: np.asarray(v, dtype=np.float32) for k, v in inputs.items()}
    nc = build_program()
    in_maps = [_prep_core_inputs(x[b], d) for b in range(8)]
    res = run_bass_kernel_spmd(nc, in_maps, core_ids=list(range(8)))
    perm = _perm()
    out = np.empty((8, NT, E), dtype=np.float32)
    for b in range(8):
        out[b][perm] = res.results[b]["out"]
    return out
